# version 01
# baseline (speedup 1.0000x reference)
import os
import math
import contextlib
import numpy as np
import ml_dtypes
import concourse.bass as bass
import concourse.mybir as mybir
from bass_rust import AP as RAP
from concourse.bass_utils import run_bass_kernel_spmd

F32 = mybir.dt.float32
BF16 = mybir.dt.bfloat16
I32 = mybir.dt.int32
AF = mybir.ActivationFunctionType
ALU = mybir.AluOpType
AX = mybir.AxisListType

L = 2048
D = 1024
NH = 16
HID = 4096
LT = L // 128
DT = D // 128
HT = HID // 128
NIC = L // 512

P6 = L + 64
P7 = 320
SBD = 320
P5 = L + 64
OFFB = 127

LN2 = float(np.log(np.float64(2.0)))
C2LN2 = float(1.0 / (2.0 * np.log(np.float64(2.0))))
NEG1E18 = -1.0e18
EPS = 1e-5
ENGS = ("sync", "gpsimd", "tensor", "vector", "scalar")


class Sem:
    def __init__(self, h):
        self.h = h
        self.v = 0

    def inc(self, n=1):
        self.v += n
        return self.v


def build_core_program(debug=(), nheads=NH, phases=99, skip_ctxrel=False):
    nc = bass.Bass("TRN2", target_bir_lowering=False, detect_race_conditions=False)

    din = lambda name, shape, dt: nc.dram_tensor(name, shape, dt, kind="ExternalInput")
    x_nat = din("x_nat", [L, D], F32)
    x_T = din("x_T", [D, L], F32)
    wq_T = din("wq_T", [D, D], F32)
    wk_T = din("wk_T", [D, D], F32)
    wv_T = din("wv_T", [D, D], F32)
    wo_T = din("wo_T", [D, D], F32)
    w1_T = din("w1_T", [D, HID], F32)
    w2_T = din("w2_T", [HID, D], F32)
    b1in = din("b1", [1, HID], F32)
    b2in = din("b2", [1, D], F32)
    rel_Tw = din("rel_Tw", [64, 33], F32)
    rel_wrev = din("rel_wrev", [33, 64], F32)
    rel0r_in = din("rel0r", [1, 64], F32)
    maskin = din("maskin", [L, D], BF16)
    c2in = din("c2", [128, L], F32)
    s2in = din("s2", [128, L], F32)
    identin = din("ident", [128, 128], BF16)

    out = nc.dram_tensor("out", [L, D], F32, kind="ExternalOutput")
    dbg = {}
    def ddbg(name, shape, dt):
        if name in debug:
            dbg[name] = nc.dram_tensor("dbg_" + name, shape, dt, kind="ExternalOutput")
    ddbg("qT", [D, L], BF16)
    ddbg("kT", [D, L], BF16)
    ddbg("vh", [L, NH * 66], BF16)
    ddbg("negu", [2, NH], F32)
    ddbg("nrm", [2, 16], F32)
    ddbg("um", [2, 48], F32)
    ddbg("unew", [2, 80], F32)
    ddbg("ctxT", [D, L], BF16)
    ddbg("sg", [1, L], F32)
    ddbg("a0", [32, L], BF16)
    ddbg("x1", [L, D], F32)
    ddbg("hq0", [128, 512], BF16)
    ddbg("qd", [4, 512], F32)

    dram = lambda name, n, dt: nc.dram_tensor(name, [n], dt, kind="Internal")
    hbuf = dram("hbuf", 2 * 32 * P6, BF16)
    mbuf = dram("mbuf", 16 * 32 * 128, BF16)
    wbuf = dram("wbuf", L * P7, BF16)
    dbuf = dram("dbuf", L * SBD, BF16)
    cbuf = dram("cbuf", L * 128, BF16)
    btbuf = dram("btbuf", 32 * P5, BF16)
    relbuf = dram("relbuf", 64 * 32, BF16)
    ubuf = dram("ubuf", 64, F32)
    sgbuf = dram("sgbuf", L, F32)
    ctxd = dram("ctxd", D * L, BF16)
    w1qd = dram("w1qd", D * HID, BF16)
    w2qd = dram("w2qd", HID * D, BF16)
    x1d = dram("x1d", L * D, F32)
    xqd = dram("xqd", L * D, BF16)

    es = contextlib.ExitStack()
    esA = contextlib.ExitStack()
    esB = contextlib.ExitStack()
    esB2 = contextlib.ExitStack()
    esC = contextlib.ExitStack()

    def mk(stk, side):
        def f(name, shape, dt):
            return stk.enter_context(nc.sbuf_tensor(name, shape, dt, side=side))
        return f
    sbG = mk(es, "left")
    sbA = mk(esA, "right")
    sbB = mk(esB, "left")
    sbB2 = mk(esB2, "right")
    sbC = mk(esC, "right")
    psum = lambda name: es.enter_context(nc.psum_tensor(name, [128, 512], F32))
    sem = lambda name: Sem(es.enter_context(nc.semaphore(name)))

    xqTb = [sbG(f"xqTb{i}", [128, L], BF16) for i in range(DT)]
    vhb = [sbG(f"vhb{i}", [128, NH * 66], BF16) for i in range(LT)]
    scr = {k: sbG("scr_" + k, [128, 512], F32) for k in ("a", "b", "c", "s", "u", "w")}
    scr["i"] = sbG("scr_i", [128, 512], I32)
    scr["j"] = sbG("scr_j", [128, 512], I32)
    qrt = [sbG(f"qrt{i}", [128, 512], F32) for i in range(3)]
    ctxe = sbG("ctxe", [64, 512], BF16)
    identb = sbG("identb", [128, 128], BF16)
    relblk = sbG("relblk", [128, 64], BF16)
    rl2 = sbG("rl2", [32, 64], BF16)
    rl2f = sbG("rl2f", [32, 64], F32)
    relrow = sbG("relrow", [1, D], BF16)
    onesrow = sbG("onesrow", [1, 128], BF16)
    onesf = sbG("onesf", [1, 128], F32)
    hhsel = sbG("hhsel", [128, 2], BF16)
    consts = sbG("consts", [128, 12], F32)
    nrm = sbG("nrm", [2, 16], F32)
    umath = sbG("umath", [2, 48], F32)
    bmcol = sbG("bmcol", [2, 2], F32)
    negu = sbG("negu", [128, NH], F32)
    unew = sbG("unew", [128, 80], F32)
    col = sbG("col", [128, 8], F32)
    b1qb = sbG("b1qb", [128, HT], F32)
    b1raw = sbG("b1raw", [128, HT], F32)
    gt = [sbG(f"gt{i}", [128, 32], BF16) for i in range(2)]
    wint = [sbG(f"wint{i}", [128, 160], BF16) for i in range(2)]
    ctxt = [sbG(f"ctxt{i}", [128, 128], BF16) for i in range(2)]
    zt = sbG("zt", [128, P7], BF16)

    c2b = sbA("c2b", [128, L], F32)
    s2b = sbA("s2b", [128, L], F32)
    stageA = sbA("stageA", [128, D], F32)
    wbfA = [sbA(f"wbfA{i}", [128, D], BF16) for i in range(DT)]
    relw = sbA("relw", [64, 33], F32)
    relv = sbA("relv", [33, 64], F32)
    rel0b = sbA("rel0b", [128, D], F32)
    rel0r = sbA("rel0r_t", [1, 64], F32)

    qTb = [sbB(f"qTb{i}", [128, L], BF16) for i in range(DT)]
    kTb = [sbB(f"kTb{i}", [128, L], BF16) for i in range(DT)]
    attnb = [sbB(f"attnb{i}", [128, L], BF16) for i in range(2)]


    pqk = [psum("pqk0"), psum("pqk1")]
    pctx = [psum(f"pctx{i}") for i in range(4)]
    paux = psum("paux")
    paux2 = psum("paux2")

    s_ld = sem("s_ld")
    s_gw = sem("s_gw")
    s_tp = sem("s_tp")
    s_mm = sem("s_mm")
    s_v = sem("s_v")
    s_a = sem("s_a")
    s_st = sem("s_st")
    s_zz = sem("s_zz")
    s_bar = sem("s_bar")

    plan = []
    em = lambda eng, fn: plan.append((eng, fn))
    def wait(eng, s, val):
        em(eng, lambda e, ss=s.h, vv=val: e.wait_ge(ss, vv))

    nbar = [0]
    def barrier():
        nbar[0] += 1
        tgt = 5 * nbar[0]
        for eng in ENGS:
            em(eng, lambda e: e.drain().then_inc(s_bar.h, 1))
            wait(eng, s_bar, tgt)

    def vsafe(fn):
        tgt = s_v.inc()
        em("vector", lambda v, f=fn: f(v).then_inc(s_v.h, 1))
        em("vector", lambda v, t=tgt: v.wait_ge(s_v.h, t))
        return tgt

    def asafe(fn):
        tgt = s_a.inc()
        em("scalar", lambda sc, f=fn: f(sc).then_inc(s_a.h, 1))
        em("scalar", lambda sc, t=tgt: sc.wait_ge(s_a.h, t))
        return tgt

    def vsync():
        tgt = s_v.inc()
        em("vector", lambda v, tg=tgt: v.drain().then_inc(s_v.h, 1))
        return tgt

    def asynC():
        tgt = s_a.inc()
        em("scalar", lambda s, tg=tgt: s.drain().then_inc(s_a.h, 1))
        return tgt

    def mm_group(items, inc=True):
        n = len(items)
        for idx, (o, l, r, st, sp) in enumerate(items):
            if idx == n - 1 and inc:
                tgt = s_mm.inc()
                em("tensor", lambda t, o=o, l=l, r=r, st=st, sp=sp, tg=tgt:
                   t.matmul(o(), l(), r(), start=st, stop=sp).then_inc(s_mm.h, 1))
            else:
                em("tensor", lambda t, o=o, l=l, r=r, st=st, sp=sp:
                   t.matmul(o(), l(), r(), start=st, stop=sp))
        return s_mm.v

    LNG = consts[:, 0:1]
    BIAS_P = {3: consts[:, 1:2], 10: consts[:, 3:4]}
    BIAS_N = {3: consts[:, 2:3], 10: consts[:, 4:5]}

    qd_done = []

    def q_chain(x_ap, W, stages, P=128):
        def V(fn):
            return vsafe(fn)
        def A(fn):
            return asafe(fn)
        sa = lambda n: scr[n][:P, :W]

        def rstrict(src_ap, r_dst, tie_even):
            V(lambda v: v.tensor_copy(sa("i"), src_ap))
            V(lambda v: v.tensor_tensor(sa("b"), src_ap, sa("i"), ALU.subtract))
            V(lambda v: v.tensor_scalar(sa("w"), sa("b"), 0.5, None, ALU.is_gt))
            V(lambda v: v.tensor_tensor(sa("w"), sa("i"), sa("w"), ALU.add))
            V(lambda v: v.tensor_scalar(sa("c"), sa("b"), -0.5, None, ALU.is_lt))
            if not tie_even:
                V(lambda v, r=r_dst: v.tensor_tensor(r, sa("w"), sa("c"),
                                                     ALU.subtract))
                return
            V(lambda v: v.tensor_tensor(sa("w"), sa("w"), sa("c"), ALU.subtract))
            V(lambda v: v.tensor_tensor(sa("c"), sa("b"), sa("b"), ALU.mult))
            V(lambda v: v.tensor_scalar(sa("c"), sa("c"), 0.25, None, ALU.is_equal))
            V(lambda v: v.tensor_scalar(sa("i"), sa("i"), 1, None, ALU.bitwise_and))
            V(lambda v: v.tensor_tensor(sa("c"), sa("c"), sa("i"), ALU.mult))
            V(lambda v: v.tensor_tensor(sa("c"), sa("c"), sa("b"), ALU.mult))
            V(lambda v: v.tensor_scalar(sa("c"), sa("c"), 2.0, None, ALU.mult))
            V(lambda v, r=r_dst: v.tensor_tensor(r, sa("w"), sa("c"), ALU.add))

        wait("scalar", s_v, s_v.v)
        A(lambda sc, x=x_ap: sc.activation(sa("a"), x, AF.Square))
        A(lambda sc: sc.activation(sa("b"), sa("a"), AF.Ln, bias=LNG[:P, :]))
        wait("vector", s_a, asynC())
        V(lambda v: v.tensor_scalar(sa("u"), sa("b"), C2LN2, -0.5,
                                    ALU.mult, ALU.add))
        rstrict(sa("u"), sa("a"), False)
        if "qd" in dbg and not qd_done:
            qd_done.append(1)
            V(lambda v: v.tensor_copy(scr["i"][0:1, 0:512], scr["u"][0:1, 0:512]))
            V(lambda v: v.tensor_scalar(scr["w"][0:1, 0:512], scr["i"][0:1, 0:512],
                                        0.0, None, ALU.add))
            wait("sync", s_v, vsync())
            for row, tt in ((0, "u"), (1, "w"), (2, "a"), (3, "b")):
                em("sync", lambda sy, row=row, tt=tt: sy.dma_start(
                    RAP(dbg["qd"], row * 512, [[512, 1], [1, 512]]),
                    scr[tt][0:1, 0:512]).then_inc(s_st.h, 16))
                s_st.inc(16)
            wait("vector", s_st, s_st.v)
        src = x_ap
        for (man, min_e, dst) in stages:
            V(lambda v, me=min_e: v.tensor_scalar(
                sa("c"), sa("a"), float(me), None, ALU.max))
            V(lambda v: v.tensor_copy(sa("i"), sa("c")))
            V(lambda v, man=man: v.tensor_scalar(
                sa("j"), sa("i"), -1, 127 + man, ALU.mult, ALU.add))
            V(lambda v: v.tensor_scalar(sa("j"), sa("j"), 23, None,
                                        ALU.logical_shift_left))
            V(lambda v, src=src: v.tensor_tensor(
                sa("u"), src, scr["j"][:P, :W].bitcast(F32), ALU.mult))
            V(lambda v, man=man: v.tensor_scalar(
                sa("j"), sa("i"), 1, 127 - man, ALU.mult, ALU.add))
            V(lambda v: v.tensor_scalar(sa("j"), sa("j"), 23, None,
                                        ALU.logical_shift_left))
            V(lambda v: v.tensor_copy(sa("s"), scr["j"][:P, :W].bitcast(F32)))
            rstrict(sa("u"), sa("u"), True)
            V(lambda v, d=dst: v.tensor_tensor(d, sa("u"), sa("s"), ALU.mult))
            src = dst
        return vsync()

    for c, val in ((0, 1e-38), (1, 3.0 * LN2), (2, -3.0 * LN2), (3, 10.0 * LN2),
                   (4, -10.0 * LN2), (5, 0.8), (7, EPS), (8, 1.0 / float(D))):
        vsafe(lambda v, c=c, val=val: v.memset(consts[:, c:c + 1], val))
    vsafe(lambda v: v.reciprocal(consts[:, 6:7], consts[:, 5:6]))
    em("vector", lambda v: v.memset(zt[:], 0.0))
    em("vector", lambda v: v.memset(onesrow[:], 1.0))
    em("vector", lambda v: v.memset(onesf[:], 1.0))
    em("vector", lambda v: v.memset(hhsel[:, :], 0.0))
    em("vector", lambda v: v.memset(hhsel[0:64, 0:1], 1.0))
    em("vector", lambda v: v.memset(hhsel[64:128, 1:2], 1.0))
    em("vector", lambda v: v.memset(relblk[:, :], 0.0))
    em("vector", lambda v: v.memset(umath[:, :], 0.0))
    em("vector", lambda v: v.memset(nrm[:, :], 0.0))
    wait("gpsimd", s_v, vsync())

    def zfill(buf, span):
        pos = 0
        while pos < span:
            n = min(128 * P7, span - pos)
            full = n // P7
            if full > 0:
                em("gpsimd", lambda g, b=buf, p=pos, f=full: g.dma_start(
                    RAP(b, p, [[P7, f], [1, P7]]),
                    zt[0:f, :]).then_inc(s_zz.h, 16))
                s_zz.inc(16)
                pos += full * P7
            else:
                em("gpsimd", lambda g, b=buf, p=pos, n=n: g.dma_start(
                    RAP(b, p, [[1, 1], [1, n]]),
                    zt[0:1, 0:n]).then_inc(s_zz.h, 16))
                s_zz.inc(16)
                pos += n

    for buf, span in ((hbuf, 2 * 32 * P6), (wbuf, L * P7), (dbuf, L * SBD),
                      (cbuf, L * 128), (btbuf, 32 * P5)):
        zfill(buf, span)
    zz_all = s_zz.v

    wait("vector", s_zz, zz_all)
    em("vector", lambda v: v.memset(zt[0:128, 0:OFFB], NEG1E18))
    wait("gpsimd", s_v, vsync())
    for c in range(LT):
        em("gpsimd", lambda g, c=c: g.dma_start(
            RAP(wbuf, c * 128 * P7, [[P7, 128], [1, OFFB]]),
            zt[0:128, 0:OFFB]).then_inc(s_gw.h, 16))
        s_gw.inc(16)

    for tile, src in ((c2b, c2in), (s2b, s2in), (identb, identin),
                      (relw, rel_Tw), (relv, rel_wrev), (rel0r, rel0r_in)):
        em("sync", lambda sy, t=tile, s=src: sy.dma_start(
            t[:], s.ap()).then_inc(s_ld.h, 16))
        s_ld.inc(16)
    ld_tab = s_ld.v

    wait("vector", s_ld, ld_tab)
    vsafe(lambda v: v.tensor_scalar(
        qrt[0][0:64, 0:32], relw[:, 0:32], relw[:, 32:33], None, ALU.subtract))
    em("vector", lambda v: v.tensor_copy(ctxe[0:64, 0:32], qrt[0][0:64, 0:32]))
    wait("gpsimd", s_v, vsync())
    em("gpsimd", lambda g: g.dma_start(
        RAP(relbuf, 0, [[32, 64], [1, 32]]), ctxe[0:64, 0:32]
    ).then_inc(s_gw.h, 16))
    s_gw.inc(16)
    wait("sync", s_gw, s_gw.v)
    em("sync", lambda sy: sy.dma_start(
        relblk[0:64, 0:32], RAP(relbuf, 0, [[32, 64], [1, 32]])
    ).then_inc(s_tp.h, 16))
    s_tp.inc(16)
    em("sync", lambda sy: sy.dma_start(
        relblk[64:128, 32:64], RAP(relbuf, 0, [[32, 64], [1, 32]])
    ).then_inc(s_tp.h, 16))
    s_tp.inc(16)

    wait("tensor", s_ld, ld_tab)
    wait("tensor", s_v, s_v.v)
    mm_t = mm_group([(lambda: paux[0:32, 0:64], lambda: onesf[0:1, 0:32],
                      lambda: rel0r[0:1, 0:64], True, True)])
    wait("vector", s_mm, mm_t)
    vsafe(lambda v: v.tensor_tensor(rl2f[:, :], relv[0:32, 0:64],
                                    paux[0:32, 0:64], ALU.subtract))
    em("vector", lambda v: v.tensor_copy(rl2[:, :], rl2f[:, :]))
    vsafe(lambda v: v.tensor_copy(relrow[0:1, 0:64], rel0r[0:1, 0:64]))
    for hh in range(1, NH):
        em("vector", lambda v, hh=hh: v.tensor_copy(
            relrow[0:1, hh * 64:(hh + 1) * 64], relrow[0:1, 0:64]))
    wait("tensor", s_v, vsync())
    for dc in range(2):
        mm_t = mm_group([(lambda: paux2[0:128, 0:512],
                          lambda: onesrow[0:1, 0:128],
                          lambda dc=dc: relrow[0:1, dc * 512:(dc + 1) * 512],
                          True, True)])
        wait("vector", s_mm, mm_t)
        vt = s_v.inc()
        em("vector", lambda v, dc=dc, tg=vt: v.tensor_copy(
            rel0b[:, dc * 512:(dc + 1) * 512], paux2[0:128, 0:512]
        ).then_inc(s_v.h, 1))
        wait("tensor", s_v, vt)
    barrier()

    xTb = xqTb
    for i in range(DT):
        for hf in range(2):
            em("sync", lambda sy, i=i, hf=hf: sy.dma_start(
                stageA[:], RAP(x_T, i * 128 * L + hf * 1024, [[L, 128], [1, 1024]])
            ).then_inc(s_ld.h, 16))
            s_ld.inc(16)
            wait("scalar", s_ld, s_ld.v)
            tgt = s_a.inc()
            em("scalar", lambda s, i=i, hf=hf, tg=tgt: s.activation(
                xTb[i][:, hf * 1024:(hf + 1) * 1024], stageA[:],
                AF.Copy).then_inc(s_a.h, 1))
            wait("sync", s_a, tgt)
    barrier()

    def load_weight_bf16(src, scale, wbf, stg):
        for i in range(DT):
            em("sync", lambda sy, i=i, s=src, stg=stg: sy.dma_start(
                stg[:, 0:D], RAP(s, i * 128 * D, [[D, 128], [1, D]])
            ).then_inc(s_ld.h, 16))
            s_ld.inc(16)
            wait("scalar", s_ld, s_ld.v)
            tgt = s_a.inc()
            em("scalar", lambda s, i=i, sc=scale, wbf=wbf, stg=stg, tg=tgt:
               s.activation(wbf[i][:, :], stg[:, 0:D], AF.Copy, scale=sc
                            ).then_inc(s_a.h, 1))
            wait("sync", s_a, tgt)

    def proj_qk(dstb):
        for dt_ in range(DT):
            for lc in range(NIC):
                pb = pqk[(dt_ * NIC + lc) % 2]
                items = [(lambda pb=pb: pb[:, 0:512],
                          lambda ti=ti, dt_=dt_: wbfA[ti][:, dt_ * 128:(dt_ + 1) * 128],
                          lambda ti=ti, lc=lc: xTb[ti][:, lc * 512:(lc + 1) * 512],
                          ti == 0, ti == DT - 1) for ti in range(DT)]
                mm_t = mm_group(items)
                wait("vector", s_mm, mm_t)
                sl = slice(lc * 512, (lc + 1) * 512)
                em("vector", lambda v, pb=pb: v.tensor_copy(
                    qrt[0][:, 0:512], pb[:, 0:512]))
                em("vector", lambda v: v.stream_shuffle(
                    qrt[1][:, 0:512], qrt[0][:, 0:512], [i ^ 1 for i in range(32)]))
                em("vector", lambda v, sl=sl: v.tensor_tensor(
                    qrt[2][:, 0:512], qrt[0][:, 0:512], c2b[:, sl], ALU.mult))
                em("vector", lambda v, sl=sl: v.tensor_tensor(
                    qrt[1][:, 0:512], qrt[1][:, 0:512], s2b[:, sl], ALU.mult))
                tgt = s_v.inc()
                em("vector", lambda v, dt_=dt_, sl=sl, dstb=dstb, tg=tgt:
                   v.tensor_tensor(dstb[dt_][:, sl], qrt[2][:, 0:512],
                                   qrt[1][:, 0:512], ALU.add).then_inc(s_v.h, 1))
                wait("tensor", s_v, tgt)

    load_weight_bf16(wq_T, 0.125, wbfA, stageA)
    barrier()
    proj_qk(qTb)
    barrier()
    load_weight_bf16(wk_T, 1.0, wbfA, stageA)
    barrier()
    proj_qk(kTb)
    barrier()
    load_weight_bf16(wv_T, 1.0, wbfA, stageA)
    for lt in range(LT):
        for h in range(NH):
            em("vector", lambda v, lt=lt, h=h: v.memset(
                vhb[lt][:, h * 66 + 64:h * 66 + 65], 1.0))
            em("vector", lambda v, lt=lt, h=h: v.memset(
                vhb[lt][:, h * 66 + 65:h * 66 + 66], 0.0))
    vsync()
    barrier()
    for lt in range(LT):
        for dc in range(2):
            pb = pqk[(lt * 2 + dc) % 2]
            items = [(lambda pb=pb: pb[:, 0:512],
                      lambda ti=ti, lt=lt: xTb[ti][:, lt * 128:(lt + 1) * 128],
                      lambda ti=ti, dc=dc: wbfA[ti][:, dc * 512:(dc + 1) * 512],
                      ti == 0, ti == DT - 1) for ti in range(DT)]
            mm_t = mm_group(items)
            wait("vector", s_mm, mm_t)
            em("vector", lambda v, pb=pb, dc=dc: v.tensor_tensor(
                qrt[0][:, 0:512], pb[:, 0:512],
                rel0b[:, dc * 512:(dc + 1) * 512], ALU.add))
            for hl in range(8):
                h = dc * 8 + hl
                em("vector", lambda v, lt=lt, h=h, hl=hl: v.tensor_copy(
                    vhb[lt][:, h * 66:h * 66 + 64],
                    qrt[0][:, hl * 64:(hl + 1) * 64]))
            tgt = s_v.inc()
            em("vector", lambda v, tg=tgt: v.drain().then_inc(s_v.h, 1))
            wait("tensor", s_v, tgt)
    barrier()
    esA.close()

    if "qT" in dbg:
        for i in range(DT):
            em("sync", lambda sy, i=i: sy.dma_start(
                RAP(dbg["qT"], i * 128 * L, [[L, 128], [1, L]]), qTb[i][:]
            ).then_inc(s_st.h, 16))
            s_st.inc(16)
    if "kT" in dbg:
        for i in range(DT):
            em("sync", lambda sy, i=i: sy.dma_start(
                RAP(dbg["kT"], i * 128 * L, [[L, 128], [1, L]]), kTb[i][:]
            ).then_inc(s_st.h, 16))
            s_st.inc(16)
    if "vh" in dbg:
        for lt in range(LT):
            em("sync", lambda sy, lt=lt: sy.dma_start(
                RAP(dbg["vh"], lt * 128 * NH * 66, [[NH * 66, 128], [1, NH * 66]]),
                vhb[lt][:]).then_inc(s_st.h, 16))
            s_st.inc(16)
    if phases < 3:
        return _finish(nc, plan, dbg, (es, esB, esB2, esC))

    sqb = sbB2("sqb", [128, L], BF16)
    sgb = sbB2("sgb", [64, L], F32)
    a0w = sbB2("a0w", [32, L], BF16)
    bndt = sbB2("bndt", [128, L], BF16)
    hsb = sbB2("hsb", [64, L], BF16)
    stageQ = sbB2("stageQ", [128, L], F32)
    wqtile = sbB2("wqtile", [128, L], BF16)

    em("scalar", lambda s: s.activation(qrt[0][0:32, 0:64], rl2f[:, :], AF.Square))
    wait("vector", s_a, asynC())
    em("vector", lambda v: v.reduce_sum(qrt[1][0:32, 0:1], qrt[0][0:32, 0:64], axis=AX.X))
    wait("gpsimd", s_v, vsync())
    em("gpsimd", lambda g: g.dma_start(RAP(ubuf, 0, [[1, 32], [1, 1]]),
                                       qrt[1][0:32, 0:1]).then_inc(s_gw.h, 16))
    s_gw.inc(16)
    wait("sync", s_gw, s_gw.v)
    em("sync", lambda sy: sy.dma_start(qrt[2][0:1, 0:32],
                                       RAP(ubuf, 0, [[32, 1], [1, 32]])
                                       ).then_inc(s_tp.h, 16))
    s_tp.inc(16)
    wait("vector", s_tp, s_tp.v)
    em("vector", lambda v: v.reduce_max(qrt[2][0:1, 40:41], qrt[2][0:1, 0:32], axis=AX.X))
    wait("gpsimd", s_v, vsync())
    em("gpsimd", lambda g: g.dma_start(RAP(ubuf, 32, [[2, 1], [1, 2]]),
                                       qrt[2][0:1, 40:42]).then_inc(s_gw.h, 16))
    s_gw.inc(16)

    for srcb, off in ((qTb, 0), (kTb, 8)):
        for th in range(DT):
            tgt = s_v.inc()
            em("vector", lambda v, th=th, srcb=srcb, tg=tgt: v.tensor_tensor(
                sqb[:, :], srcb[th][:], srcb[th][:], ALU.mult).then_inc(s_v.h, 1))
            wait("tensor", s_v, tgt)
            vsafe(lambda v, off=off, th=th: v.memset(
                nrm[0:2, off + th:off + th + 1], 0.0))
            for ic in range(NIC):
                mm_t = mm_group([(lambda: paux[0:2, 0:512],
                                  lambda: hhsel[:, 0:2],
                                  lambda ic=ic: sqb[:, ic * 512:(ic + 1) * 512],
                                  True, True)])
                wait("vector", s_mm, mm_t)
                vsafe(lambda v: v.reduce_max(umath[0:2, 0:1],
                                             paux[0:2, 0:512], axis=AX.X))
                tgt = vsafe(lambda v, off=off, th=th: v.tensor_tensor(
                    nrm[0:2, off + th:off + th + 1],
                    nrm[0:2, off + th:off + th + 1],
                    umath[0:2, 0:1], ALU.max))
                wait("tensor", s_v, tgt)

    wait("gpsimd", s_v, s_v.v)
    em("gpsimd", lambda g: g.dma_start(RAP(ubuf, 0, [[16, 2], [1, 16]]),
                                       nrm[0:2, 0:16]).then_inc(s_gw.h, 16))
    s_gw.inc(16)
    wait("sync", s_gw, s_gw.v)
    em("sync", lambda sy: sy.dma_start(
        unew[:, 0:33], RAP(ubuf, 0, [[0, 128], [1, 33]])).then_inc(s_tp.h, 16))
    s_tp.inc(16)
    wait("scalar", s_tp, s_tp.v)
    asafe(lambda sc: sc.activation(unew[:, 40:73], unew[:, 0:33], AF.Ln,
                                   bias=LNG[:, :]))
    asafe(lambda sc: sc.activation(unew[:, 0:33], unew[:, 40:73],
                                   AF.Exp, scale=0.5))
    wait("vector", s_a, asynC())
    for hh in range(2):
        vsafe(lambda v, hh=hh: v.tensor_scalar(
            unew[:, 40 + hh * 8:48 + hh * 8], unew[:, hh * 16 + 8:hh * 16 + 16],
            unew[:, 32:33], None, ALU.add))
        vsafe(lambda v, hh=hh: v.tensor_tensor(
            unew[:, 60 + hh * 8:68 + hh * 8], unew[:, hh * 16:hh * 16 + 8],
            unew[:, 40 + hh * 8:48 + hh * 8], ALU.mult))
        vsafe(lambda v, hh=hh: v.tensor_scalar(
            negu[:, hh * 8:(hh + 1) * 8], unew[:, 60 + hh * 8:68 + hh * 8],
            -1.0, -2.0, ALU.mult, ALU.add))
    vsync()
    if "negu" in dbg:
        wait("sync", s_v, s_v.v)
        em("sync", lambda sy: sy.dma_start(
            RAP(dbg["negu"], 0, [[16, 2], [1, 16]]), negu[0:2, 0:16]
        ).then_inc(s_st.h, 16))
        s_st.inc(16)
    if "nrm" in dbg:
        wait("sync", s_v, s_v.v)
        em("sync", lambda sy: sy.dma_start(
            RAP(dbg["nrm"], 0, [[16, 2], [1, 16]]), nrm[0:2, 0:16]
        ).then_inc(s_st.h, 16))
        s_st.inc(16)
    if "um" in dbg:
        wait("sync", s_v, s_v.v)
        em("sync", lambda sy: sy.dma_start(
            RAP(dbg["um"], 0, [[48, 2], [1, 48]]), umath[0:2, 0:48]
        ).then_inc(s_st.h, 16))
        s_st.inc(16)
    if "unew" in dbg:
        wait("sync", s_v, s_v.v)
        em("sync", lambda sy: sy.dma_start(
            RAP(dbg["unew"], 0, [[80, 2], [1, 80]]), unew[0:2, 0:80]
        ).then_inc(s_st.h, 16))
        s_st.inc(16)
    barrier()
    if phases < 4:
        return _finish(nc, plan, dbg, (es, esB, esB2, esC))

    for ti in range(DT):
        for half in range(2):
            em("sync", lambda sy, ti=ti, half=half: sy.dma_start(
                stageQ[:, :], RAP(w1_T, ti * 128 * HID + half * 2048,
                                  [[HID, 128], [1, 2048]])
            ).then_inc(s_ld.h, 16))
            s_ld.inc(16)
            wait("scalar", s_ld, s_ld.v)
            wait("vector", s_ld, s_ld.v)
            for qq in range(4):
                vtg = q_chain(stageQ[:, qq * 512:(qq + 1) * 512], 512,
                              [(3, -6, wqtile[:, qq * 512:(qq + 1) * 512])])
            wait("gpsimd", s_v, vtg)
            em("gpsimd", lambda g, ti=ti, half=half: g.dma_start(
                RAP(w1qd, (half * 16) * (DT * 16384) + ti * 16384,
                    [[128, 128], [DT * 16384, 16], [1, 128]]),
                wqtile[:, 0:2048].rearrange("p (b c) -> p b c", c=128)
            ).then_inc(s_gw.h, 16))
            s_gw.inc(16)
            wait("sync", s_gw, s_gw.v)
            wait("vector", s_gw, s_gw.v)
    for ht in range(HT):
        em("sync", lambda sy, ht=ht: sy.dma_start(
            stageQ[:, 0:D], RAP(w2_T, ht * 128 * D, [[D, 128], [1, D]])
        ).then_inc(s_ld.h, 16))
        s_ld.inc(16)
        wait("scalar", s_ld, s_ld.v)
        wait("vector", s_ld, s_ld.v)
        for qq in range(2):
            vtg = q_chain(stageQ[:, qq * 512:(qq + 1) * 512], 512,
                          [(3, -6, wqtile[:, qq * 512:(qq + 1) * 512])])
        wait("gpsimd", s_v, vtg)
        em("gpsimd", lambda g, ht=ht: g.dma_start(
            RAP(w2qd, ht * 128 * D, [[D, 128], [1, D]]), wqtile[:, 0:D]
        ).then_inc(s_gw.h, 16))
        s_gw.inc(16)
        wait("sync", s_gw, s_gw.v)
        wait("vector", s_gw, s_gw.v)
    for ht in range(HT):
        em("sync", lambda sy, ht=ht: sy.dma_start(
            b1raw[:, ht:ht + 1], RAP(b1in, ht * 128, [[1, 128], [1, 1]])
        ).then_inc(s_ld.h, 16))
        s_ld.inc(16)
    wait("scalar", s_ld, s_ld.v)
    wait("vector", s_ld, s_ld.v)
    q_chain(b1raw[:, 0:HT], HT, [(3, -6, b1qb[:, 0:HT])])
    em("sync", lambda sy: sy.dma_start(stageQ[0:1, 0:1024],
                                       RAP(b2in, 0, [[1, 1], [1, 1024]])
                                       ).then_inc(s_ld.h, 16))
    s_ld.inc(16)
    wait("scalar", s_ld, s_ld.v)
    wait("vector", s_ld, s_ld.v)
    q_chain(stageQ[0:1, 0:512], 512, [(3, -6, qrt[0][0:1, 0:512])], P=1)
    q_chain(stageQ[0:1, 512:1024], 512, [(3, -6, qrt[1][0:1, 0:512])], P=1)
    em("vector", lambda v: v.tensor_copy(relrow[0:1, 0:512], qrt[0][0:1, 0:512]))
    em("vector", lambda v: v.tensor_copy(relrow[0:1, 512:1024], qrt[1][0:1, 0:512]))
    vsync()
    barrier()
    if phases < 5:
        return _finish(nc, plan, dbg, (es, esB, esB2, esC))

    for h in range(nheads):
        th, hh = h // 2, h % 2
        rows = slice(hh * 64, hh * 64 + 64)
        if hh == 0:
            for ic in range(NIC):
                mm_t = mm_group([(lambda: paux[0:64, 0:512],
                                  lambda: relblk[:, 0:64],
                                  lambda th=th, ic=ic:
                                  qTb[th][:, ic * 512:(ic + 1) * 512],
                                  True, True)])
                wait("scalar", s_mm, mm_t)
                tgt = s_a.inc()
                em("scalar", lambda s, ic=ic, tg=tgt: s.activation(
                    hsb[0:64, ic * 512:(ic + 1) * 512], paux[0:64, 0:512],
                    AF.Copy).then_inc(s_a.h, 1))
                wait("tensor", s_a, tgt)
            wait("gpsimd", s_a, s_a.v)
            for slot in range(2):
                em("gpsimd", lambda g, slot=slot: g.dma_start(
                    RAP(hbuf, slot * 32 * P6, [[P6, 32], [1, L]]),
                    hsb[slot * 32:(slot + 1) * 32, 0:L]
                ).then_inc(s_gw.h, 16))
                s_gw.inc(16)
        slot = hh
        wait("gpsimd", s_gw, s_gw.v)
        wait("gpsimd", s_tp, s_tp.v)
        em("gpsimd", lambda g, slot=slot: g.dma_start(
            RAP(mbuf, 0, [[4096, 16], [128, 32], [1, 128]]),
            RAP(hbuf, slot * 32 * P6, [[128, 16], [P6 + 1, 32], [1, 128]])
        ).then_inc(s_gw.h, 16))
        s_gw.inc(16)
        wait("sync", s_gw, s_gw.v)
        for jt in range(LT):
            em("sync", lambda sy, jt=jt: sy.dma_start_transpose(
                gt[jt % 2][:], RAP(mbuf, jt * 4096, [[128, 32], [1, 128]])
            ).then_inc(s_tp.h, 16))
            s_tp.inc(16)
            wait("gpsimd", s_tp, s_tp.v)
            em("gpsimd", lambda g, jt=jt: g.dma_start(
                RAP(wbuf, jt * 128 * P7 + OFFB, [[P7, 128], [1, 32]]),
                gt[jt % 2][:]).then_inc(s_gw.h, 16))
            s_gw.inc(16)
            wait("sync", s_gw, s_gw.v)

        for jt in range(LT):
            wwidth = min(160, L - jt * 128)
            wait("sync", s_mm, s_mm.v)
            em("sync", lambda sy, jt=jt, ww=wwidth: sy.dma_start(
                wint[jt % 2][:, 0:ww],
                RAP(wbuf, jt * 128 * P7 + OFFB, [[P7 - 1, 128], [1, ww]])
            ).then_inc(s_tp.h, 16))
            s_tp.inc(16)
            win_t = s_tp.v
            icmin = (jt * 128) // 512
            coff0 = jt * 128 - icmin * 512
            for ic in range(icmin, NIC):
                pb = pqk[ic % 2]
                coff = coff0 if ic == icmin else 0
                haswin = (ic == icmin) or (
                    ic == icmin + 1 and wwidth > 512 - coff0)
                items = [(lambda pb=pb, coff=coff: pb[:, coff:512],
                          lambda th=th, rows=rows, jt=jt:
                          kTb[th][rows, jt * 128:(jt + 1) * 128],
                          lambda th=th, rows=rows, ic=ic, coff=coff:
                          qTb[th][rows, ic * 512 + coff:(ic + 1) * 512],
                          True, not haswin)]
                if ic == icmin:
                    wait("tensor", s_tp, win_t)
                    w1w = min(wwidth, 512 - coff0)
                    items.append((lambda pb=pb, coff=coff, w1w=w1w:
                                  pb[:, coff:coff + w1w],
                                  lambda: identb[:, :],
                                  lambda jt=jt, w1w=w1w: wint[jt % 2][:, 0:w1w],
                                  False, True))
                elif haswin:
                    w1w = 512 - coff0
                    items.append((lambda pb=pb, w2w=wwidth - w1w: pb[:, 0:w2w],
                                  lambda: identb[:, :],
                                  lambda jt=jt, w1w=w1w, ww=wwidth:
                                  wint[jt % 2][:, w1w:ww],
                                  False, True))
                mm_t = mm_group(items)
                wait("scalar", s_mm, mm_t)
                tgt = s_a.inc()
                em("scalar", lambda s, jt=jt, ic=ic, coff=coff, pb=pb, h=h, tg=tgt:
                   s.activation(attnb[jt % 2][:, ic * 512 + coff:(ic + 1) * 512],
                                pb[:, coff:512], AF.Exp,
                                bias=negu[:, (h % 2) * 8 + h // 2:
                                          (h % 2) * 8 + h // 2 + 1]).then_inc(s_a.h, 1))
                wait("tensor", s_a, tgt)
            wait("gpsimd", s_a, s_a.v)
            em("gpsimd", lambda g, jt=jt, ww=wwidth: g.dma_start(
                RAP(dbuf, jt * 128 * SBD + OFFB, [[SBD - 1, 128], [1, ww]]),
                attnb[jt % 2][:, jt * 128:jt * 128 + ww]
            ).then_inc(s_gw.h, 16))
            s_gw.inc(16)
            for ic in range(icmin, NIC):
                coff = coff0 if ic == icmin else 0
                mm_group([(lambda ic=ic, coff=coff: pctx[ic][0:65, coff:512],
                           lambda jt=jt, h=h: vhb[jt][:, h * 66:h * 66 + 65],
                           lambda jt=jt, ic=ic, coff=coff:
                           attnb[jt % 2][:, ic * 512 + coff:(ic + 1) * 512],
                           jt == 0, False)], inc=(ic == NIC - 1))

        wait("gpsimd", s_gw, s_gw.v)
        em("gpsimd", lambda g: g.dma_start(
            RAP(cbuf, 0, [[128, L], [1, 32]]),
            RAP(dbuf, OFFB, [[SBD, L], [1, 32]])).then_inc(s_gw.h, 16))
        s_gw.inc(16)
        wait("sync", s_gw, s_gw.v)
        em("sync", lambda sy: sy.dma_start_transpose(
            bndt[:], RAP(cbuf, 0, [[128, L], [1, 128]])).then_inc(s_tp.h, 16))
        s_tp.inc(16)
        wait("gpsimd", s_tp, s_tp.v)
        em("gpsimd", lambda g: g.dma_start(
            RAP(btbuf, 0, [[P5, 32], [1, L]]), bndt[0:32, :]
        ).then_inc(s_gw.h, 16))
        s_gw.inc(16)
        wait("sync", s_gw, s_gw.v)
        em("sync", lambda sy: sy.dma_start(
            a0w[:, :], RAP(btbuf, 0, [[P5 - 1, 32], [1, L]])
        ).then_inc(s_tp.h, 16))
        s_tp.inc(16)
        wait("tensor", s_tp, s_tp.v)
        wait("tensor", s_mm, s_mm.v)
        for ic in range(NIC):
            if not skip_ctxrel:
                mm_group([(lambda ic=ic: pctx[ic][0:64, 0:512],
                           lambda: rl2[:, :],
                           lambda ic=ic: a0w[:, ic * 512:(ic + 1) * 512],
                           False, False)], inc=False)
            mm_group([(lambda ic=ic: pctx[ic][0:65, 0:1],
                       lambda: zt[:, 160:225],
                       lambda: zt[:, 0:1],
                       False, True)], inc=(ic == NIC - 1))
        mm_ctx = s_mm.v
        wait("scalar", s_mm, mm_ctx)
        for ic in range(NIC):
            em("scalar", lambda s, ic=ic: s.activation(
                qrt[0][0:1, 0:512], pctx[ic][64:65, 0:512], AF.Copy))
            em("scalar", lambda s: s.activation(qrt[1][0:1, 0:512],
                                                qrt[0][0:1, 0:512], AF.Ln,
                                                bias=LNG[0:1, :]))
            tgt = s_a.inc()
            em("scalar", lambda s, tg=tgt: s.activation(
                qrt[0][0:1, 0:512], qrt[1][0:1, 0:512], AF.Exp,
                scale=-1.0).then_inc(s_a.h, 1))
            wait("gpsimd", s_a, tgt)
            gtg = s_gw.inc(16)
            em("gpsimd", lambda g, ic=ic, tg=gtg: g.dma_start(
                RAP(sgbuf, ic * 512, [[1, 1], [1, 512]]),
                qrt[0][0:1, 0:512]).then_inc(s_gw.h, 16))
            wait("scalar", s_gw, gtg)
        wait("sync", s_gw, s_gw.v)
        em("sync", lambda sy: sy.dma_start(
            sgb[0:64, :], RAP(sgbuf, 0, [[0, 64], [1, L]])
        ).then_inc(s_tp.h, 16))
        s_tp.inc(16)
        wait("vector", s_tp, s_tp.v)
        for ic in range(NIC):
            tgt = s_v.inc()
            em("vector", lambda v, ic=ic, tg=tgt: v.tensor_tensor(
                ctxe[0:64, 0:512], pctx[ic][0:64, 0:512],
                sgb[0:64, ic * 512:(ic + 1) * 512], ALU.mult).then_inc(s_v.h, 1))
            wait("gpsimd", s_v, tgt)
            gtg = s_gw.inc(16)
            em("gpsimd", lambda g, h=h, ic=ic, tg=gtg: g.dma_start(
                RAP(ctxd, (h // 2) * 128 * L + (h % 2) * 64 * L + ic * 512,
                    [[L, 64], [1, 512]]), ctxe[0:64, 0:512]).then_inc(s_gw.h, 16))
            wait("vector", s_gw, gtg)
        wait("tensor", s_gw, s_gw.v)

    barrier()
    esB.close()
    esB2.close()
    if "ctxT" in dbg:
        tmpc = sbC("tmpc", [128, L], BF16)
        for i in range(DT):
            em("sync", lambda sy, i=i, tmpc=tmpc: sy.dma_start(
                tmpc[:], RAP(ctxd, i * 128 * L, [[L, 128], [1, L]])
            ).then_inc(s_tp.h, 16))
            s_tp.inc(16)
            wait("sync", s_tp, s_tp.v)
            em("sync", lambda sy, i=i, tmpc=tmpc: sy.dma_start(
                RAP(dbg["ctxT"], i * 128 * L, [[L, 128], [1, L]]), tmpc[:]
            ).then_inc(s_st.h, 16))
            s_st.inc(16)
            wait("sync", s_st, s_st.v)
    if phases < 6:
        return _finish(nc, plan, dbg, (es, esC))

    wbfC = [sbC(f"wbfC{i}", [128, D], BF16) for i in range(DT)]
    stageC = sbC("stageC", [128, D], F32)
    strm = [sbC(f"strm{i}", [128, D], F32) for i in range(2)]
    maskt = [sbC(f"maskt{i}", [128, D], BF16) for i in range(2)]
    lnt = sbC("lnt", [128, D], F32)
    lnsq = sbC("lnsq", [128, D], F32)
    ytile = sbC("ytile", [128, D], F32)
    xqtile = sbC("xqtile", [128, D], BF16)
    hqTb = [sbC(f"hqTb{i}", [128, 512], BF16) for i in range(HT)]
    w1qt = [sbC(f"w1qt{i}", [128, 1024], BF16) for i in range(2)]
    w2qs = [sbC(f"w2qs{i}", [128, 512], BF16) for i in range(2)]
    ysb = [sbC(f"ysb{i}", [128, D], F32) for i in range(4)]
    b2qb = sbC("b2qb", [128, D], F32)

    for dc in range(2):
        mm_t = mm_group([(lambda: paux2[0:128, 0:512],
                          lambda: onesrow[0:1, 0:128],
                          lambda dc=dc: relrow[0:1, dc * 512:(dc + 1) * 512],
                          True, True)])
        wait("vector", s_mm, mm_t)
        tgt = s_v.inc()
        em("vector", lambda v, dc=dc, tg=tgt: v.tensor_copy(
            b2qb[:, dc * 512:(dc + 1) * 512], paux2[0:128, 0:512]
        ).then_inc(s_v.h, 1))
        wait("tensor", s_v, tgt)

    load_weight_bf16(wo_T, 1.0, wbfC, stageC)
    barrier()

    def ln_tile(src_ap, dst_ap):
        vsafe(lambda v: v.reduce_sum(col[:, 0:1], src_ap, axis=AX.X))
        vsafe(lambda v: v.tensor_scalar(col[:, 0:1], col[:, 0:1],
                                        consts[:, 8:9], None, ALU.mult))
        wait("scalar", s_v, vsync())
        asafe(lambda sc: sc.activation(lnsq[:, 0:D], src_ap, AF.Square,
                                       accum_out=col[:, 1:2]))
        wait("vector", s_a, asynC())
        vsafe(lambda v: v.tensor_scalar(col[:, 1:2], col[:, 1:2],
                                        consts[:, 8:9], None, ALU.mult))
        vsafe(lambda v: v.tensor_tensor(col[:, 2:3], col[:, 0:1],
                                        col[:, 0:1], ALU.mult))
        vsafe(lambda v: v.tensor_tensor(col[:, 3:4], col[:, 1:2],
                                        col[:, 2:3], ALU.subtract))
        wait("scalar", s_v, vsync())
        asafe(lambda sc: sc.activation(col[:, 4:5], col[:, 3:4], AF.Ln,
                                       bias=consts[:, 7:8]))
        asafe(lambda sc: sc.activation(col[:, 4:5], col[:, 4:5], AF.Exp,
                                       scale=-0.5))
        wait("vector", s_a, asynC())
        vsafe(lambda v: v.scalar_tensor_tensor(
            col[:, 5:6], col[:, 0:1], -1.0, col[:, 4:5], ALU.mult, ALU.mult))
        tgt = s_v.inc()
        em("vector", lambda v, d=dst_ap, x=src_ap: v.tensor_scalar(
            d, x, col[:, 4:5], col[:, 5:6], ALU.mult, ALU.add).then_inc(s_v.h, 1))
        return tgt

    for lt in range(LT):
        em("sync", lambda sy, lt=lt: sy.dma_start(
            strm[lt % 2][:, :], RAP(x_nat, lt * 128 * D, [[D, 128], [1, D]])
        ).then_inc(s_ld.h, 16))
        s_ld.inc(16)
        x0_t = s_ld.v
        for dc in range(2):
            for ti in range(DT):
                em("sync", lambda sy, lt=lt, ti=ti: sy.dma_start(
                    ctxt[ti % 2][:, :], RAP(ctxd, ti * 128 * L + lt * 128,
                                            [[L, 128], [1, 128]])
                ).then_inc(s_tp.h, 16))
                s_tp.inc(16)
                wait("tensor", s_tp, s_tp.v)
                mm_t = mm_group([(lambda: paux[:, 0:512],
                                  lambda ti=ti: ctxt[ti % 2][:, :],
                                  lambda ti=ti, dc=dc:
                                  wbfC[ti][:, dc * 512:(dc + 1) * 512],
                                  ti == 0, ti == DT - 1)])
                wait("sync", s_mm, mm_t)
            wait("vector", s_mm, s_mm.v)
            wait("vector", s_ld, x0_t)
            tgt = s_v.inc()
            em("vector", lambda v, lt=lt, dc=dc, tg=tgt: v.tensor_tensor(
                lnt[:, dc * 512:(dc + 1) * 512], paux[:, 0:512],
                strm[lt % 2][:, dc * 512:(dc + 1) * 512], ALU.add
            ).then_inc(s_v.h, 1))
            wait("tensor", s_v, tgt)
        ln_t = ln_tile(lnt[:, 0:D], ytile[:, 0:D])
        wait("gpsimd", s_v, ln_t)
        em("gpsimd", lambda g, lt=lt: g.dma_start(
            RAP(x1d, lt * 128 * D, [[D, 128], [1, D]]), ytile[:, 0:D]
        ).then_inc(s_gw.h, 16))
        s_gw.inc(16)
        for qq in range(2):
            vtg = q_chain(ytile[:, qq * 512:(qq + 1) * 512], 512,
                          [(3, -6, xqtile[:, qq * 512:(qq + 1) * 512])])
        wait("gpsimd", s_v, vtg)
        em("gpsimd", lambda g, lt=lt: g.dma_start(
            RAP(xqd, lt * 128 * 128, [[128, 128], [L * 128, DT], [1, 128]]),
            xqtile[:, 0:D].rearrange("p (b c) -> p b c", c=128)
        ).then_inc(s_gw.h, 16))
        s_gw.inc(16)
        wait("sync", s_gw, s_gw.v)
        wait("vector", s_gw, s_gw.v)
        wait("scalar", s_gw, s_gw.v)
        if "x1" in dbg:
            em("sync", lambda sy, lt=lt: sy.dma_start(
                RAP(dbg["x1"], lt * 128 * D, [[D, 128], [1, D]]), ytile[:, 0:D]
            ).then_inc(s_st.h, 16))
            s_st.inc(16)
            wait("sync", s_st, s_st.v)
    barrier()
    for ti in range(DT):
        em("sync", lambda sy, ti=ti: sy.dma_start_transpose(
            xqTb[ti][:], RAP(xqd, ti * L * 128, [[128, L], [1, 128]])
        ).then_inc(s_tp.h, 16))
        s_tp.inc(16)
    wait("sync", s_tp, s_tp.v)
    barrier()
    if phases < 7:
        return _finish(nc, plan, dbg, (es, esC))

    for lc in range(NIC):
        for ht in range(HT):
            em("sync", lambda sy, ht=ht: sy.dma_start(
                w1qt[ht % 2][:, :], RAP(w1qd, ht * DT * 16384,
                                        [[128, 128], [16384, DT], [1, 128]])
            ).then_inc(s_tp.h, 16))
            s_tp.inc(16)
            wait("tensor", s_tp, s_tp.v)
            items = [(lambda: paux[:, 0:512],
                      lambda ht=ht, ti=ti: w1qt[ht % 2][:, ti * 128:(ti + 1) * 128],
                      lambda ti=ti, lc=lc: xqTb[ti][:, lc * 512:(lc + 1) * 512],
                      ti == 0, ti == DT - 1) for ti in range(DT)]
            mm_t = mm_group(items)
            wait("sync", s_mm, mm_t)
            wait("vector", s_mm, mm_t)
            tgt = s_v.inc()
            em("vector", lambda v, ht=ht, tg=tgt: v.tensor_scalar(
                qrt[0][:, 0:512], paux[:, 0:512], b1qb[:, ht:ht + 1], 0.0,
                ALU.add, ALU.max).then_inc(s_v.h, 1))
            wait("tensor", s_v, tgt)
            wait("scalar", s_v, tgt)
            vtg = q_chain(qrt[0][:, 0:512], 512,
                          [(10, -14, qrt[1][:, 0:512]),
                           (3, -6, hqTb[ht][:, 0:512])])
            if "hq0" in dbg and lc == 0 and ht == 0:
                wait("sync", s_v, vtg)
                em("sync", lambda sy: sy.dma_start(
                    dbg["hq0"].ap(), hqTb[0][:, 0:512]).then_inc(s_st.h, 16))
                s_st.inc(16)
        wait("tensor", s_v, s_v.v)
        for dc in range(2):
            for ht in range(HT):
                em("sync", lambda sy, ht=ht, dc=dc: sy.dma_start(
                    w2qs[ht % 2][:, :], RAP(w2qd, ht * 128 * D + dc * 512,
                                            [[D, 128], [1, 512]])
                ).then_inc(s_tp.h, 16))
                s_tp.inc(16)
                wait("tensor", s_tp, s_tp.v)
                for lt2 in range(4):
                    mm_group([(lambda lt2=lt2: pctx[lt2][:, 0:512],
                               lambda ht=ht, lt2=lt2:
                               hqTb[ht][:, lt2 * 128:(lt2 + 1) * 128],
                               lambda ht=ht: w2qs[ht % 2][:, :],
                               ht == 0, ht == HT - 1)],
                              inc=(lt2 == 3))
                wait("sync", s_mm, s_mm.v)
            mmB = s_mm.v
            wait("vector", s_mm, mmB)
            for lt2 in range(4):
                tgt = s_v.inc()
                em("vector", lambda v, lt2=lt2, dc=dc, tg=tgt: v.tensor_tensor(
                    ysb[lt2][:, dc * 512:(dc + 1) * 512], pctx[lt2][:, 0:512],
                    b2qb[:, dc * 512:(dc + 1) * 512], ALU.add).then_inc(s_v.h, 1))
            wait("tensor", s_v, s_v.v)
        wait("scalar", s_v, s_v.v)
        wait("vector", s_v, s_v.v)
        for lt2 in range(4):
            lt = lc * 4 + lt2
            em("sync", lambda sy, lt=lt: sy.dma_start(
                maskt[lt % 2][:, :],
                RAP(maskin, lt * 128 * D, [[D, 128], [1, D]])
            ).then_inc(s_ld.h, 16))
            s_ld.inc(16)
            em("sync", lambda sy, lt=lt: sy.dma_start(
                strm[lt % 2][:, :], RAP(x1d, lt * 128 * D, [[D, 128], [1, D]])
            ).then_inc(s_ld.h, 16))
            s_ld.inc(16)
            ld_t = s_ld.v
            for qq in range(2):
                vtg = q_chain(ysb[lt2][:, qq * 512:(qq + 1) * 512], 512,
                              [(10, -14, ytile[:, qq * 512:(qq + 1) * 512])])
            wait("vector", s_ld, ld_t)
            em("vector", lambda v, lt=lt: v.scalar_tensor_tensor(
                ytile[:, 0:D], ytile[:, 0:D], consts[:, 6:7], maskt[lt % 2][:, :],
                ALU.mult, ALU.mult))
            tgt = s_v.inc()
            em("vector", lambda v, lt=lt, tg=tgt: v.tensor_tensor(
                lnt[:, 0:D], ytile[:, 0:D], strm[lt % 2][:, :], ALU.add
            ).then_inc(s_v.h, 1))
            wait("scalar", s_v, tgt)
            ln_t = ln_tile(lnt[:, 0:D], ytile[:, 0:D])
            wait("sync", s_v, ln_t)
            em("sync", lambda sy, lt=lt: sy.dma_start(
                RAP(out, lt * 128 * D, [[D, 128], [1, D]]), ytile[:, 0:D]
            ).then_inc(s_st.h, 16))
            s_st.inc(16)
            wait("vector", s_st, s_st.v)
            wait("scalar", s_st, s_st.v)

    st_all = s_st.v
    for eng in ENGS:
        wait(eng, s_st, st_all)
    return _finish(nc, plan, dbg, (es, esC))


def _finish(nc, plan, dbg, stacks):
    with nc.Block() as block:
        def run_for(eng_name):
            def body(eng):
                for e, fn in plan:
                    if e == eng_name:
                        fn(eng)
            return body
        block.sync(run_for("sync"))
        block.gpsimd(run_for("gpsimd"))
        block.tensor(run_for("tensor"))
        block.vector(run_for("vector"))
        block.scalar(run_for("scalar"))
    for st in stacks:
        try:
            st.close()
        except Exception:
            pass
    return nc, dbg


def _host_constants():
    t = np.arange(L, dtype=np.float64)
    inv_freq = 1.0 / (10000.0 ** (np.arange(0, 64, 2, dtype=np.float64) / 64.0))
    ang = np.outer(t, inv_freq)
    cos = np.cos(ang)
    sin = np.sin(ang)
    p = np.arange(128)
    m = (p % 64) // 2
    c2 = cos[:, m].T.astype(np.float32)
    s2 = sin[:, m].T.astype(np.float32)
    s2 = np.where((p % 2 == 0)[:, None], -s2, s2).astype(np.float32)
    ident = np.eye(128, dtype=ml_dtypes.bfloat16)
    return np.ascontiguousarray(c2), np.ascontiguousarray(s2), ident


def _threefry2x32(k0, k1, x0, x1):
    k0 = np.uint32(k0); k1 = np.uint32(k1)
    k2 = np.uint32(k0 ^ k1 ^ np.uint32(0x1BD11BDA))
    ks = [k0, k1, k2]
    R = [[13, 15, 26, 6], [17, 29, 16, 24]]
    x0 = (x0 + ks[0]).astype(np.uint32)
    x1 = (x1 + ks[1]).astype(np.uint32)
    for i in range(5):
        for r in (R[0] if i % 2 == 0 else R[1]):
            x0 = (x0 + x1).astype(np.uint32)
            x1 = ((x1 << np.uint32(r)) | (x1 >> np.uint32(32 - r))).astype(np.uint32)
            x1 = (x0 ^ x1).astype(np.uint32)
        x0 = (x0 + ks[(i + 1) % 3]).astype(np.uint32)
        x1 = (x1 + ks[(i + 2) % 3] + np.uint32(i + 1)).astype(np.uint32)
    return x0, x1


def _dropout_mask():
    n = 8 * L * D
    iota = np.arange(n, dtype=np.uint64)
    c1 = (iota >> np.uint64(32)).astype(np.uint32)
    c2 = (iota & np.uint64(0xFFFFFFFF)).astype(np.uint32)
    b1, b2 = _threefry2x32(0, 42, c1, c2)
    if os.environ.get("BASS_SIM_FP64") == "1":
        bits = (b1.astype(np.uint64) << np.uint64(32)) | b2.astype(np.uint64)
        u = ((bits >> np.uint64(12)) | np.uint64(0x3FF0000000000000)
             ).view(np.float64) - 1.0
        keep = u < 0.8
    else:
        bits = (b1 ^ b2).astype(np.uint32)
        u = ((bits >> np.uint32(9)) | np.uint32(0x3F800000)
             ).view(np.float32) - np.float32(1.0)
        keep = u < np.float32(0.8)
    return keep.reshape(8, L, D).astype(ml_dtypes.bfloat16)


_CACHE = {}


def kernel(L0_attn_x, L0_attn_w_q, L0_attn_w_k, L0_attn_w_v, L0_attn_w_o,
           L0_attn_rel_emb, L0_ffn_w1, L0_ffn_b1, L0_ffn_w2, L0_ffn_b2):
    if "nc" not in _CACHE:
        _CACHE["nc"], _ = build_core_program()
    nc = _CACHE["nc"]

    c2, s2, ident = _host_constants()
    maskf = _dropout_mask()

    x = np.ascontiguousarray(np.asarray(L0_attn_x, dtype=np.float32))
    wq = np.ascontiguousarray(np.asarray(L0_attn_w_q, dtype=np.float32).T)
    wk = np.ascontiguousarray(np.asarray(L0_attn_w_k, dtype=np.float32).T)
    wv = np.ascontiguousarray(np.asarray(L0_attn_w_v, dtype=np.float32).T)
    wo = np.ascontiguousarray(np.asarray(L0_attn_w_o, dtype=np.float32).T)
    w1 = np.ascontiguousarray(np.asarray(L0_ffn_w1, dtype=np.float32).T)
    w2 = np.ascontiguousarray(np.asarray(L0_ffn_w2, dtype=np.float32).T)
    b1 = np.ascontiguousarray(np.asarray(L0_ffn_b1, dtype=np.float32)[None, :])
    b2 = np.ascontiguousarray(np.asarray(L0_ffn_b2, dtype=np.float32)[None, :])
    rel = np.asarray(L0_attn_rel_emb, dtype=np.float32)
    idx = [32 - w for w in range(32)]
    rel_Tw = np.ascontiguousarray(
        np.concatenate([rel[idx, :64].T, rel[0:1, :64].T], axis=1))
    rel_wrev = np.ascontiguousarray(
        np.concatenate([rel[idx, :64], rel[0:1, :64]], axis=0))

    in_maps = []
    for b in range(8):
        xb = np.ascontiguousarray(x[b])
        in_maps.append({
            "x_nat": xb,
            "x_T": np.ascontiguousarray(xb.T),
            "wq_T": wq, "wk_T": wk, "wv_T": wv, "wo_T": wo,
            "w1_T": w1, "w2_T": w2, "b1": b1, "b2": b2,
            "rel_Tw": rel_Tw, "rel_wrev": rel_wrev,
            "rel0r": np.ascontiguousarray(rel[0:1, :64]),
            "maskin": np.ascontiguousarray(maskf[b]),
            "c2": c2, "s2": s2, "ident": ident,
        })
    res = run_bass_kernel_spmd(nc, in_maps, core_ids=list(range(8)))
    return np.stack([r["out"].reshape(L, D) for r in res.results], axis=0)


# version 02
# speedup vs baseline: 1.7727x; 1.7727x over previous
import os
import math
import contextlib
import numpy as np
import ml_dtypes
import concourse.bass as bass
import concourse.mybir as mybir
from bass_rust import AP as RAP
from concourse.bass_utils import run_bass_kernel_spmd

F32 = mybir.dt.float32
BF16 = mybir.dt.bfloat16
I32 = mybir.dt.int32
_EXACT_LADDER = (os.environ.get("BASS_SIM_FP64") == "1"
                 or os.environ.get("BASS_DEV_LADDER") == "1")
AF = mybir.ActivationFunctionType
ALU = mybir.AluOpType
AX = mybir.AxisListType

L = 2048
D = 1024
NH = 16
HID = 4096
LT = L // 128
DT = D // 128
HT = HID // 128
NIC = L // 512

P6 = L + 64
P7 = 320
SBD = 320
P5 = L + 64
OFFB = 127

LN2 = float(np.log(np.float64(2.0)))
C2LN2 = float(1.0 / (2.0 * np.log(np.float64(2.0))))
NEG1E18 = -1.0e18
EPS = 1e-5
ENGS = ("sync", "gpsimd", "tensor", "vector", "scalar")


class Sem:
    def __init__(self, h):
        self.h = h
        self.v = 0

    def inc(self, n=1):
        self.v += n
        return self.v


def build_core_program(debug=(), nheads=NH, phases=99, skip_ctxrel=False):
    nc = bass.Bass("TRN2", target_bir_lowering=False, detect_race_conditions=False)

    din = lambda name, shape, dt: nc.dram_tensor(name, shape, dt, kind="ExternalInput")
    x_nat = din("x_nat", [L, D], F32)
    x_T = din("x_T", [D, L], F32)
    wq_T = din("wq_T", [D, D], F32)
    wk_T = din("wk_T", [D, D], F32)
    wv_T = din("wv_T", [D, D], F32)
    wo_T = din("wo_T", [D, D], F32)
    w1_T = din("w1_T", [D, HID], F32)
    w2_T = din("w2_T", [HID, D], F32)
    b1in = din("b1", [1, HID], F32)
    b2in = din("b2", [1, D], F32)
    rel_Tw = din("rel_Tw", [64, 33], F32)
    rel_wrev = din("rel_wrev", [33, 64], F32)
    rel0r_in = din("rel0r", [1, 64], F32)
    maskin = din("maskin", [L, D], BF16)
    c2in = din("c2", [128, L], F32)
    s2in = din("s2", [128, L], F32)
    identin = din("ident", [128, 128], BF16)

    out = nc.dram_tensor("out", [L, D], F32, kind="ExternalOutput")
    dbg = {}
    def ddbg(name, shape, dt):
        if name in debug:
            dbg[name] = nc.dram_tensor("dbg_" + name, shape, dt, kind="ExternalOutput")
    ddbg("qT", [D, L], BF16)
    ddbg("kT", [D, L], BF16)
    ddbg("vh", [L, NH * 66], BF16)
    ddbg("negu", [2, NH], F32)
    ddbg("nrm", [2, 16], F32)
    ddbg("um", [2, 48], F32)
    ddbg("unew", [2, 80], F32)
    ddbg("ctxT", [D, L], BF16)
    ddbg("sg", [1, L], F32)
    ddbg("a0", [32, L], BF16)
    ddbg("x1", [L, D], F32)
    ddbg("hq0", [128, 512], BF16)
    ddbg("qd", [4, 512], F32)

    dram = lambda name, n, dt: nc.dram_tensor(name, [n], dt, kind="Internal")
    hbuf = dram("hbuf", 2 * 32 * P6, BF16)
    mbuf = dram("mbuf", 16 * 32 * 128, BF16)
    wbuf = dram("wbuf", L * P7, BF16)
    dbuf = dram("dbuf", L * SBD, BF16)
    cbuf = dram("cbuf", L * 128, BF16)
    btbuf = dram("btbuf", 32 * P5, BF16)
    relbuf = dram("relbuf", 64 * 32, BF16)
    ubuf = dram("ubuf", 64, F32)
    sgbuf = dram("sgbuf", L, F32)
    ctxd = dram("ctxd", D * L, BF16)
    w1qd = dram("w1qd", D * HID, BF16)
    w2qd = dram("w2qd", HID * D, BF16)
    x1d = dram("x1d", L * D, F32)
    xqd = dram("xqd", L * D, BF16)

    es = contextlib.ExitStack()
    esA = contextlib.ExitStack()
    esB = contextlib.ExitStack()
    esB2 = contextlib.ExitStack()
    esC = contextlib.ExitStack()

    def mk(stk, side):
        def f(name, shape, dt):
            return stk.enter_context(nc.sbuf_tensor(name, shape, dt, side=side))
        return f
    sbG = mk(es, "left")
    sbA = mk(esA, "right")
    sbB = mk(esB, "left")
    sbB2 = mk(esB2, "right")
    sbC = mk(esC, "right")
    psum = lambda name: es.enter_context(nc.psum_tensor(name, [128, 512], F32))
    sem = lambda name: Sem(es.enter_context(nc.semaphore(name)))

    xqTb = [sbG(f"xqTb{i}", [128, L], BF16) for i in range(DT)]
    vhb = [sbG(f"vhb{i}", [128, NH * 66], BF16) for i in range(LT)]
    scr = {k: sbG("scr_" + k, [128, 512], F32) for k in ("a", "b", "c", "s", "u", "w")}
    scr["i"] = sbG("scr_i", [128, 512], I32)
    scr["j"] = sbG("scr_j", [128, 512], I32)
    qrt = [sbG(f"qrt{i}", [128, 512], F32) for i in range(3)]
    ctxe = sbG("ctxe", [64, 512], BF16)
    identb = sbG("identb", [128, 128], BF16)
    relblk = sbG("relblk", [128, 64], BF16)
    rl2 = sbG("rl2", [32, 64], BF16)
    rl2f = sbG("rl2f", [32, 64], F32)
    relrow = sbG("relrow", [1, D], BF16)
    onesrow = sbG("onesrow", [1, 128], BF16)
    onesf = sbG("onesf", [1, 128], F32)
    hhsel = sbG("hhsel", [128, 2], BF16)
    consts = sbG("consts", [128, 12], F32)
    nrm = sbG("nrm", [2, 16], F32)
    umath = sbG("umath", [2, 48], F32)
    bmcol = sbG("bmcol", [2, 2], F32)
    negu = sbG("negu", [128, NH], F32)
    unew = sbG("unew", [128, 80], F32)
    col = sbG("col", [128, 8], F32)
    b1qb = sbG("b1qb", [128, HT], F32)
    b1raw = sbG("b1raw", [128, HT], F32)
    gt = [sbG(f"gt{i}", [128, 32], BF16) for i in range(2)]
    wint = [sbG(f"wint{i}", [128, 160], BF16) for i in range(2)]
    ctxt = [sbG(f"ctxt{i}", [128, 128], BF16) for i in range(2)]
    zt = sbG("zt", [128, P7], BF16)

    c2b = sbA("c2b", [128, L], F32)
    s2b = sbA("s2b", [128, L], F32)
    stageA = sbA("stageA", [128, D], F32)
    wbfA = [sbA(f"wbfA{i}", [128, D], BF16) for i in range(DT)]
    relw = sbA("relw", [64, 33], F32)
    relv = sbA("relv", [33, 64], F32)
    rel0b = sbA("rel0b", [128, D], F32)
    rel0r = sbA("rel0r_t", [1, 64], F32)

    qTb = [sbB(f"qTb{i}", [128, L], BF16) for i in range(DT)]
    kTb = [sbB(f"kTb{i}", [128, L], BF16) for i in range(DT)]
    attnb = [sbB(f"attnb{i}", [128, L], BF16) for i in range(2)]


    pqk = [psum("pqk0"), psum("pqk1")]
    pctx = [psum(f"pctx{i}") for i in range(4)]
    paux = psum("paux")
    paux2 = psum("paux2")

    s_ld = sem("s_ld")
    s_gw = sem("s_gw")
    s_tp = sem("s_tp")
    s_mm = sem("s_mm")
    s_v = sem("s_v")
    s_a = sem("s_a")
    s_st = sem("s_st")
    s_zz = sem("s_zz")
    s_bar = sem("s_bar")

    plan = []
    em = lambda eng, fn: plan.append((eng, fn))
    def wait(eng, s, val):
        em(eng, lambda e, ss=s.h, vv=val: e.wait_ge(ss, vv))

    nbar = [0]
    def barrier():
        nbar[0] += 1
        tgt = 5 * nbar[0]
        for eng in ENGS:
            em(eng, lambda e: e.drain().then_inc(s_bar.h, 1))
            wait(eng, s_bar, tgt)

    def vsafe(fn):
        tgt = s_v.inc()
        em("vector", lambda v, f=fn: f(v).then_inc(s_v.h, 1))
        em("vector", lambda v, t=tgt: v.wait_ge(s_v.h, t))
        return tgt

    def asafe(fn):
        tgt = s_a.inc()
        em("scalar", lambda sc, f=fn: f(sc).then_inc(s_a.h, 1))
        em("scalar", lambda sc, t=tgt: sc.wait_ge(s_a.h, t))
        return tgt

    def vsync():
        tgt = s_v.inc()
        em("vector", lambda v, tg=tgt: v.drain().then_inc(s_v.h, 1))
        return tgt

    def asynC():
        tgt = s_a.inc()
        em("scalar", lambda s, tg=tgt: s.drain().then_inc(s_a.h, 1))
        return tgt

    def mm_group(items, inc=True):
        n = len(items)
        for idx, (o, l, r, st, sp) in enumerate(items):
            if idx == n - 1 and inc:
                tgt = s_mm.inc()
                em("tensor", lambda t, o=o, l=l, r=r, st=st, sp=sp, tg=tgt:
                   t.matmul(o(), l(), r(), start=st, stop=sp).then_inc(s_mm.h, 1))
            else:
                em("tensor", lambda t, o=o, l=l, r=r, st=st, sp=sp:
                   t.matmul(o(), l(), r(), start=st, stop=sp))
        return s_mm.v

    LNG = consts[:, 0:1]
    BIAS_P = {3: consts[:, 1:2], 10: consts[:, 3:4]}
    BIAS_N = {3: consts[:, 2:3], 10: consts[:, 4:5]}

    qd_done = []

    def q_chain_fast(x_ap, W, stages, P=128):
        E = vsafe if W < 64 else (lambda fn: em("vector", fn))
        sa = lambda n: scr[n][:P, :W]
        src = x_ap
        for (man, min_e, dst) in stages:
            E(lambda v: v.tensor_scalar(sa("i"), x_ap.bitcast(I32), 23, 255,
                                        ALU.logical_shift_right, ALU.bitwise_and))
            E(lambda v, me=min_e: v.tensor_scalar(
                sa("c"), sa("i"), -127, int(me), ALU.add, ALU.max))
            E(lambda v, man=man: v.tensor_scalar(
                sa("j"), sa("c"), -(1 << 23), (127 + man) << 23,
                ALU.mult, ALU.add))
            E(lambda v, src=src: v.tensor_tensor(
                sa("u"), src, scr["j"][:P, :W].bitcast(F32), ALU.mult))
            E(lambda v, man=man: v.tensor_scalar(
                sa("j"), sa("c"), 1 << 23, (127 - man) << 23,
                ALU.mult, ALU.add))
            E(lambda v: v.tensor_copy(sa("i"), sa("u")))
            E(lambda v, d=dst: v.tensor_tensor(
                d, sa("i"), scr["j"][:P, :W].bitcast(F32), ALU.mult))
            src = dst
        return vsync()

    def q_chain(x_ap, W, stages, P=128):
        if not _EXACT_LADDER:
            return q_chain_fast(x_ap, W, stages, P)
        def V(fn):
            return vsafe(fn)
        def A(fn):
            return asafe(fn)
        sa = lambda n: scr[n][:P, :W]

        def rstrict(src_ap, r_dst, tie_even):
            V(lambda v: v.tensor_copy(sa("i"), src_ap))
            V(lambda v: v.tensor_tensor(sa("b"), src_ap, sa("i"), ALU.subtract))
            V(lambda v: v.tensor_scalar(sa("w"), sa("b"), 0.5, None, ALU.is_gt))
            V(lambda v: v.tensor_tensor(sa("w"), sa("i"), sa("w"), ALU.add))
            V(lambda v: v.tensor_scalar(sa("c"), sa("b"), -0.5, None, ALU.is_lt))
            if not tie_even:
                V(lambda v, r=r_dst: v.tensor_tensor(r, sa("w"), sa("c"),
                                                     ALU.subtract))
                return
            V(lambda v: v.tensor_tensor(sa("w"), sa("w"), sa("c"), ALU.subtract))
            V(lambda v: v.tensor_tensor(sa("c"), sa("b"), sa("b"), ALU.mult))
            V(lambda v: v.tensor_scalar(sa("c"), sa("c"), 0.25, None, ALU.is_equal))
            V(lambda v: v.tensor_scalar(sa("i"), sa("i"), 1, None, ALU.bitwise_and))
            V(lambda v: v.tensor_tensor(sa("c"), sa("c"), sa("i"), ALU.mult))
            V(lambda v: v.tensor_tensor(sa("c"), sa("c"), sa("b"), ALU.mult))
            V(lambda v: v.tensor_scalar(sa("c"), sa("c"), 2.0, None, ALU.mult))
            V(lambda v, r=r_dst: v.tensor_tensor(r, sa("w"), sa("c"), ALU.add))

        wait("scalar", s_v, s_v.v)
        A(lambda sc, x=x_ap: sc.activation(sa("a"), x, AF.Square))
        A(lambda sc: sc.activation(sa("b"), sa("a"), AF.Ln, bias=LNG[:P, :]))
        wait("vector", s_a, asynC())
        V(lambda v: v.tensor_scalar(sa("u"), sa("b"), C2LN2, -0.5,
                                    ALU.mult, ALU.add))
        rstrict(sa("u"), sa("a"), False)
        if "qd" in dbg and not qd_done:
            qd_done.append(1)
            V(lambda v: v.tensor_copy(scr["i"][0:1, 0:512], scr["u"][0:1, 0:512]))
            V(lambda v: v.tensor_scalar(scr["w"][0:1, 0:512], scr["i"][0:1, 0:512],
                                        0.0, None, ALU.add))
            wait("sync", s_v, vsync())
            for row, tt in ((0, "u"), (1, "w"), (2, "a"), (3, "b")):
                em("sync", lambda sy, row=row, tt=tt: sy.dma_start(
                    RAP(dbg["qd"], row * 512, [[512, 1], [1, 512]]),
                    scr[tt][0:1, 0:512]).then_inc(s_st.h, 16))
                s_st.inc(16)
            wait("vector", s_st, s_st.v)
        src = x_ap
        for (man, min_e, dst) in stages:
            V(lambda v, me=min_e: v.tensor_scalar(
                sa("c"), sa("a"), float(me), None, ALU.max))
            V(lambda v: v.tensor_copy(sa("i"), sa("c")))
            V(lambda v, man=man: v.tensor_scalar(
                sa("j"), sa("i"), -1, 127 + man, ALU.mult, ALU.add))
            V(lambda v: v.tensor_scalar(sa("j"), sa("j"), 23, None,
                                        ALU.logical_shift_left))
            V(lambda v, src=src: v.tensor_tensor(
                sa("u"), src, scr["j"][:P, :W].bitcast(F32), ALU.mult))
            V(lambda v, man=man: v.tensor_scalar(
                sa("j"), sa("i"), 1, 127 - man, ALU.mult, ALU.add))
            V(lambda v: v.tensor_scalar(sa("j"), sa("j"), 23, None,
                                        ALU.logical_shift_left))
            V(lambda v: v.tensor_copy(sa("s"), scr["j"][:P, :W].bitcast(F32)))
            rstrict(sa("u"), sa("u"), True)
            V(lambda v, d=dst: v.tensor_tensor(d, sa("u"), sa("s"), ALU.mult))
            src = dst
        return vsync()

    for c, val in ((0, 1e-38), (1, 3.0 * LN2), (2, -3.0 * LN2), (3, 10.0 * LN2),
                   (4, -10.0 * LN2), (5, 0.8), (7, EPS), (8, 1.0 / float(D))):
        vsafe(lambda v, c=c, val=val: v.memset(consts[:, c:c + 1], val))
    vsafe(lambda v: v.reciprocal(consts[:, 6:7], consts[:, 5:6]))
    em("vector", lambda v: v.memset(zt[:], 0.0))
    em("vector", lambda v: v.memset(onesrow[:], 1.0))
    em("vector", lambda v: v.memset(onesf[:], 1.0))
    em("vector", lambda v: v.memset(hhsel[:, :], 0.0))
    em("vector", lambda v: v.memset(hhsel[0:64, 0:1], 1.0))
    em("vector", lambda v: v.memset(hhsel[64:128, 1:2], 1.0))
    em("vector", lambda v: v.memset(relblk[:, :], 0.0))
    em("vector", lambda v: v.memset(umath[:, :], 0.0))
    em("vector", lambda v: v.memset(nrm[:, :], 0.0))
    wait("gpsimd", s_v, vsync())

    def zfill(buf, span):
        pos = 0
        while pos < span:
            n = min(128 * P7, span - pos)
            full = n // P7
            if full > 0:
                em("gpsimd", lambda g, b=buf, p=pos, f=full: g.dma_start(
                    RAP(b, p, [[P7, f], [1, P7]]),
                    zt[0:f, :]).then_inc(s_zz.h, 16))
                s_zz.inc(16)
                pos += full * P7
            else:
                em("gpsimd", lambda g, b=buf, p=pos, n=n: g.dma_start(
                    RAP(b, p, [[1, 1], [1, n]]),
                    zt[0:1, 0:n]).then_inc(s_zz.h, 16))
                s_zz.inc(16)
                pos += n

    for buf, span in ((hbuf, 2 * 32 * P6), (wbuf, L * P7), (dbuf, L * SBD),
                      (cbuf, L * 128), (btbuf, 32 * P5)):
        zfill(buf, span)
    zz_all = s_zz.v

    wait("vector", s_zz, zz_all)
    em("vector", lambda v: v.memset(zt[0:128, 0:OFFB], NEG1E18))
    wait("gpsimd", s_v, vsync())
    for c in range(LT):
        em("gpsimd", lambda g, c=c: g.dma_start(
            RAP(wbuf, c * 128 * P7, [[P7, 128], [1, OFFB]]),
            zt[0:128, 0:OFFB]).then_inc(s_gw.h, 16))
        s_gw.inc(16)

    for tile, src in ((c2b, c2in), (s2b, s2in), (identb, identin),
                      (relw, rel_Tw), (relv, rel_wrev), (rel0r, rel0r_in)):
        em("sync", lambda sy, t=tile, s=src: sy.dma_start(
            t[:], s.ap()).then_inc(s_ld.h, 16))
        s_ld.inc(16)
    ld_tab = s_ld.v

    wait("vector", s_ld, ld_tab)
    vsafe(lambda v: v.tensor_scalar(
        qrt[0][0:64, 0:32], relw[:, 0:32], relw[:, 32:33], None, ALU.subtract))
    em("vector", lambda v: v.tensor_copy(ctxe[0:64, 0:32], qrt[0][0:64, 0:32]))
    wait("gpsimd", s_v, vsync())
    em("gpsimd", lambda g: g.dma_start(
        RAP(relbuf, 0, [[32, 64], [1, 32]]), ctxe[0:64, 0:32]
    ).then_inc(s_gw.h, 16))
    s_gw.inc(16)
    wait("sync", s_gw, s_gw.v)
    em("sync", lambda sy: sy.dma_start(
        relblk[0:64, 0:32], RAP(relbuf, 0, [[32, 64], [1, 32]])
    ).then_inc(s_tp.h, 16))
    s_tp.inc(16)
    em("sync", lambda sy: sy.dma_start(
        relblk[64:128, 32:64], RAP(relbuf, 0, [[32, 64], [1, 32]])
    ).then_inc(s_tp.h, 16))
    s_tp.inc(16)

    wait("tensor", s_ld, ld_tab)
    wait("tensor", s_v, s_v.v)
    mm_t = mm_group([(lambda: paux[0:32, 0:64], lambda: onesf[0:1, 0:32],
                      lambda: rel0r[0:1, 0:64], True, True)])
    wait("vector", s_mm, mm_t)
    vsafe(lambda v: v.tensor_tensor(rl2f[:, :], relv[0:32, 0:64],
                                    paux[0:32, 0:64], ALU.subtract))
    em("vector", lambda v: v.tensor_copy(rl2[:, :], rl2f[:, :]))
    vsafe(lambda v: v.tensor_copy(relrow[0:1, 0:64], rel0r[0:1, 0:64]))
    for hh in range(1, NH):
        em("vector", lambda v, hh=hh: v.tensor_copy(
            relrow[0:1, hh * 64:(hh + 1) * 64], relrow[0:1, 0:64]))
    wait("tensor", s_v, vsync())
    for dc in range(2):
        mm_t = mm_group([(lambda: paux2[0:128, 0:512],
                          lambda: onesrow[0:1, 0:128],
                          lambda dc=dc: relrow[0:1, dc * 512:(dc + 1) * 512],
                          True, True)])
        wait("vector", s_mm, mm_t)
        vt = s_v.inc()
        em("vector", lambda v, dc=dc, tg=vt: v.tensor_copy(
            rel0b[:, dc * 512:(dc + 1) * 512], paux2[0:128, 0:512]
        ).then_inc(s_v.h, 1))
        wait("tensor", s_v, vt)
    barrier()

    xTb = xqTb
    for i in range(DT):
        for hf in range(2):
            em("sync", lambda sy, i=i, hf=hf: sy.dma_start(
                stageA[:], RAP(x_T, i * 128 * L + hf * 1024, [[L, 128], [1, 1024]])
            ).then_inc(s_ld.h, 16))
            s_ld.inc(16)
            wait("scalar", s_ld, s_ld.v)
            tgt = s_a.inc()
            em("scalar", lambda s, i=i, hf=hf, tg=tgt: s.activation(
                xTb[i][:, hf * 1024:(hf + 1) * 1024], stageA[:],
                AF.Copy).then_inc(s_a.h, 1))
            wait("sync", s_a, tgt)
    barrier()

    def load_weight_bf16(src, scale, wbf, stg):
        for i in range(DT):
            em("sync", lambda sy, i=i, s=src, stg=stg: sy.dma_start(
                stg[:, 0:D], RAP(s, i * 128 * D, [[D, 128], [1, D]])
            ).then_inc(s_ld.h, 16))
            s_ld.inc(16)
            wait("scalar", s_ld, s_ld.v)
            tgt = s_a.inc()
            em("scalar", lambda s, i=i, sc=scale, wbf=wbf, stg=stg, tg=tgt:
               s.activation(wbf[i][:, :], stg[:, 0:D], AF.Copy, scale=sc
                            ).then_inc(s_a.h, 1))
            wait("sync", s_a, tgt)

    def proj_qk(dstb):
        for dt_ in range(DT):
            for lc in range(NIC):
                pb = pqk[(dt_ * NIC + lc) % 2]
                items = [(lambda pb=pb: pb[:, 0:512],
                          lambda ti=ti, dt_=dt_: wbfA[ti][:, dt_ * 128:(dt_ + 1) * 128],
                          lambda ti=ti, lc=lc: xTb[ti][:, lc * 512:(lc + 1) * 512],
                          ti == 0, ti == DT - 1) for ti in range(DT)]
                mm_t = mm_group(items)
                wait("vector", s_mm, mm_t)
                sl = slice(lc * 512, (lc + 1) * 512)
                em("vector", lambda v, pb=pb: v.tensor_copy(
                    qrt[0][:, 0:512], pb[:, 0:512]))
                em("vector", lambda v: v.stream_shuffle(
                    qrt[1][:, 0:512], qrt[0][:, 0:512], [i ^ 1 for i in range(32)]))
                em("vector", lambda v, sl=sl: v.tensor_tensor(
                    qrt[2][:, 0:512], qrt[0][:, 0:512], c2b[:, sl], ALU.mult))
                em("vector", lambda v, sl=sl: v.tensor_tensor(
                    qrt[1][:, 0:512], qrt[1][:, 0:512], s2b[:, sl], ALU.mult))
                tgt = s_v.inc()
                em("vector", lambda v, dt_=dt_, sl=sl, dstb=dstb, tg=tgt:
                   v.tensor_tensor(dstb[dt_][:, sl], qrt[2][:, 0:512],
                                   qrt[1][:, 0:512], ALU.add).then_inc(s_v.h, 1))
                wait("tensor", s_v, tgt)

    load_weight_bf16(wq_T, 0.125, wbfA, stageA)
    barrier()
    proj_qk(qTb)
    barrier()
    load_weight_bf16(wk_T, 1.0, wbfA, stageA)
    barrier()
    proj_qk(kTb)
    barrier()
    load_weight_bf16(wv_T, 1.0, wbfA, stageA)
    for lt in range(LT):
        for h in range(NH):
            em("vector", lambda v, lt=lt, h=h: v.memset(
                vhb[lt][:, h * 66 + 64:h * 66 + 65], 1.0))
            em("vector", lambda v, lt=lt, h=h: v.memset(
                vhb[lt][:, h * 66 + 65:h * 66 + 66], 0.0))
    vsync()
    barrier()
    for lt in range(LT):
        for dc in range(2):
            pb = pqk[(lt * 2 + dc) % 2]
            items = [(lambda pb=pb: pb[:, 0:512],
                      lambda ti=ti, lt=lt: xTb[ti][:, lt * 128:(lt + 1) * 128],
                      lambda ti=ti, dc=dc: wbfA[ti][:, dc * 512:(dc + 1) * 512],
                      ti == 0, ti == DT - 1) for ti in range(DT)]
            mm_t = mm_group(items)
            wait("vector", s_mm, mm_t)
            em("vector", lambda v, pb=pb, dc=dc: v.tensor_tensor(
                qrt[0][:, 0:512], pb[:, 0:512],
                rel0b[:, dc * 512:(dc + 1) * 512], ALU.add))
            for hl in range(8):
                h = dc * 8 + hl
                em("vector", lambda v, lt=lt, h=h, hl=hl: v.tensor_copy(
                    vhb[lt][:, h * 66:h * 66 + 64],
                    qrt[0][:, hl * 64:(hl + 1) * 64]))
            tgt = s_v.inc()
            em("vector", lambda v, tg=tgt: v.drain().then_inc(s_v.h, 1))
            wait("tensor", s_v, tgt)
    barrier()
    esA.close()

    if "qT" in dbg:
        for i in range(DT):
            em("sync", lambda sy, i=i: sy.dma_start(
                RAP(dbg["qT"], i * 128 * L, [[L, 128], [1, L]]), qTb[i][:]
            ).then_inc(s_st.h, 16))
            s_st.inc(16)
    if "kT" in dbg:
        for i in range(DT):
            em("sync", lambda sy, i=i: sy.dma_start(
                RAP(dbg["kT"], i * 128 * L, [[L, 128], [1, L]]), kTb[i][:]
            ).then_inc(s_st.h, 16))
            s_st.inc(16)
    if "vh" in dbg:
        for lt in range(LT):
            em("sync", lambda sy, lt=lt: sy.dma_start(
                RAP(dbg["vh"], lt * 128 * NH * 66, [[NH * 66, 128], [1, NH * 66]]),
                vhb[lt][:]).then_inc(s_st.h, 16))
            s_st.inc(16)
    if phases < 3:
        return _finish(nc, plan, dbg, (es, esB, esB2, esC))

    sqb = sbB2("sqb", [128, L], BF16)
    sgb = sbB2("sgb", [64, L], F32)
    a0w = sbB2("a0w", [32, L], BF16)
    bndt = sbB2("bndt", [128, L], BF16)
    hsb = sbB2("hsb", [64, L], BF16)
    stageQ = sbB2("stageQ", [128, L], F32)
    wqtile = sbB2("wqtile", [128, L], BF16)

    em("scalar", lambda s: s.activation(qrt[0][0:32, 0:64], rl2f[:, :], AF.Square))
    wait("vector", s_a, asynC())
    em("vector", lambda v: v.reduce_sum(qrt[1][0:32, 0:1], qrt[0][0:32, 0:64], axis=AX.X))
    wait("gpsimd", s_v, vsync())
    em("gpsimd", lambda g: g.dma_start(RAP(ubuf, 0, [[1, 32], [1, 1]]),
                                       qrt[1][0:32, 0:1]).then_inc(s_gw.h, 16))
    s_gw.inc(16)
    wait("sync", s_gw, s_gw.v)
    em("sync", lambda sy: sy.dma_start(qrt[2][0:1, 0:32],
                                       RAP(ubuf, 0, [[32, 1], [1, 32]])
                                       ).then_inc(s_tp.h, 16))
    s_tp.inc(16)
    wait("vector", s_tp, s_tp.v)
    em("vector", lambda v: v.reduce_max(qrt[2][0:1, 40:41], qrt[2][0:1, 0:32], axis=AX.X))
    wait("gpsimd", s_v, vsync())
    em("gpsimd", lambda g: g.dma_start(RAP(ubuf, 32, [[2, 1], [1, 2]]),
                                       qrt[2][0:1, 40:42]).then_inc(s_gw.h, 16))
    s_gw.inc(16)

    for srcb, off in ((qTb, 0), (kTb, 8)):
        for th in range(DT):
            tgt = s_v.inc()
            em("vector", lambda v, th=th, srcb=srcb, tg=tgt: v.tensor_tensor(
                sqb[:, :], srcb[th][:], srcb[th][:], ALU.mult).then_inc(s_v.h, 1))
            wait("tensor", s_v, tgt)
            vsafe(lambda v, off=off, th=th: v.memset(
                nrm[0:2, off + th:off + th + 1], 0.0))
            for ic in range(NIC):
                mm_t = mm_group([(lambda: paux[0:2, 0:512],
                                  lambda: hhsel[:, 0:2],
                                  lambda ic=ic: sqb[:, ic * 512:(ic + 1) * 512],
                                  True, True)])
                wait("vector", s_mm, mm_t)
                vsafe(lambda v: v.reduce_max(umath[0:2, 0:1],
                                             paux[0:2, 0:512], axis=AX.X))
                tgt = vsafe(lambda v, off=off, th=th: v.tensor_tensor(
                    nrm[0:2, off + th:off + th + 1],
                    nrm[0:2, off + th:off + th + 1],
                    umath[0:2, 0:1], ALU.max))
                wait("tensor", s_v, tgt)

    wait("gpsimd", s_v, s_v.v)
    em("gpsimd", lambda g: g.dma_start(RAP(ubuf, 0, [[16, 2], [1, 16]]),
                                       nrm[0:2, 0:16]).then_inc(s_gw.h, 16))
    s_gw.inc(16)
    wait("sync", s_gw, s_gw.v)
    em("sync", lambda sy: sy.dma_start(
        unew[:, 0:33], RAP(ubuf, 0, [[0, 128], [1, 33]])).then_inc(s_tp.h, 16))
    s_tp.inc(16)
    wait("scalar", s_tp, s_tp.v)
    asafe(lambda sc: sc.activation(unew[:, 40:73], unew[:, 0:33], AF.Ln,
                                   bias=LNG[:, :]))
    asafe(lambda sc: sc.activation(unew[:, 0:33], unew[:, 40:73],
                                   AF.Exp, scale=0.5))
    wait("vector", s_a, asynC())
    for hh in range(2):
        vsafe(lambda v, hh=hh: v.tensor_scalar(
            unew[:, 40 + hh * 8:48 + hh * 8], unew[:, hh * 16 + 8:hh * 16 + 16],
            unew[:, 32:33], None, ALU.add))
        vsafe(lambda v, hh=hh: v.tensor_tensor(
            unew[:, 60 + hh * 8:68 + hh * 8], unew[:, hh * 16:hh * 16 + 8],
            unew[:, 40 + hh * 8:48 + hh * 8], ALU.mult))
        vsafe(lambda v, hh=hh: v.tensor_scalar(
            negu[:, hh * 8:(hh + 1) * 8], unew[:, 60 + hh * 8:68 + hh * 8],
            -1.0, -2.0, ALU.mult, ALU.add))
    vsync()
    if "negu" in dbg:
        wait("sync", s_v, s_v.v)
        em("sync", lambda sy: sy.dma_start(
            RAP(dbg["negu"], 0, [[16, 2], [1, 16]]), negu[0:2, 0:16]
        ).then_inc(s_st.h, 16))
        s_st.inc(16)
    if "nrm" in dbg:
        wait("sync", s_v, s_v.v)
        em("sync", lambda sy: sy.dma_start(
            RAP(dbg["nrm"], 0, [[16, 2], [1, 16]]), nrm[0:2, 0:16]
        ).then_inc(s_st.h, 16))
        s_st.inc(16)
    if "um" in dbg:
        wait("sync", s_v, s_v.v)
        em("sync", lambda sy: sy.dma_start(
            RAP(dbg["um"], 0, [[48, 2], [1, 48]]), umath[0:2, 0:48]
        ).then_inc(s_st.h, 16))
        s_st.inc(16)
    if "unew" in dbg:
        wait("sync", s_v, s_v.v)
        em("sync", lambda sy: sy.dma_start(
            RAP(dbg["unew"], 0, [[80, 2], [1, 80]]), unew[0:2, 0:80]
        ).then_inc(s_st.h, 16))
        s_st.inc(16)
    barrier()
    if phases < 4:
        return _finish(nc, plan, dbg, (es, esB, esB2, esC))

    for ti in range(DT):
        for half in range(2):
            em("sync", lambda sy, ti=ti, half=half: sy.dma_start(
                stageQ[:, :], RAP(w1_T, ti * 128 * HID + half * 2048,
                                  [[HID, 128], [1, 2048]])
            ).then_inc(s_ld.h, 16))
            s_ld.inc(16)
            wait("scalar", s_ld, s_ld.v)
            wait("vector", s_ld, s_ld.v)
            for qq in range(4):
                vtg = q_chain(stageQ[:, qq * 512:(qq + 1) * 512], 512,
                              [(3, -6, wqtile[:, qq * 512:(qq + 1) * 512])])
            wait("gpsimd", s_v, vtg)
            em("gpsimd", lambda g, ti=ti, half=half: g.dma_start(
                RAP(w1qd, (half * 16) * (DT * 16384) + ti * 16384,
                    [[128, 128], [DT * 16384, 16], [1, 128]]),
                wqtile[:, 0:2048].rearrange("p (b c) -> p b c", c=128)
            ).then_inc(s_gw.h, 16))
            s_gw.inc(16)
            wait("sync", s_gw, s_gw.v)
            wait("vector", s_gw, s_gw.v)
    for ht in range(HT):
        em("sync", lambda sy, ht=ht: sy.dma_start(
            stageQ[:, 0:D], RAP(w2_T, ht * 128 * D, [[D, 128], [1, D]])
        ).then_inc(s_ld.h, 16))
        s_ld.inc(16)
        wait("scalar", s_ld, s_ld.v)
        wait("vector", s_ld, s_ld.v)
        for qq in range(2):
            vtg = q_chain(stageQ[:, qq * 512:(qq + 1) * 512], 512,
                          [(3, -6, wqtile[:, qq * 512:(qq + 1) * 512])])
        wait("gpsimd", s_v, vtg)
        em("gpsimd", lambda g, ht=ht: g.dma_start(
            RAP(w2qd, ht * 128 * D, [[D, 128], [1, D]]), wqtile[:, 0:D]
        ).then_inc(s_gw.h, 16))
        s_gw.inc(16)
        wait("sync", s_gw, s_gw.v)
        wait("vector", s_gw, s_gw.v)
    for ht in range(HT):
        em("sync", lambda sy, ht=ht: sy.dma_start(
            b1raw[:, ht:ht + 1], RAP(b1in, ht * 128, [[1, 128], [1, 1]])
        ).then_inc(s_ld.h, 16))
        s_ld.inc(16)
    wait("scalar", s_ld, s_ld.v)
    wait("vector", s_ld, s_ld.v)
    q_chain(b1raw[:, 0:HT], HT, [(3, -6, b1qb[:, 0:HT])])
    em("sync", lambda sy: sy.dma_start(stageQ[0:1, 0:1024],
                                       RAP(b2in, 0, [[1, 1], [1, 1024]])
                                       ).then_inc(s_ld.h, 16))
    s_ld.inc(16)
    wait("scalar", s_ld, s_ld.v)
    wait("vector", s_ld, s_ld.v)
    q_chain(stageQ[0:1, 0:512], 512, [(3, -6, qrt[0][0:1, 0:512])], P=1)
    q_chain(stageQ[0:1, 512:1024], 512, [(3, -6, qrt[1][0:1, 0:512])], P=1)
    em("vector", lambda v: v.tensor_copy(relrow[0:1, 0:512], qrt[0][0:1, 0:512]))
    em("vector", lambda v: v.tensor_copy(relrow[0:1, 512:1024], qrt[1][0:1, 0:512]))
    vsync()
    barrier()
    if phases < 5:
        return _finish(nc, plan, dbg, (es, esB, esB2, esC))

    for h in range(nheads):
        th, hh = h // 2, h % 2
        rows = slice(hh * 64, hh * 64 + 64)
        if hh == 0:
            for ic in range(NIC):
                mm_t = mm_group([(lambda: paux[0:64, 0:512],
                                  lambda: relblk[:, 0:64],
                                  lambda th=th, ic=ic:
                                  qTb[th][:, ic * 512:(ic + 1) * 512],
                                  True, True)])
                wait("scalar", s_mm, mm_t)
                tgt = s_a.inc()
                em("scalar", lambda s, ic=ic, tg=tgt: s.activation(
                    hsb[0:64, ic * 512:(ic + 1) * 512], paux[0:64, 0:512],
                    AF.Copy).then_inc(s_a.h, 1))
                wait("tensor", s_a, tgt)
            wait("gpsimd", s_a, s_a.v)
            for slot in range(2):
                em("gpsimd", lambda g, slot=slot: g.dma_start(
                    RAP(hbuf, slot * 32 * P6, [[P6, 32], [1, L]]),
                    hsb[slot * 32:(slot + 1) * 32, 0:L]
                ).then_inc(s_gw.h, 16))
                s_gw.inc(16)
        slot = hh
        wait("gpsimd", s_gw, s_gw.v)
        wait("gpsimd", s_tp, s_tp.v)
        em("gpsimd", lambda g, slot=slot: g.dma_start(
            RAP(mbuf, 0, [[4096, 16], [128, 32], [1, 128]]),
            RAP(hbuf, slot * 32 * P6, [[128, 16], [P6 + 1, 32], [1, 128]])
        ).then_inc(s_gw.h, 16))
        s_gw.inc(16)
        wait("sync", s_gw, s_gw.v)
        for jt in range(LT):
            em("sync", lambda sy, jt=jt: sy.dma_start_transpose(
                gt[jt % 2][:], RAP(mbuf, jt * 4096, [[128, 32], [1, 128]])
            ).then_inc(s_tp.h, 16))
            s_tp.inc(16)
            wait("gpsimd", s_tp, s_tp.v)
            em("gpsimd", lambda g, jt=jt: g.dma_start(
                RAP(wbuf, jt * 128 * P7 + OFFB, [[P7, 128], [1, 32]]),
                gt[jt % 2][:]).then_inc(s_gw.h, 16))
            s_gw.inc(16)
            wait("sync", s_gw, s_gw.v)

        for jt in range(LT):
            wwidth = min(160, L - jt * 128)
            wait("sync", s_mm, s_mm.v)
            em("sync", lambda sy, jt=jt, ww=wwidth: sy.dma_start(
                wint[jt % 2][:, 0:ww],
                RAP(wbuf, jt * 128 * P7 + OFFB, [[P7 - 1, 128], [1, ww]])
            ).then_inc(s_tp.h, 16))
            s_tp.inc(16)
            win_t = s_tp.v
            icmin = (jt * 128) // 512
            coff0 = jt * 128 - icmin * 512
            for ic in range(icmin, NIC):
                pb = pqk[ic % 2]
                coff = coff0 if ic == icmin else 0
                haswin = (ic == icmin) or (
                    ic == icmin + 1 and wwidth > 512 - coff0)
                items = [(lambda pb=pb, coff=coff: pb[:, coff:512],
                          lambda th=th, rows=rows, jt=jt:
                          kTb[th][rows, jt * 128:(jt + 1) * 128],
                          lambda th=th, rows=rows, ic=ic, coff=coff:
                          qTb[th][rows, ic * 512 + coff:(ic + 1) * 512],
                          True, not haswin)]
                if ic == icmin:
                    wait("tensor", s_tp, win_t)
                    w1w = min(wwidth, 512 - coff0)
                    items.append((lambda pb=pb, coff=coff, w1w=w1w:
                                  pb[:, coff:coff + w1w],
                                  lambda: identb[:, :],
                                  lambda jt=jt, w1w=w1w: wint[jt % 2][:, 0:w1w],
                                  False, True))
                elif haswin:
                    w1w = 512 - coff0
                    items.append((lambda pb=pb, w2w=wwidth - w1w: pb[:, 0:w2w],
                                  lambda: identb[:, :],
                                  lambda jt=jt, w1w=w1w, ww=wwidth:
                                  wint[jt % 2][:, w1w:ww],
                                  False, True))
                mm_t = mm_group(items)
                wait("scalar", s_mm, mm_t)
                tgt = s_a.inc()
                em("scalar", lambda s, jt=jt, ic=ic, coff=coff, pb=pb, h=h, tg=tgt:
                   s.activation(attnb[jt % 2][:, ic * 512 + coff:(ic + 1) * 512],
                                pb[:, coff:512], AF.Exp,
                                bias=negu[:, (h % 2) * 8 + h // 2:
                                          (h % 2) * 8 + h // 2 + 1]).then_inc(s_a.h, 1))
                wait("tensor", s_a, tgt)
            wait("gpsimd", s_a, s_a.v)
            em("gpsimd", lambda g, jt=jt, ww=wwidth: g.dma_start(
                RAP(dbuf, jt * 128 * SBD + OFFB, [[SBD - 1, 128], [1, ww]]),
                attnb[jt % 2][:, jt * 128:jt * 128 + ww]
            ).then_inc(s_gw.h, 16))
            s_gw.inc(16)
            for ic in range(icmin, NIC):
                coff = coff0 if ic == icmin else 0
                mm_group([(lambda ic=ic, coff=coff: pctx[ic][0:65, coff:512],
                           lambda jt=jt, h=h: vhb[jt][:, h * 66:h * 66 + 65],
                           lambda jt=jt, ic=ic, coff=coff:
                           attnb[jt % 2][:, ic * 512 + coff:(ic + 1) * 512],
                           jt == 0, False)], inc=(ic == NIC - 1))

        wait("gpsimd", s_gw, s_gw.v)
        em("gpsimd", lambda g: g.dma_start(
            RAP(cbuf, 0, [[128, L], [1, 32]]),
            RAP(dbuf, OFFB, [[SBD, L], [1, 32]])).then_inc(s_gw.h, 16))
        s_gw.inc(16)
        wait("sync", s_gw, s_gw.v)
        em("sync", lambda sy: sy.dma_start_transpose(
            bndt[:], RAP(cbuf, 0, [[128, L], [1, 128]])).then_inc(s_tp.h, 16))
        s_tp.inc(16)
        wait("gpsimd", s_tp, s_tp.v)
        em("gpsimd", lambda g: g.dma_start(
            RAP(btbuf, 0, [[P5, 32], [1, L]]), bndt[0:32, :]
        ).then_inc(s_gw.h, 16))
        s_gw.inc(16)
        wait("sync", s_gw, s_gw.v)
        em("sync", lambda sy: sy.dma_start(
            a0w[:, :], RAP(btbuf, 0, [[P5 - 1, 32], [1, L]])
        ).then_inc(s_tp.h, 16))
        s_tp.inc(16)
        wait("tensor", s_tp, s_tp.v)
        wait("tensor", s_mm, s_mm.v)
        for ic in range(NIC):
            if not skip_ctxrel:
                mm_group([(lambda ic=ic: pctx[ic][0:64, 0:512],
                           lambda: rl2[:, :],
                           lambda ic=ic: a0w[:, ic * 512:(ic + 1) * 512],
                           False, False)], inc=False)
            mm_group([(lambda ic=ic: pctx[ic][0:65, 0:1],
                       lambda: zt[:, 160:225],
                       lambda: zt[:, 0:1],
                       False, True)], inc=(ic == NIC - 1))
        mm_ctx = s_mm.v
        wait("scalar", s_mm, mm_ctx)
        for ic in range(NIC):
            em("scalar", lambda s, ic=ic: s.activation(
                qrt[0][0:1, 0:512], pctx[ic][64:65, 0:512], AF.Copy))
            em("scalar", lambda s: s.activation(qrt[1][0:1, 0:512],
                                                qrt[0][0:1, 0:512], AF.Ln,
                                                bias=LNG[0:1, :]))
            tgt = s_a.inc()
            em("scalar", lambda s, tg=tgt: s.activation(
                qrt[0][0:1, 0:512], qrt[1][0:1, 0:512], AF.Exp,
                scale=-1.0).then_inc(s_a.h, 1))
            wait("gpsimd", s_a, tgt)
            gtg = s_gw.inc(16)
            em("gpsimd", lambda g, ic=ic, tg=gtg: g.dma_start(
                RAP(sgbuf, ic * 512, [[1, 1], [1, 512]]),
                qrt[0][0:1, 0:512]).then_inc(s_gw.h, 16))
            wait("scalar", s_gw, gtg)
        wait("sync", s_gw, s_gw.v)
        em("sync", lambda sy: sy.dma_start(
            sgb[0:64, :], RAP(sgbuf, 0, [[0, 64], [1, L]])
        ).then_inc(s_tp.h, 16))
        s_tp.inc(16)
        wait("vector", s_tp, s_tp.v)
        for ic in range(NIC):
            tgt = s_v.inc()
            em("vector", lambda v, ic=ic, tg=tgt: v.tensor_tensor(
                ctxe[0:64, 0:512], pctx[ic][0:64, 0:512],
                sgb[0:64, ic * 512:(ic + 1) * 512], ALU.mult).then_inc(s_v.h, 1))
            wait("gpsimd", s_v, tgt)
            gtg = s_gw.inc(16)
            em("gpsimd", lambda g, h=h, ic=ic, tg=gtg: g.dma_start(
                RAP(ctxd, (h // 2) * 128 * L + (h % 2) * 64 * L + ic * 512,
                    [[L, 64], [1, 512]]), ctxe[0:64, 0:512]).then_inc(s_gw.h, 16))
            wait("vector", s_gw, gtg)
        wait("tensor", s_gw, s_gw.v)

    barrier()
    esB.close()
    esB2.close()
    if "ctxT" in dbg:
        tmpc = sbC("tmpc", [128, L], BF16)
        for i in range(DT):
            em("sync", lambda sy, i=i, tmpc=tmpc: sy.dma_start(
                tmpc[:], RAP(ctxd, i * 128 * L, [[L, 128], [1, L]])
            ).then_inc(s_tp.h, 16))
            s_tp.inc(16)
            wait("sync", s_tp, s_tp.v)
            em("sync", lambda sy, i=i, tmpc=tmpc: sy.dma_start(
                RAP(dbg["ctxT"], i * 128 * L, [[L, 128], [1, L]]), tmpc[:]
            ).then_inc(s_st.h, 16))
            s_st.inc(16)
            wait("sync", s_st, s_st.v)
    if phases < 6:
        return _finish(nc, plan, dbg, (es, esC))

    wbfC = [sbC(f"wbfC{i}", [128, D], BF16) for i in range(DT)]
    stageC = sbC("stageC", [128, D], F32)
    strm = [sbC(f"strm{i}", [128, D], F32) for i in range(2)]
    maskt = [sbC(f"maskt{i}", [128, D], BF16) for i in range(2)]
    lnt = sbC("lnt", [128, D], F32)
    lnsq = sbC("lnsq", [128, D], F32)
    ytile = sbC("ytile", [128, D], F32)
    xqtile = sbC("xqtile", [128, D], BF16)
    hqTb = [sbC(f"hqTb{i}", [128, 512], BF16) for i in range(HT)]
    w1qt = [sbC(f"w1qt{i}", [128, 1024], BF16) for i in range(2)]
    w2qs = [sbC(f"w2qs{i}", [128, 512], BF16) for i in range(2)]
    ysb = [sbC(f"ysb{i}", [128, D], F32) for i in range(4)]
    b2qb = sbC("b2qb", [128, D], F32)

    for dc in range(2):
        mm_t = mm_group([(lambda: paux2[0:128, 0:512],
                          lambda: onesrow[0:1, 0:128],
                          lambda dc=dc: relrow[0:1, dc * 512:(dc + 1) * 512],
                          True, True)])
        wait("vector", s_mm, mm_t)
        tgt = s_v.inc()
        em("vector", lambda v, dc=dc, tg=tgt: v.tensor_copy(
            b2qb[:, dc * 512:(dc + 1) * 512], paux2[0:128, 0:512]
        ).then_inc(s_v.h, 1))
        wait("tensor", s_v, tgt)

    load_weight_bf16(wo_T, 1.0, wbfC, stageC)
    barrier()

    def ln_tile(src_ap, dst_ap):
        vsafe(lambda v: v.reduce_sum(col[:, 0:1], src_ap, axis=AX.X))
        vsafe(lambda v: v.tensor_scalar(col[:, 0:1], col[:, 0:1],
                                        consts[:, 8:9], None, ALU.mult))
        wait("scalar", s_v, vsync())
        asafe(lambda sc: sc.activation(lnsq[:, 0:D], src_ap, AF.Square,
                                       accum_out=col[:, 1:2]))
        wait("vector", s_a, asynC())
        vsafe(lambda v: v.tensor_scalar(col[:, 1:2], col[:, 1:2],
                                        consts[:, 8:9], None, ALU.mult))
        vsafe(lambda v: v.tensor_tensor(col[:, 2:3], col[:, 0:1],
                                        col[:, 0:1], ALU.mult))
        vsafe(lambda v: v.tensor_tensor(col[:, 3:4], col[:, 1:2],
                                        col[:, 2:3], ALU.subtract))
        wait("scalar", s_v, vsync())
        asafe(lambda sc: sc.activation(col[:, 4:5], col[:, 3:4], AF.Ln,
                                       bias=consts[:, 7:8]))
        asafe(lambda sc: sc.activation(col[:, 4:5], col[:, 4:5], AF.Exp,
                                       scale=-0.5))
        wait("vector", s_a, asynC())
        vsafe(lambda v: v.scalar_tensor_tensor(
            col[:, 5:6], col[:, 0:1], -1.0, col[:, 4:5], ALU.mult, ALU.mult))
        tgt = s_v.inc()
        em("vector", lambda v, d=dst_ap, x=src_ap: v.tensor_scalar(
            d, x, col[:, 4:5], col[:, 5:6], ALU.mult, ALU.add).then_inc(s_v.h, 1))
        return tgt

    for lt in range(LT):
        em("sync", lambda sy, lt=lt: sy.dma_start(
            strm[lt % 2][:, :], RAP(x_nat, lt * 128 * D, [[D, 128], [1, D]])
        ).then_inc(s_ld.h, 16))
        s_ld.inc(16)
        x0_t = s_ld.v
        for dc in range(2):
            for ti in range(DT):
                em("sync", lambda sy, lt=lt, ti=ti: sy.dma_start(
                    ctxt[ti % 2][:, :], RAP(ctxd, ti * 128 * L + lt * 128,
                                            [[L, 128], [1, 128]])
                ).then_inc(s_tp.h, 16))
                s_tp.inc(16)
                wait("tensor", s_tp, s_tp.v)
                mm_t = mm_group([(lambda: paux[:, 0:512],
                                  lambda ti=ti: ctxt[ti % 2][:, :],
                                  lambda ti=ti, dc=dc:
                                  wbfC[ti][:, dc * 512:(dc + 1) * 512],
                                  ti == 0, ti == DT - 1)])
                wait("sync", s_mm, mm_t)
            wait("vector", s_mm, s_mm.v)
            wait("vector", s_ld, x0_t)
            tgt = s_v.inc()
            em("vector", lambda v, lt=lt, dc=dc, tg=tgt: v.tensor_tensor(
                lnt[:, dc * 512:(dc + 1) * 512], paux[:, 0:512],
                strm[lt % 2][:, dc * 512:(dc + 1) * 512], ALU.add
            ).then_inc(s_v.h, 1))
            wait("tensor", s_v, tgt)
        ln_t = ln_tile(lnt[:, 0:D], ytile[:, 0:D])
        wait("gpsimd", s_v, ln_t)
        em("gpsimd", lambda g, lt=lt: g.dma_start(
            RAP(x1d, lt * 128 * D, [[D, 128], [1, D]]), ytile[:, 0:D]
        ).then_inc(s_gw.h, 16))
        s_gw.inc(16)
        for qq in range(2):
            vtg = q_chain(ytile[:, qq * 512:(qq + 1) * 512], 512,
                          [(3, -6, xqtile[:, qq * 512:(qq + 1) * 512])])
        wait("gpsimd", s_v, vtg)
        em("gpsimd", lambda g, lt=lt: g.dma_start(
            RAP(xqd, lt * 128 * 128, [[128, 128], [L * 128, DT], [1, 128]]),
            xqtile[:, 0:D].rearrange("p (b c) -> p b c", c=128)
        ).then_inc(s_gw.h, 16))
        s_gw.inc(16)
        wait("sync", s_gw, s_gw.v)
        wait("vector", s_gw, s_gw.v)
        wait("scalar", s_gw, s_gw.v)
        if "x1" in dbg:
            em("sync", lambda sy, lt=lt: sy.dma_start(
                RAP(dbg["x1"], lt * 128 * D, [[D, 128], [1, D]]), ytile[:, 0:D]
            ).then_inc(s_st.h, 16))
            s_st.inc(16)
            wait("sync", s_st, s_st.v)
    barrier()
    for ti in range(DT):
        em("sync", lambda sy, ti=ti: sy.dma_start_transpose(
            xqTb[ti][:], RAP(xqd, ti * L * 128, [[128, L], [1, 128]])
        ).then_inc(s_tp.h, 16))
        s_tp.inc(16)
    wait("sync", s_tp, s_tp.v)
    barrier()
    if phases < 7:
        return _finish(nc, plan, dbg, (es, esC))

    for lc in range(NIC):
        for ht in range(HT):
            em("sync", lambda sy, ht=ht: sy.dma_start(
                w1qt[ht % 2][:, :], RAP(w1qd, ht * DT * 16384,
                                        [[128, 128], [16384, DT], [1, 128]])
            ).then_inc(s_tp.h, 16))
            s_tp.inc(16)
            wait("tensor", s_tp, s_tp.v)
            items = [(lambda: paux[:, 0:512],
                      lambda ht=ht, ti=ti: w1qt[ht % 2][:, ti * 128:(ti + 1) * 128],
                      lambda ti=ti, lc=lc: xqTb[ti][:, lc * 512:(lc + 1) * 512],
                      ti == 0, ti == DT - 1) for ti in range(DT)]
            mm_t = mm_group(items)
            wait("sync", s_mm, mm_t)
            wait("vector", s_mm, mm_t)
            tgt = s_v.inc()
            em("vector", lambda v, ht=ht, tg=tgt: v.tensor_scalar(
                qrt[0][:, 0:512], paux[:, 0:512], b1qb[:, ht:ht + 1], 0.0,
                ALU.add, ALU.max).then_inc(s_v.h, 1))
            wait("tensor", s_v, tgt)
            wait("scalar", s_v, tgt)
            vtg = q_chain(qrt[0][:, 0:512], 512,
                          [(10, -14, qrt[1][:, 0:512]),
                           (3, -6, hqTb[ht][:, 0:512])])
            if "hq0" in dbg and lc == 0 and ht == 0:
                wait("sync", s_v, vtg)
                em("sync", lambda sy: sy.dma_start(
                    dbg["hq0"].ap(), hqTb[0][:, 0:512]).then_inc(s_st.h, 16))
                s_st.inc(16)
        wait("tensor", s_v, s_v.v)
        for dc in range(2):
            for ht in range(HT):
                em("sync", lambda sy, ht=ht, dc=dc: sy.dma_start(
                    w2qs[ht % 2][:, :], RAP(w2qd, ht * 128 * D + dc * 512,
                                            [[D, 128], [1, 512]])
                ).then_inc(s_tp.h, 16))
                s_tp.inc(16)
                wait("tensor", s_tp, s_tp.v)
                for lt2 in range(4):
                    mm_group([(lambda lt2=lt2: pctx[lt2][:, 0:512],
                               lambda ht=ht, lt2=lt2:
                               hqTb[ht][:, lt2 * 128:(lt2 + 1) * 128],
                               lambda ht=ht: w2qs[ht % 2][:, :],
                               ht == 0, ht == HT - 1)],
                              inc=(lt2 == 3))
                wait("sync", s_mm, s_mm.v)
            mmB = s_mm.v
            wait("vector", s_mm, mmB)
            for lt2 in range(4):
                tgt = s_v.inc()
                em("vector", lambda v, lt2=lt2, dc=dc, tg=tgt: v.tensor_tensor(
                    ysb[lt2][:, dc * 512:(dc + 1) * 512], pctx[lt2][:, 0:512],
                    b2qb[:, dc * 512:(dc + 1) * 512], ALU.add).then_inc(s_v.h, 1))
            wait("tensor", s_v, s_v.v)
        wait("scalar", s_v, s_v.v)
        wait("vector", s_v, s_v.v)
        for lt2 in range(4):
            lt = lc * 4 + lt2
            em("sync", lambda sy, lt=lt: sy.dma_start(
                maskt[lt % 2][:, :],
                RAP(maskin, lt * 128 * D, [[D, 128], [1, D]])
            ).then_inc(s_ld.h, 16))
            s_ld.inc(16)
            em("sync", lambda sy, lt=lt: sy.dma_start(
                strm[lt % 2][:, :], RAP(x1d, lt * 128 * D, [[D, 128], [1, D]])
            ).then_inc(s_ld.h, 16))
            s_ld.inc(16)
            ld_t = s_ld.v
            for qq in range(2):
                vtg = q_chain(ysb[lt2][:, qq * 512:(qq + 1) * 512], 512,
                              [(10, -14, ytile[:, qq * 512:(qq + 1) * 512])])
            wait("vector", s_ld, ld_t)
            em("vector", lambda v, lt=lt: v.scalar_tensor_tensor(
                ytile[:, 0:D], ytile[:, 0:D], consts[:, 6:7], maskt[lt % 2][:, :],
                ALU.mult, ALU.mult))
            tgt = s_v.inc()
            em("vector", lambda v, lt=lt, tg=tgt: v.tensor_tensor(
                lnt[:, 0:D], ytile[:, 0:D], strm[lt % 2][:, :], ALU.add
            ).then_inc(s_v.h, 1))
            wait("scalar", s_v, tgt)
            ln_t = ln_tile(lnt[:, 0:D], ytile[:, 0:D])
            wait("sync", s_v, ln_t)
            em("sync", lambda sy, lt=lt: sy.dma_start(
                RAP(out, lt * 128 * D, [[D, 128], [1, D]]), ytile[:, 0:D]
            ).then_inc(s_st.h, 16))
            s_st.inc(16)
            wait("vector", s_st, s_st.v)
            wait("scalar", s_st, s_st.v)

    st_all = s_st.v
    for eng in ENGS:
        wait(eng, s_st, st_all)
    return _finish(nc, plan, dbg, (es, esC))


def _finish(nc, plan, dbg, stacks):
    with nc.Block() as block:
        def run_for(eng_name):
            def body(eng):
                for e, fn in plan:
                    if e == eng_name:
                        fn(eng)
            return body
        block.sync(run_for("sync"))
        block.gpsimd(run_for("gpsimd"))
        block.tensor(run_for("tensor"))
        block.vector(run_for("vector"))
        block.scalar(run_for("scalar"))
    for st in stacks:
        try:
            st.close()
        except Exception:
            pass
    return nc, dbg


def _host_constants():
    t = np.arange(L, dtype=np.float64)
    inv_freq = 1.0 / (10000.0 ** (np.arange(0, 64, 2, dtype=np.float64) / 64.0))
    ang = np.outer(t, inv_freq)
    cos = np.cos(ang)
    sin = np.sin(ang)
    p = np.arange(128)
    m = (p % 64) // 2
    c2 = cos[:, m].T.astype(np.float32)
    s2 = sin[:, m].T.astype(np.float32)
    s2 = np.where((p % 2 == 0)[:, None], -s2, s2).astype(np.float32)
    ident = np.eye(128, dtype=ml_dtypes.bfloat16)
    return np.ascontiguousarray(c2), np.ascontiguousarray(s2), ident


def _threefry2x32(k0, k1, x0, x1):
    k0 = np.uint32(k0); k1 = np.uint32(k1)
    k2 = np.uint32(k0 ^ k1 ^ np.uint32(0x1BD11BDA))
    ks = [k0, k1, k2]
    R = [[13, 15, 26, 6], [17, 29, 16, 24]]
    x0 = (x0 + ks[0]).astype(np.uint32)
    x1 = (x1 + ks[1]).astype(np.uint32)
    for i in range(5):
        for r in (R[0] if i % 2 == 0 else R[1]):
            x0 = (x0 + x1).astype(np.uint32)
            x1 = ((x1 << np.uint32(r)) | (x1 >> np.uint32(32 - r))).astype(np.uint32)
            x1 = (x0 ^ x1).astype(np.uint32)
        x0 = (x0 + ks[(i + 1) % 3]).astype(np.uint32)
        x1 = (x1 + ks[(i + 2) % 3] + np.uint32(i + 1)).astype(np.uint32)
    return x0, x1


def _dropout_mask():
    n = 8 * L * D
    iota = np.arange(n, dtype=np.uint64)
    c1 = (iota >> np.uint64(32)).astype(np.uint32)
    c2 = (iota & np.uint64(0xFFFFFFFF)).astype(np.uint32)
    b1, b2 = _threefry2x32(0, 42, c1, c2)
    if os.environ.get("BASS_SIM_FP64") == "1":
        bits = (b1.astype(np.uint64) << np.uint64(32)) | b2.astype(np.uint64)
        u = ((bits >> np.uint64(12)) | np.uint64(0x3FF0000000000000)
             ).view(np.float64) - 1.0
        keep = u < 0.8
    else:
        bits = (b1 ^ b2).astype(np.uint32)
        u = ((bits >> np.uint32(9)) | np.uint32(0x3F800000)
             ).view(np.float32) - np.float32(1.0)
        keep = u < np.float32(0.8)
    return keep.reshape(8, L, D).astype(ml_dtypes.bfloat16)


_CACHE = {}


def kernel(L0_attn_x, L0_attn_w_q, L0_attn_w_k, L0_attn_w_v, L0_attn_w_o,
           L0_attn_rel_emb, L0_ffn_w1, L0_ffn_b1, L0_ffn_w2, L0_ffn_b2):
    if "nc" not in _CACHE:
        _CACHE["nc"], _ = build_core_program()
    nc = _CACHE["nc"]

    c2, s2, ident = _host_constants()
    maskf = _dropout_mask()

    x = np.ascontiguousarray(np.asarray(L0_attn_x, dtype=np.float32))
    wq = np.ascontiguousarray(np.asarray(L0_attn_w_q, dtype=np.float32).T)
    wk = np.ascontiguousarray(np.asarray(L0_attn_w_k, dtype=np.float32).T)
    wv = np.ascontiguousarray(np.asarray(L0_attn_w_v, dtype=np.float32).T)
    wo = np.ascontiguousarray(np.asarray(L0_attn_w_o, dtype=np.float32).T)
    w1 = np.ascontiguousarray(np.asarray(L0_ffn_w1, dtype=np.float32).T)
    w2 = np.ascontiguousarray(np.asarray(L0_ffn_w2, dtype=np.float32).T)
    b1 = np.ascontiguousarray(np.asarray(L0_ffn_b1, dtype=np.float32)[None, :])
    b2 = np.ascontiguousarray(np.asarray(L0_ffn_b2, dtype=np.float32)[None, :])
    rel = np.asarray(L0_attn_rel_emb, dtype=np.float32)
    idx = [32 - w for w in range(32)]
    rel_Tw = np.ascontiguousarray(
        np.concatenate([rel[idx, :64].T, rel[0:1, :64].T], axis=1))
    rel_wrev = np.ascontiguousarray(
        np.concatenate([rel[idx, :64], rel[0:1, :64]], axis=0))

    in_maps = []
    for b in range(8):
        xb = np.ascontiguousarray(x[b])
        in_maps.append({
            "x_nat": xb,
            "x_T": np.ascontiguousarray(xb.T),
            "wq_T": wq, "wk_T": wk, "wv_T": wv, "wo_T": wo,
            "w1_T": w1, "w2_T": w2, "b1": b1, "b2": b2,
            "rel_Tw": rel_Tw, "rel_wrev": rel_wrev,
            "rel0r": np.ascontiguousarray(rel[0:1, :64]),
            "maskin": np.ascontiguousarray(maskf[b]),
            "c2": c2, "s2": s2, "ident": ident,
        })
    res = run_bass_kernel_spmd(nc, in_maps, core_ids=list(range(8)))
    return np.stack([r["out"].reshape(L, D) for r in res.results], axis=0)


# version 04
# speedup vs baseline: 2.1478x; 1.2116x over previous
import os
import math
import contextlib
import numpy as np
import ml_dtypes
import concourse.bass as bass
import concourse.mybir as mybir
from bass_rust import AP as RAP
from concourse.bass_utils import run_bass_kernel_spmd

F32 = mybir.dt.float32
BF16 = mybir.dt.bfloat16
I32 = mybir.dt.int32
_EXACT_LADDER = (os.environ.get("BASS_SIM_FP64") == "1"
                 or os.environ.get("BASS_DEV_LADDER") == "1")
AF = mybir.ActivationFunctionType
ALU = mybir.AluOpType
AX = mybir.AxisListType

L = 2048
D = 1024
NH = 16
HID = 4096
LT = L // 128
DT = D // 128
HT = HID // 128
NIC = L // 512

P6 = L + 64
P7 = 320
SBD = 320
P5 = L + 64
OFFB = 127

LN2 = float(np.log(np.float64(2.0)))
C2LN2 = float(1.0 / (2.0 * np.log(np.float64(2.0))))
NEG1E18 = -1.0e18
EPS = 1e-5
ENGS = ("sync", "gpsimd", "tensor", "vector", "scalar")


class Sem:
    def __init__(self, h):
        self.h = h
        self.v = 0

    def inc(self, n=1):
        self.v += n
        return self.v


def build_core_program(debug=(), nheads=NH, phases=99, skip_ctxrel=False):
    nc = bass.Bass("TRN2", target_bir_lowering=False, detect_race_conditions=False)

    din = lambda name, shape, dt: nc.dram_tensor(name, shape, dt, kind="ExternalInput")
    x_nat = din("x_nat", [L, D], F32)
    x_T = din("x_T", [D, L], F32)
    wq_T = din("wq_T", [D, D], F32)
    wk_T = din("wk_T", [D, D], F32)
    wv_T = din("wv_T", [D, D], F32)
    wo_T = din("wo_T", [D, D], F32)
    w1_T = din("w1_T", [D, HID], F32)
    w2_T = din("w2_T", [HID, D], F32)
    b1in = din("b1", [1, HID], F32)
    b2in = din("b2", [1, D], F32)
    rel_Tw = din("rel_Tw", [64, 33], F32)
    rel_wrev = din("rel_wrev", [33, 64], F32)
    rel0r_in = din("rel0r", [1, 64], F32)
    maskin = din("maskin", [L, D], BF16)
    c2in = din("c2", [128, L], F32)
    s2in = din("s2", [128, L], F32)
    identin = din("ident", [128, 128], BF16)

    out = nc.dram_tensor("out", [L, D], F32, kind="ExternalOutput")
    dbg = {}
    def ddbg(name, shape, dt):
        if name in debug:
            dbg[name] = nc.dram_tensor("dbg_" + name, shape, dt, kind="ExternalOutput")
    ddbg("qT", [D, L], BF16)
    ddbg("kT", [D, L], BF16)
    ddbg("vh", [L, NH * 66], BF16)
    ddbg("negu", [2, NH], F32)
    ddbg("nrm", [2, 16], F32)
    ddbg("um", [2, 48], F32)
    ddbg("unew", [2, 80], F32)
    ddbg("ctxT", [D, L], BF16)
    ddbg("sg", [1, L], F32)
    ddbg("a0", [32, L], BF16)
    ddbg("x1", [L, D], F32)
    ddbg("hq0", [128, 512], BF16)
    ddbg("qd", [4, 512], F32)

    dram = lambda name, n, dt: nc.dram_tensor(name, [n], dt, kind="Internal")
    hbuf = dram("hbuf", 4 * 32 * P6, BF16)
    mbuf = dram("mbuf", 2 * 16 * 32 * 128, BF16)
    wbuf = dram("wbuf", NH * L * P7, BF16)
    dbuf = dram("dbuf", L * SBD, BF16)
    cbuf = dram("cbuf", L * 128, BF16)
    btbuf = dram("btbuf", 32 * P5, BF16)
    relbuf = dram("relbuf", 64 * 32, BF16)
    ubuf = dram("ubuf", 64, F32)
    sgbuf = dram("sgbuf", L, F32)
    ctxd = dram("ctxd", D * L, BF16)
    w1qd = dram("w1qd", D * HID, BF16)
    w2qd = dram("w2qd", HID * D, BF16)
    x1d = dram("x1d", L * D, F32)
    xqd = dram("xqd", L * D, BF16)

    es = contextlib.ExitStack()
    esA = contextlib.ExitStack()
    esB = contextlib.ExitStack()
    esB2 = contextlib.ExitStack()
    esC = contextlib.ExitStack()

    def mk(stk, side):
        def f(name, shape, dt):
            return stk.enter_context(nc.sbuf_tensor(name, shape, dt, side=side))
        return f
    sbG = mk(es, "left")
    sbA = mk(esA, "right")
    sbB = mk(esB, "left")
    sbB2 = mk(esB2, "right")
    sbC = mk(esC, "right")
    psum = lambda name: es.enter_context(nc.psum_tensor(name, [128, 512], F32))
    sem = lambda name: Sem(es.enter_context(nc.semaphore(name)))

    xqTb = [sbG(f"xqTb{i}", [128, L], BF16) for i in range(DT)]
    vhb = [sbG(f"vhb{i}", [128, NH * 66], BF16) for i in range(LT)]
    scr = {k: sbG("scr_" + k, [128, 512], F32) for k in ("a", "b", "c", "s", "u", "w")}
    scr["i"] = sbG("scr_i", [128, 512], I32)
    scr["j"] = sbG("scr_j", [128, 512], I32)
    qrt = [sbG(f"qrt{i}", [128, 512], F32) for i in range(3)]
    ctxe = sbG("ctxe", [64, 512], BF16)
    identb = sbG("identb", [128, 128], BF16)
    relblk = sbG("relblk", [128, 64], BF16)
    rl2 = sbG("rl2", [32, 64], BF16)
    rl2f = sbG("rl2f", [32, 64], F32)
    relrow = sbG("relrow", [1, D], BF16)
    onesrow = sbG("onesrow", [1, 128], BF16)
    onesf = sbG("onesf", [1, 128], F32)
    hhsel = sbG("hhsel", [128, 2], BF16)
    consts = sbG("consts", [128, 12], F32)
    nrm = sbG("nrm", [2, 16], F32)
    umath = sbG("umath", [2, 48], F32)
    bmcol = sbG("bmcol", [2, 2], F32)
    negu = sbG("negu", [128, NH], F32)
    unew = sbG("unew", [128, 80], F32)
    col = sbG("col", [128, 8], F32)
    b1qb = sbG("b1qb", [128, HT], F32)
    b1raw = sbG("b1raw", [128, HT], F32)
    gt = [sbG(f"gt{i}", [128, 32], BF16) for i in range(2)]
    wint = [sbG(f"wint{i}", [128, 160], BF16) for i in range(2)]
    ctxt = [sbG(f"ctxt{i}", [128, 128], BF16) for i in range(2)]
    zt = sbG("zt", [128, P7], BF16)

    c2b = sbA("c2b", [128, L], F32)
    s2b = sbA("s2b", [128, L], F32)
    stageA = sbA("stageA", [128, D], F32)
    wbfA = [sbA(f"wbfA{i}", [128, D], BF16) for i in range(DT)]
    relw = sbA("relw", [64, 33], F32)
    relv = sbA("relv", [33, 64], F32)
    rel0b = sbA("rel0b", [128, D], F32)
    rel0r = sbA("rel0r_t", [1, 64], F32)

    qTb = [sbB(f"qTb{i}", [128, L], BF16) for i in range(DT)]
    kTb = [sbB(f"kTb{i}", [128, L], BF16) for i in range(DT)]
    attnb = [sbB(f"attnb{i}", [128, L], BF16) for i in range(2)]


    pqk = [psum("pqk0"), psum("pqk1")]
    pctx = [psum(f"pctx{i}") for i in range(4)]
    paux = psum("paux")
    paux2 = psum("paux2")

    s_ld = sem("s_ld")
    s_gw = sem("s_gw")
    s_tp = sem("s_tp")
    s_mm = sem("s_mm")
    s_v = sem("s_v")
    s_a = sem("s_a")
    s_st = sem("s_st")
    s_zz = sem("s_zz")
    s_bar = sem("s_bar")

    plan = []
    em = lambda eng, fn: plan.append((eng, fn))
    def wait(eng, s, val):
        em(eng, lambda e, ss=s.h, vv=val: e.wait_ge(ss, vv))

    nbar = [0]
    def barrier():
        nbar[0] += 1
        tgt = 5 * nbar[0]
        for eng in ENGS:
            em(eng, lambda e: e.drain().then_inc(s_bar.h, 1))
            wait(eng, s_bar, tgt)

    def vsafe(fn):
        tgt = s_v.inc()
        em("vector", lambda v, f=fn: f(v).then_inc(s_v.h, 1))
        em("vector", lambda v, t=tgt: v.wait_ge(s_v.h, t))
        return tgt

    def asafe(fn):
        tgt = s_a.inc()
        em("scalar", lambda sc, f=fn: f(sc).then_inc(s_a.h, 1))
        em("scalar", lambda sc, t=tgt: sc.wait_ge(s_a.h, t))
        return tgt

    def vsync():
        tgt = s_v.inc()
        em("vector", lambda v, tg=tgt: v.drain().then_inc(s_v.h, 1))
        return tgt

    def asynC():
        tgt = s_a.inc()
        em("scalar", lambda s, tg=tgt: s.drain().then_inc(s_a.h, 1))
        return tgt

    def mm_group(items, inc=True):
        n = len(items)
        for idx, (o, l, r, st, sp) in enumerate(items):
            if idx == n - 1 and inc:
                tgt = s_mm.inc()
                em("tensor", lambda t, o=o, l=l, r=r, st=st, sp=sp, tg=tgt:
                   t.matmul(o(), l(), r(), start=st, stop=sp).then_inc(s_mm.h, 1))
            else:
                em("tensor", lambda t, o=o, l=l, r=r, st=st, sp=sp:
                   t.matmul(o(), l(), r(), start=st, stop=sp))
        return s_mm.v

    LNG = consts[:, 0:1]
    BIAS_P = {3: consts[:, 1:2], 10: consts[:, 3:4]}
    BIAS_N = {3: consts[:, 2:3], 10: consts[:, 4:5]}

    qd_done = []

    def q_chain_fast(x_ap, W, stages, P=128):
        E = vsafe if W < 64 else (lambda fn: em("vector", fn))
        sa = lambda n: scr[n][:P, :W]
        src = x_ap
        for st_i, (man, min_e, dst) in enumerate(stages):
            E(lambda v: v.tensor_scalar(sa("i"), x_ap.bitcast(I32), 23, 255,
                                        ALU.logical_shift_right, ALU.bitwise_and))
            E(lambda v, me=min_e: v.tensor_scalar(
                sa("c"), sa("i"), -127, int(me), ALU.add, ALU.max))
            E(lambda v, man=man: v.tensor_scalar(
                sa("j"), sa("c"), -(1 << 23), (127 + man) << 23,
                ALU.mult, ALU.add))
            E(lambda v, src=src: v.tensor_tensor(
                sa("u"), src, scr["j"][:P, :W].bitcast(F32), ALU.mult))
            E(lambda v, man=man: v.tensor_scalar(
                sa("j"), sa("c"), 1 << 23, (127 - man) << 23,
                ALU.mult, ALU.add))
            E(lambda v: v.tensor_copy(sa("i"), sa("u")))
            if st_i == len(stages) - 1 and W >= 64:
                tgt = s_v.inc()
                em("vector", lambda v, d=dst, tg=tgt: v.tensor_tensor(
                    d, sa("i"), scr["j"][:P, :W].bitcast(F32), ALU.mult
                ).then_inc(s_v.h, 1))
            else:
                E(lambda v, d=dst: v.tensor_tensor(
                    d, sa("i"), scr["j"][:P, :W].bitcast(F32), ALU.mult))
            src = dst
        return s_v.v

    def q_chain(x_ap, W, stages, P=128):
        if not _EXACT_LADDER:
            return q_chain_fast(x_ap, W, stages, P)
        def V(fn):
            return vsafe(fn)
        def A(fn):
            return asafe(fn)
        sa = lambda n: scr[n][:P, :W]

        def rstrict(src_ap, r_dst, tie_even):
            V(lambda v: v.tensor_copy(sa("i"), src_ap))
            V(lambda v: v.tensor_tensor(sa("b"), src_ap, sa("i"), ALU.subtract))
            V(lambda v: v.tensor_scalar(sa("w"), sa("b"), 0.5, None, ALU.is_gt))
            V(lambda v: v.tensor_tensor(sa("w"), sa("i"), sa("w"), ALU.add))
            V(lambda v: v.tensor_scalar(sa("c"), sa("b"), -0.5, None, ALU.is_lt))
            if not tie_even:
                V(lambda v, r=r_dst: v.tensor_tensor(r, sa("w"), sa("c"),
                                                     ALU.subtract))
                return
            V(lambda v: v.tensor_tensor(sa("w"), sa("w"), sa("c"), ALU.subtract))
            V(lambda v: v.tensor_tensor(sa("c"), sa("b"), sa("b"), ALU.mult))
            V(lambda v: v.tensor_scalar(sa("c"), sa("c"), 0.25, None, ALU.is_equal))
            V(lambda v: v.tensor_scalar(sa("i"), sa("i"), 1, None, ALU.bitwise_and))
            V(lambda v: v.tensor_tensor(sa("c"), sa("c"), sa("i"), ALU.mult))
            V(lambda v: v.tensor_tensor(sa("c"), sa("c"), sa("b"), ALU.mult))
            V(lambda v: v.tensor_scalar(sa("c"), sa("c"), 2.0, None, ALU.mult))
            V(lambda v, r=r_dst: v.tensor_tensor(r, sa("w"), sa("c"), ALU.add))

        wait("scalar", s_v, s_v.v)
        A(lambda sc, x=x_ap: sc.activation(sa("a"), x, AF.Square))
        A(lambda sc: sc.activation(sa("b"), sa("a"), AF.Ln, bias=LNG[:P, :]))
        wait("vector", s_a, asynC())
        V(lambda v: v.tensor_scalar(sa("u"), sa("b"), C2LN2, -0.5,
                                    ALU.mult, ALU.add))
        rstrict(sa("u"), sa("a"), False)
        if "qd" in dbg and not qd_done:
            qd_done.append(1)
            V(lambda v: v.tensor_copy(scr["i"][0:1, 0:512], scr["u"][0:1, 0:512]))
            V(lambda v: v.tensor_scalar(scr["w"][0:1, 0:512], scr["i"][0:1, 0:512],
                                        0.0, None, ALU.add))
            wait("sync", s_v, vsync())
            for row, tt in ((0, "u"), (1, "w"), (2, "a"), (3, "b")):
                em("sync", lambda sy, row=row, tt=tt: sy.dma_start(
                    RAP(dbg["qd"], row * 512, [[512, 1], [1, 512]]),
                    scr[tt][0:1, 0:512]).then_inc(s_st.h, 16))
                s_st.inc(16)
            wait("vector", s_st, s_st.v)
        src = x_ap
        for (man, min_e, dst) in stages:
            V(lambda v, me=min_e: v.tensor_scalar(
                sa("c"), sa("a"), float(me), None, ALU.max))
            V(lambda v: v.tensor_copy(sa("i"), sa("c")))
            V(lambda v, man=man: v.tensor_scalar(
                sa("j"), sa("i"), -1, 127 + man, ALU.mult, ALU.add))
            V(lambda v: v.tensor_scalar(sa("j"), sa("j"), 23, None,
                                        ALU.logical_shift_left))
            V(lambda v, src=src: v.tensor_tensor(
                sa("u"), src, scr["j"][:P, :W].bitcast(F32), ALU.mult))
            V(lambda v, man=man: v.tensor_scalar(
                sa("j"), sa("i"), 1, 127 - man, ALU.mult, ALU.add))
            V(lambda v: v.tensor_scalar(sa("j"), sa("j"), 23, None,
                                        ALU.logical_shift_left))
            V(lambda v: v.tensor_copy(sa("s"), scr["j"][:P, :W].bitcast(F32)))
            rstrict(sa("u"), sa("u"), True)
            V(lambda v, d=dst: v.tensor_tensor(d, sa("u"), sa("s"), ALU.mult))
            src = dst
        return vsync()

    for c, val in ((0, 1e-38), (1, 3.0 * LN2), (2, -3.0 * LN2), (3, 10.0 * LN2),
                   (4, -10.0 * LN2), (5, 0.8), (7, EPS), (8, 1.0 / float(D))):
        vsafe(lambda v, c=c, val=val: v.memset(consts[:, c:c + 1], val))
    vsafe(lambda v: v.reciprocal(consts[:, 6:7], consts[:, 5:6]))
    em("vector", lambda v: v.memset(zt[:], 0.0))
    em("vector", lambda v: v.memset(onesrow[:], 1.0))
    em("vector", lambda v: v.memset(onesf[:], 1.0))
    em("vector", lambda v: v.memset(hhsel[:, :], 0.0))
    em("vector", lambda v: v.memset(hhsel[0:64, 0:1], 1.0))
    em("vector", lambda v: v.memset(hhsel[64:128, 1:2], 1.0))
    em("vector", lambda v: v.memset(relblk[:, :], 0.0))
    em("vector", lambda v: v.memset(umath[:, :], 0.0))
    em("vector", lambda v: v.memset(nrm[:, :], 0.0))
    wait("gpsimd", s_v, vsync())

    def zfill(buf, span):
        pos = 0
        while pos < span:
            n = min(128 * P7, span - pos)
            full = n // P7
            if full > 0:
                em("gpsimd", lambda g, b=buf, p=pos, f=full: g.dma_start(
                    RAP(b, p, [[P7, f], [1, P7]]),
                    zt[0:f, :]).then_inc(s_zz.h, 16))
                s_zz.inc(16)
                pos += full * P7
            else:
                em("gpsimd", lambda g, b=buf, p=pos, n=n: g.dma_start(
                    RAP(b, p, [[1, 1], [1, n]]),
                    zt[0:1, 0:n]).then_inc(s_zz.h, 16))
                s_zz.inc(16)
                pos += n

    for buf, span in ((hbuf, 4 * 32 * P6), (wbuf, NH * L * P7), (dbuf, L * SBD),
                      (cbuf, L * 128), (btbuf, 32 * P5)):
        zfill(buf, span)
    zz_all = s_zz.v

    wait("vector", s_zz, zz_all)
    em("vector", lambda v: v.memset(zt[0:128, 0:OFFB], NEG1E18))
    wait("gpsimd", s_v, vsync())
    for h16 in range(NH):
        for c in range(LT):
            em("gpsimd", lambda g, h16=h16, c=c: g.dma_start(
                RAP(wbuf, h16 * L * P7 + c * 128 * P7, [[P7, 128], [1, OFFB]]),
                zt[0:128, 0:OFFB]).then_inc(s_gw.h, 16))
            s_gw.inc(16)

    for tile, src in ((c2b, c2in), (s2b, s2in), (identb, identin),
                      (relw, rel_Tw), (relv, rel_wrev), (rel0r, rel0r_in)):
        em("sync", lambda sy, t=tile, s=src: sy.dma_start(
            t[:], s.ap()).then_inc(s_ld.h, 16))
        s_ld.inc(16)
    ld_tab = s_ld.v

    wait("vector", s_ld, ld_tab)
    vsafe(lambda v: v.tensor_scalar(
        qrt[0][0:64, 0:32], relw[:, 0:32], relw[:, 32:33], None, ALU.subtract))
    em("vector", lambda v: v.tensor_copy(ctxe[0:64, 0:32], qrt[0][0:64, 0:32]))
    wait("gpsimd", s_v, vsync())
    em("gpsimd", lambda g: g.dma_start(
        RAP(relbuf, 0, [[32, 64], [1, 32]]), ctxe[0:64, 0:32]
    ).then_inc(s_gw.h, 16))
    s_gw.inc(16)
    wait("sync", s_gw, s_gw.v)
    em("sync", lambda sy: sy.dma_start(
        relblk[0:64, 0:32], RAP(relbuf, 0, [[32, 64], [1, 32]])
    ).then_inc(s_tp.h, 16))
    s_tp.inc(16)
    em("sync", lambda sy: sy.dma_start(
        relblk[64:128, 32:64], RAP(relbuf, 0, [[32, 64], [1, 32]])
    ).then_inc(s_tp.h, 16))
    s_tp.inc(16)

    wait("tensor", s_ld, ld_tab)
    wait("tensor", s_v, s_v.v)
    mm_t = mm_group([(lambda: paux[0:32, 0:64], lambda: onesf[0:1, 0:32],
                      lambda: rel0r[0:1, 0:64], True, True)])
    wait("vector", s_mm, mm_t)
    vsafe(lambda v: v.tensor_tensor(rl2f[:, :], relv[0:32, 0:64],
                                    paux[0:32, 0:64], ALU.subtract))
    em("vector", lambda v: v.tensor_copy(rl2[:, :], rl2f[:, :]))
    vsafe(lambda v: v.tensor_copy(relrow[0:1, 0:64], rel0r[0:1, 0:64]))
    for hh in range(1, NH):
        em("vector", lambda v, hh=hh: v.tensor_copy(
            relrow[0:1, hh * 64:(hh + 1) * 64], relrow[0:1, 0:64]))
    wait("tensor", s_v, vsync())
    for dc in range(2):
        mm_t = mm_group([(lambda: paux2[0:128, 0:512],
                          lambda: onesrow[0:1, 0:128],
                          lambda dc=dc: relrow[0:1, dc * 512:(dc + 1) * 512],
                          True, True)])
        wait("vector", s_mm, mm_t)
        vt = s_v.inc()
        em("vector", lambda v, dc=dc, tg=vt: v.tensor_copy(
            rel0b[:, dc * 512:(dc + 1) * 512], paux2[0:128, 0:512]
        ).then_inc(s_v.h, 1))
        wait("tensor", s_v, vt)
    barrier()

    xTb = xqTb
    for i in range(DT):
        for hf in range(2):
            em("sync", lambda sy, i=i, hf=hf: sy.dma_start(
                stageA[:], RAP(x_T, i * 128 * L + hf * 1024, [[L, 128], [1, 1024]])
            ).then_inc(s_ld.h, 16))
            s_ld.inc(16)
            wait("scalar", s_ld, s_ld.v)
            tgt = s_a.inc()
            em("scalar", lambda s, i=i, hf=hf, tg=tgt: s.activation(
                xTb[i][:, hf * 1024:(hf + 1) * 1024], stageA[:],
                AF.Copy).then_inc(s_a.h, 1))
            wait("sync", s_a, tgt)
    barrier()

    def load_weight_bf16(src, scale, wbf, stg):
        for i in range(DT):
            em("sync", lambda sy, i=i, s=src, stg=stg: sy.dma_start(
                stg[:, 0:D], RAP(s, i * 128 * D, [[D, 128], [1, D]])
            ).then_inc(s_ld.h, 16))
            s_ld.inc(16)
            wait("scalar", s_ld, s_ld.v)
            tgt = s_a.inc()
            em("scalar", lambda s, i=i, sc=scale, wbf=wbf, stg=stg, tg=tgt:
               s.activation(wbf[i][:, :], stg[:, 0:D], AF.Copy, scale=sc
                            ).then_inc(s_a.h, 1))
            wait("sync", s_a, tgt)

    def proj_qk(dstb):
        for dt_ in range(DT):
            for lc in range(NIC):
                pb = pqk[(dt_ * NIC + lc) % 2]
                items = [(lambda pb=pb: pb[:, 0:512],
                          lambda ti=ti, dt_=dt_: wbfA[ti][:, dt_ * 128:(dt_ + 1) * 128],
                          lambda ti=ti, lc=lc: xTb[ti][:, lc * 512:(lc + 1) * 512],
                          ti == 0, ti == DT - 1) for ti in range(DT)]
                mm_t = mm_group(items)
                wait("vector", s_mm, mm_t)
                sl = slice(lc * 512, (lc + 1) * 512)
                em("vector", lambda v, pb=pb: v.tensor_copy(
                    qrt[0][:, 0:512], pb[:, 0:512]))
                em("vector", lambda v: v.stream_shuffle(
                    qrt[1][:, 0:512], qrt[0][:, 0:512], [i ^ 1 for i in range(32)]))
                em("vector", lambda v, sl=sl: v.tensor_tensor(
                    qrt[2][:, 0:512], qrt[0][:, 0:512], c2b[:, sl], ALU.mult))
                em("vector", lambda v, sl=sl: v.tensor_tensor(
                    qrt[1][:, 0:512], qrt[1][:, 0:512], s2b[:, sl], ALU.mult))
                tgt = s_v.inc()
                em("vector", lambda v, dt_=dt_, sl=sl, dstb=dstb, tg=tgt:
                   v.tensor_tensor(dstb[dt_][:, sl], qrt[2][:, 0:512],
                                   qrt[1][:, 0:512], ALU.add).then_inc(s_v.h, 1))
                wait("tensor", s_v, tgt)

    load_weight_bf16(wq_T, 0.125, wbfA, stageA)
    barrier()
    proj_qk(qTb)
    barrier()
    load_weight_bf16(wk_T, 1.0, wbfA, stageA)
    barrier()
    proj_qk(kTb)
    barrier()
    load_weight_bf16(wv_T, 1.0, wbfA, stageA)
    for lt in range(LT):
        for h in range(NH):
            em("vector", lambda v, lt=lt, h=h: v.memset(
                vhb[lt][:, h * 66 + 64:h * 66 + 65], 1.0))
            em("vector", lambda v, lt=lt, h=h: v.memset(
                vhb[lt][:, h * 66 + 65:h * 66 + 66], 0.0))
    vsync()
    barrier()
    for lt in range(LT):
        for dc in range(2):
            pb = pqk[(lt * 2 + dc) % 2]
            items = [(lambda pb=pb: pb[:, 0:512],
                      lambda ti=ti, lt=lt: xTb[ti][:, lt * 128:(lt + 1) * 128],
                      lambda ti=ti, dc=dc: wbfA[ti][:, dc * 512:(dc + 1) * 512],
                      ti == 0, ti == DT - 1) for ti in range(DT)]
            mm_t = mm_group(items)
            wait("vector", s_mm, mm_t)
            em("vector", lambda v, pb=pb, dc=dc: v.tensor_tensor(
                qrt[0][:, 0:512], pb[:, 0:512],
                rel0b[:, dc * 512:(dc + 1) * 512], ALU.add))
            for hl in range(8):
                h = dc * 8 + hl
                em("vector", lambda v, lt=lt, h=h, hl=hl: v.tensor_copy(
                    vhb[lt][:, h * 66:h * 66 + 64],
                    qrt[0][:, hl * 64:(hl + 1) * 64]))
            tgt = s_v.inc()
            em("vector", lambda v, tg=tgt: v.drain().then_inc(s_v.h, 1))
            wait("tensor", s_v, tgt)
    barrier()
    esA.close()

    if "qT" in dbg:
        for i in range(DT):
            em("sync", lambda sy, i=i: sy.dma_start(
                RAP(dbg["qT"], i * 128 * L, [[L, 128], [1, L]]), qTb[i][:]
            ).then_inc(s_st.h, 16))
            s_st.inc(16)
    if "kT" in dbg:
        for i in range(DT):
            em("sync", lambda sy, i=i: sy.dma_start(
                RAP(dbg["kT"], i * 128 * L, [[L, 128], [1, L]]), kTb[i][:]
            ).then_inc(s_st.h, 16))
            s_st.inc(16)
    if "vh" in dbg:
        for lt in range(LT):
            em("sync", lambda sy, lt=lt: sy.dma_start(
                RAP(dbg["vh"], lt * 128 * NH * 66, [[NH * 66, 128], [1, NH * 66]]),
                vhb[lt][:]).then_inc(s_st.h, 16))
            s_st.inc(16)
    if phases < 3:
        return _finish(nc, plan, dbg, (es, esB, esB2, esC))

    sqb = sbB2("sqb", [128, L], BF16)
    sgb = sbB2("sgb", [64, L], F32)
    a0w = sbB2("a0w", [32, L], BF16)
    bndt = sbB2("bndt", [128, L], BF16)
    hsb = sbB2("hsb", [64, L], BF16)
    stageQ = sbB2("stageQ", [128, L], F32)
    wqtile = sbB2("wqtile", [128, L], BF16)

    wprep_done = [0] * NH
    prep_state = {"copy_last": 0}
    mb_guard = [0, 0]
    gt_guard = [0, 0]
    hsb_guard = [0]

    def prep_pair(p):
        prep_copy_last = 0
        wait("scalar", s_gw, hsb_guard[0])
        for ic in range(NIC):
            mm_t = mm_group([(lambda: paux[0:64, 0:512],
                              lambda: relblk[:, 0:64],
                              lambda p=p, ic=ic:
                              qTb[p][:, ic * 512:(ic + 1) * 512],
                              True, True)])
            wait("scalar", s_mm, mm_t)
            tgt = s_a.inc()
            em("scalar", lambda sc, ic=ic: sc.activation(
                hsb[0:64, ic * 512:(ic + 1) * 512], paux[0:64, 0:512],
                AF.Copy).then_inc(s_a.h, 1))
            wait("tensor", s_a, tgt)
            prep_copy_last = tgt
        wait("gpsimd", s_a, prep_copy_last)
        for hh in range(2):
            s4 = (p % 2) * 2 + hh
            em("gpsimd", lambda g, hh=hh, s4=s4: g.dma_start(
                RAP(hbuf, s4 * 32 * P6, [[P6, 32], [1, L]]),
                hsb[hh * 32:(hh + 1) * 32, 0:L]).then_inc(s_gw.h, 16))
            s_gw.inc(16)
        hsb_guard[0] = s_gw.v
        for hh in range(2):
            h2 = 2 * p + hh
            s4 = (p % 2) * 2 + hh
            wait("gpsimd", s_tp, mb_guard[h2 % 2])
            em("gpsimd", lambda g, h2=h2, s4=s4: g.dma_start(
                RAP(mbuf, (h2 % 2) * 16 * 4096,
                    [[4096, 16], [128, 32], [1, 128]]),
                RAP(hbuf, s4 * 32 * P6, [[128, 16], [P6 + 1, 32], [1, 128]])
            ).then_inc(s_gw.h, 16))
            s_gw.inc(16)
            mh_t = s_gw.v
            wait("sync", s_gw, mh_t)
            for jt in range(LT):
                wait("sync", s_gw, gt_guard[jt % 2])
                em("sync", lambda sy, h2=h2, jt=jt: sy.dma_start_transpose(
                    gt[jt % 2][:], RAP(mbuf, (h2 % 2) * 16 * 4096 + jt * 4096,
                                       [[128, 32], [1, 128]])
                ).then_inc(s_tp.h, 16))
                s_tp.inc(16)
                wait("gpsimd", s_tp, s_tp.v)
                em("gpsimd", lambda g, h2=h2, jt=jt: g.dma_start(
                    RAP(wbuf, h2 * L * P7 + jt * 128 * P7 + OFFB,
                        [[P7, 128], [1, 32]]),
                    gt[jt % 2][:]).then_inc(s_gw.h, 16))
                s_gw.inc(16)
                gt_guard[jt % 2] = s_gw.v
            mb_guard[h2 % 2] = s_tp.v
            wprep_done[h2] = s_gw.v
        prep_state["copy_last"] = prep_copy_last

    prep_units = [(lambda p=p: prep_pair(p)) for p in range(NH // 2)]
    for _ in range(3):
        prep_units.pop(0)()

    wait("tensor", s_a, prep_state["copy_last"])
    em("scalar", lambda s: s.activation(qrt[0][0:32, 0:64], rl2f[:, :], AF.Square))
    wait("vector", s_a, asynC())
    em("vector", lambda v: v.reduce_sum(qrt[1][0:32, 0:1], qrt[0][0:32, 0:64], axis=AX.X))
    wait("gpsimd", s_v, vsync())
    em("gpsimd", lambda g: g.dma_start(RAP(ubuf, 0, [[1, 32], [1, 1]]),
                                       qrt[1][0:32, 0:1]).then_inc(s_gw.h, 16))
    s_gw.inc(16)
    wait("sync", s_gw, s_gw.v)
    em("sync", lambda sy: sy.dma_start(qrt[2][0:1, 0:32],
                                       RAP(ubuf, 0, [[32, 1], [1, 32]])
                                       ).then_inc(s_tp.h, 16))
    s_tp.inc(16)
    wait("vector", s_tp, s_tp.v)
    em("vector", lambda v: v.reduce_max(qrt[2][0:1, 40:41], qrt[2][0:1, 0:32], axis=AX.X))
    wait("gpsimd", s_v, vsync())
    em("gpsimd", lambda g: g.dma_start(RAP(ubuf, 32, [[2, 1], [1, 2]]),
                                       qrt[2][0:1, 40:42]).then_inc(s_gw.h, 16))
    s_gw.inc(16)

    for srcb, off in ((qTb, 0), (kTb, 8)):
        for th in range(DT):
            tgt = s_v.inc()
            em("vector", lambda v, th=th, srcb=srcb, tg=tgt: v.tensor_tensor(
                sqb[:, :], srcb[th][:], srcb[th][:], ALU.mult).then_inc(s_v.h, 1))
            wait("tensor", s_v, tgt)
            vsafe(lambda v, off=off, th=th: v.memset(
                nrm[0:2, off + th:off + th + 1], 0.0))
            for ic in range(NIC):
                mm_t = mm_group([(lambda: paux[0:2, 0:512],
                                  lambda: hhsel[:, 0:2],
                                  lambda ic=ic: sqb[:, ic * 512:(ic + 1) * 512],
                                  True, True)])
                wait("vector", s_mm, mm_t)
                vsafe(lambda v: v.reduce_max(umath[0:2, 0:1],
                                             paux[0:2, 0:512], axis=AX.X))
                tgt = vsafe(lambda v, off=off, th=th: v.tensor_tensor(
                    nrm[0:2, off + th:off + th + 1],
                    nrm[0:2, off + th:off + th + 1],
                    umath[0:2, 0:1], ALU.max))
                wait("tensor", s_v, tgt)

    wait("gpsimd", s_v, s_v.v)
    em("gpsimd", lambda g: g.dma_start(RAP(ubuf, 0, [[16, 2], [1, 16]]),
                                       nrm[0:2, 0:16]).then_inc(s_gw.h, 16))
    s_gw.inc(16)
    wait("sync", s_gw, s_gw.v)
    em("sync", lambda sy: sy.dma_start(
        unew[:, 0:33], RAP(ubuf, 0, [[0, 128], [1, 33]])).then_inc(s_tp.h, 16))
    s_tp.inc(16)
    wait("scalar", s_tp, s_tp.v)
    asafe(lambda sc: sc.activation(unew[:, 40:73], unew[:, 0:33], AF.Ln,
                                   bias=LNG[:, :]))
    asafe(lambda sc: sc.activation(unew[:, 0:33], unew[:, 40:73],
                                   AF.Exp, scale=0.5))
    wait("vector", s_a, asynC())
    for hh in range(2):
        vsafe(lambda v, hh=hh: v.tensor_scalar(
            unew[:, 40 + hh * 8:48 + hh * 8], unew[:, hh * 16 + 8:hh * 16 + 16],
            unew[:, 32:33], None, ALU.add))
        vsafe(lambda v, hh=hh: v.tensor_tensor(
            unew[:, 60 + hh * 8:68 + hh * 8], unew[:, hh * 16:hh * 16 + 8],
            unew[:, 40 + hh * 8:48 + hh * 8], ALU.mult))
        vsafe(lambda v, hh=hh: v.tensor_scalar(
            negu[:, hh * 8:(hh + 1) * 8], unew[:, 60 + hh * 8:68 + hh * 8],
            -1.0, -2.0, ALU.mult, ALU.add))
    vsync()
    if "negu" in dbg:
        wait("sync", s_v, s_v.v)
        em("sync", lambda sy: sy.dma_start(
            RAP(dbg["negu"], 0, [[16, 2], [1, 16]]), negu[0:2, 0:16]
        ).then_inc(s_st.h, 16))
        s_st.inc(16)
    if "nrm" in dbg:
        wait("sync", s_v, s_v.v)
        em("sync", lambda sy: sy.dma_start(
            RAP(dbg["nrm"], 0, [[16, 2], [1, 16]]), nrm[0:2, 0:16]
        ).then_inc(s_st.h, 16))
        s_st.inc(16)
    if "um" in dbg:
        wait("sync", s_v, s_v.v)
        em("sync", lambda sy: sy.dma_start(
            RAP(dbg["um"], 0, [[48, 2], [1, 48]]), umath[0:2, 0:48]
        ).then_inc(s_st.h, 16))
        s_st.inc(16)
    if "unew" in dbg:
        wait("sync", s_v, s_v.v)
        em("sync", lambda sy: sy.dma_start(
            RAP(dbg["unew"], 0, [[80, 2], [1, 80]]), unew[0:2, 0:80]
        ).then_inc(s_st.h, 16))
        s_st.inc(16)
    barrier()
    if phases < 4:
        return _finish(nc, plan, dbg, (es, esB, esB2, esC))

    for ht in range(HT):
        em("sync", lambda sy, ht=ht: sy.dma_start(
            b1raw[:, ht:ht + 1], RAP(b1in, ht * 128, [[1, 128], [1, 1]])
        ).then_inc(s_ld.h, 16))
        s_ld.inc(16)
    wait("scalar", s_ld, s_ld.v)
    wait("vector", s_ld, s_ld.v)
    q_chain(b1raw[:, 0:HT], HT, [(3, -6, b1qb[:, 0:HT])])
    em("sync", lambda sy: sy.dma_start(stageQ[0:1, 0:1024],
                                       RAP(b2in, 0, [[1, 1], [1, 1024]])
                                       ).then_inc(s_ld.h, 16))
    s_ld.inc(16)
    wait("scalar", s_ld, s_ld.v)
    wait("vector", s_ld, s_ld.v)
    q_chain(stageQ[0:1, 0:512], 512, [(3, -6, qrt[0][0:1, 0:512])], P=1)
    q_chain(stageQ[0:1, 512:1024], 512, [(3, -6, qrt[1][0:1, 0:512])], P=1)
    em("vector", lambda v: v.tensor_copy(relrow[0:1, 0:512], qrt[0][0:1, 0:512]))
    em("vector", lambda v: v.tensor_copy(relrow[0:1, 512:1024], qrt[1][0:1, 0:512]))
    vsync()

    def _w1_unit(ti, half):
        wait("sync", s_v, s_v.v)
        em("sync", lambda sy: sy.dma_start(
            stageQ[:, :], RAP(w1_T, ti * 128 * HID + half * 2048,
                              [[HID, 128], [1, 2048]])
        ).then_inc(s_ld.h, 16))
        s_ld.inc(16)
        wait("vector", s_ld, s_ld.v)
        for qq in range(4):
            vtg = q_chain(stageQ[:, qq * 512:(qq + 1) * 512], 512,
                          [(3, -6, wqtile[:, qq * 512:(qq + 1) * 512])])
        wait("gpsimd", s_v, vtg)
        em("gpsimd", lambda g: g.dma_start(
            RAP(w1qd, (half * 16) * (DT * 16384) + ti * 16384,
                [[128, 128], [DT * 16384, 16], [1, 128]]),
            wqtile[:, 0:2048].rearrange("p (b c) -> p b c", c=128)
        ).then_inc(s_gw.h, 16))
        s_gw.inc(16)
        wait("vector", s_gw, s_gw.v)

    def _w2_unit(ht):
        wait("sync", s_v, s_v.v)
        em("sync", lambda sy: sy.dma_start(
            stageQ[:, 0:D], RAP(w2_T, ht * 128 * D, [[D, 128], [1, D]])
        ).then_inc(s_ld.h, 16))
        s_ld.inc(16)
        wait("vector", s_ld, s_ld.v)
        for qq in range(2):
            vtg = q_chain(stageQ[:, qq * 512:(qq + 1) * 512], 512,
                          [(3, -6, wqtile[:, qq * 512:(qq + 1) * 512])])
        wait("gpsimd", s_v, vtg)
        em("gpsimd", lambda g: g.dma_start(
            RAP(w2qd, ht * 128 * D, [[D, 128], [1, D]]), wqtile[:, 0:D]
        ).then_inc(s_gw.h, 16))
        s_gw.inc(16)
        wait("vector", s_gw, s_gw.v)

    p4_units = ([(lambda ti=ti, half=half: _w1_unit(ti, half))
                 for ti in range(DT) for half in range(2)]
                + [(lambda ht=ht: _w2_unit(ht)) for ht in range(HT)])
    if phases < 5:
        for u in p4_units:
            u()
        p4_units = []
        barrier()
        return _finish(nc, plan, dbg, (es, esB, esB2, esC))

    pqk_guard = [0, 0]
    attnb_guard = [(0, 0), (0, 0)]
    wint_guard = [0, 0]
    pctx_guard = [0]
    sqb_guard = [0]
    n_units_per_head = (len(p4_units) + nheads - 1) // max(nheads, 1)
    for h in range(nheads):
        if h % 2 == 0 and prep_units:
            prep_units.pop(0)()
        for _ in range(n_units_per_head):
            if p4_units:
                p4_units.pop(0)()
        th, hh = h // 2, h % 2
        rows = slice(hh * 64, hh * 64 + 64)
        wait("sync", s_gw, wprep_done[h])

        for jt in range(LT):
            wwidth = min(160, L - jt * 128)
            wait("sync", s_mm, wint_guard[jt % 2])
            em("sync", lambda sy, h=h, jt=jt, ww=wwidth: sy.dma_start(
                wint[jt % 2][:, 0:ww],
                RAP(wbuf, h * L * P7 + jt * 128 * P7 + OFFB,
                    [[P7 - 1, 128], [1, ww]])
            ).then_inc(s_tp.h, 16))
            s_tp.inc(16)
            win_t = s_tp.v
            icmin = (jt * 128) // 512
            coff0 = jt * 128 - icmin * 512
            gmm, ggw = attnb_guard[jt % 2]
            wait("scalar", s_mm, gmm)
            wait("scalar", s_gw, ggw)
            last_exp = 0
            for ic in range(icmin, NIC):
                pb = pqk[ic % 2]
                coff = coff0 if ic == icmin else 0
                haswin = (ic == icmin) or (
                    ic == icmin + 1 and wwidth > 512 - coff0)
                items = [(lambda pb=pb, coff=coff: pb[:, coff:512],
                          lambda th=th, rows=rows, jt=jt:
                          kTb[th][rows, jt * 128:(jt + 1) * 128],
                          lambda th=th, rows=rows, ic=ic, coff=coff:
                          qTb[th][rows, ic * 512 + coff:(ic + 1) * 512],
                          True, not haswin)]
                if ic == icmin:
                    wait("tensor", s_tp, win_t)
                    w1w = min(wwidth, 512 - coff0)
                    items.append((lambda pb=pb, coff=coff, w1w=w1w:
                                  pb[:, coff:coff + w1w],
                                  lambda: identb[:, :],
                                  lambda jt=jt, w1w=w1w: wint[jt % 2][:, 0:w1w],
                                  False, True))
                elif haswin:
                    w1w = 512 - coff0
                    items.append((lambda pb=pb, w2w=wwidth - w1w: pb[:, 0:w2w],
                                  lambda: identb[:, :],
                                  lambda jt=jt, w1w=w1w, ww=wwidth:
                                  wint[jt % 2][:, w1w:ww],
                                  False, True))
                wait("tensor", s_a, pqk_guard[ic % 2])
                mm_t = mm_group(items)
                if haswin:
                    wint_guard[jt % 2] = mm_t
                wait("scalar", s_mm, mm_t)
                tgt = s_a.inc()
                em("scalar", lambda s, jt=jt, ic=ic, coff=coff, pb=pb, h=h, tg=tgt:
                   s.activation(attnb[jt % 2][:, ic * 512 + coff:(ic + 1) * 512],
                                pb[:, coff:512], AF.Exp,
                                bias=negu[:, (h % 2) * 8 + h // 2:
                                          (h % 2) * 8 + h // 2 + 1]).then_inc(s_a.h, 1))
                pqk_guard[ic % 2] = tgt
                last_exp = tgt
            wait("gpsimd", s_a, last_exp)
            em("gpsimd", lambda g, jt=jt, ww=wwidth: g.dma_start(
                RAP(dbuf, jt * 128 * SBD + OFFB, [[SBD - 1, 128], [1, ww]]),
                attnb[jt % 2][:, jt * 128:jt * 128 + ww]
            ).then_inc(s_gw.h, 16))
            s_gw.inc(16)
            dbuf_t = s_gw.v
            wait("tensor", s_a, last_exp)
            if jt == 0:
                wait("tensor", s_a, pctx_guard[0])
            for ic in range(icmin, NIC):
                coff = coff0 if ic == icmin else 0
                mm_group([(lambda ic=ic, coff=coff: pctx[ic][0:65, coff:512],
                           lambda jt=jt, h=h: vhb[jt][:, h * 66:h * 66 + 65],
                           lambda jt=jt, ic=ic, coff=coff:
                           attnb[jt % 2][:, ic * 512 + coff:(ic + 1) * 512],
                           jt == 0, False)], inc=(ic == NIC - 1))
            attnb_guard[jt % 2] = (s_mm.v, dbuf_t)

        wait("gpsimd", s_gw, s_gw.v)
        em("gpsimd", lambda g: g.dma_start(
            RAP(cbuf, 0, [[128, L], [1, 32]]),
            RAP(dbuf, OFFB, [[SBD, L], [1, 32]])).then_inc(s_gw.h, 16))
        s_gw.inc(16)
        wait("sync", s_gw, s_gw.v)
        em("sync", lambda sy: sy.dma_start_transpose(
            bndt[:], RAP(cbuf, 0, [[128, L], [1, 128]])).then_inc(s_tp.h, 16))
        s_tp.inc(16)
        wait("gpsimd", s_tp, s_tp.v)
        em("gpsimd", lambda g: g.dma_start(
            RAP(btbuf, 0, [[P5, 32], [1, L]]), bndt[0:32, :]
        ).then_inc(s_gw.h, 16))
        s_gw.inc(16)
        wait("sync", s_gw, s_gw.v)
        em("sync", lambda sy: sy.dma_start(
            a0w[:, :], RAP(btbuf, 0, [[P5 - 1, 32], [1, L]])
        ).then_inc(s_tp.h, 16))
        s_tp.inc(16)
        wait("tensor", s_tp, s_tp.v)
        wait("tensor", s_mm, s_mm.v)
        for ic in range(NIC):
            if not skip_ctxrel:
                mm_group([(lambda ic=ic: pctx[ic][0:64, 0:512],
                           lambda: rl2[:, :],
                           lambda ic=ic: a0w[:, ic * 512:(ic + 1) * 512],
                           False, False)], inc=False)
            mm_group([(lambda ic=ic: pctx[ic][0:65, 0:1],
                       lambda: zt[:, 160:225],
                       lambda: zt[:, 0:1],
                       False, True)], inc=(ic == NIC - 1))
        mm_ctx = s_mm.v
        wait("scalar", s_mm, mm_ctx)
        wait("scalar", s_v, sqb_guard[0])
        for ic in range(NIC):
            em("scalar", lambda s, ic=ic: s.activation(
                sgb[0:1, ic * 512:ic * 512 + 512], pctx[ic][64:65, 0:512],
                AF.Copy))
        ev_t = 0
        for ic in range(NIC):
            if ic == NIC - 1:
                ev_t = s_a.inc()
                em("scalar", lambda s, ic=ic: s.activation(
                    sqb[0:64, ic * 512:ic * 512 + 512], pctx[ic][0:64, 0:512],
                    AF.Copy).then_inc(s_a.h, 1))
            else:
                em("scalar", lambda s, ic=ic: s.activation(
                    sqb[0:64, ic * 512:ic * 512 + 512], pctx[ic][0:64, 0:512],
                    AF.Copy))
        pctx_guard[0] = ev_t
        for ic in range(NIC):
            em("scalar", lambda s, ic=ic: s.activation(
                qrt[1][0:1, 0:512], sgb[0:1, ic * 512:ic * 512 + 512], AF.Ln,
                bias=LNG[0:1, :]))
            buf = qrt[0] if ic % 2 == 0 else qrt[2]
            tgt = s_a.inc()
            em("scalar", lambda s, b=buf, tg=tgt: s.activation(
                b[0:1, 0:512], qrt[1][0:1, 0:512], AF.Exp,
                scale=-1.0).then_inc(s_a.h, 1))
            wait("gpsimd", s_a, tgt)
            gtg = s_gw.inc(16)
            em("gpsimd", lambda g, ic=ic, b=buf: g.dma_start(
                RAP(sgbuf, ic * 512, [[1, 1], [1, 512]]),
                b[0:1, 0:512]).then_inc(s_gw.h, 16))
            if ic >= 1:
                wait("scalar", s_gw, gtg - 16)
        wait("sync", s_gw, s_gw.v)
        em("sync", lambda sy: sy.dma_start(
            sgb[0:64, :], RAP(sgbuf, 0, [[0, 64], [1, L]])
        ).then_inc(s_tp.h, 16))
        s_tp.inc(16)
        wait("vector", s_tp, s_tp.v)
        for ic in range(NIC):
            tgt = s_v.inc()
            em("vector", lambda v, ic=ic, tg=tgt: v.tensor_tensor(
                ctxe[0:64, 0:512], sqb[0:64, ic * 512:(ic + 1) * 512],
                sgb[0:64, ic * 512:(ic + 1) * 512], ALU.mult).then_inc(s_v.h, 1))
            wait("gpsimd", s_v, tgt)
            gtg = s_gw.inc(16)
            em("gpsimd", lambda g, h=h, ic=ic, tg=gtg: g.dma_start(
                RAP(ctxd, (h // 2) * 128 * L + (h % 2) * 64 * L + ic * 512,
                    [[L, 64], [1, 512]]), ctxe[0:64, 0:512]).then_inc(s_gw.h, 16))
            wait("vector", s_gw, gtg)
        sqb_guard[0] = s_v.v

    for u in p4_units:
        u()
    p4_units = []
    barrier()
    esB.close()
    esB2.close()
    if "ctxT" in dbg:
        tmpc = sbC("tmpc", [128, L], BF16)
        for i in range(DT):
            em("sync", lambda sy, i=i, tmpc=tmpc: sy.dma_start(
                tmpc[:], RAP(ctxd, i * 128 * L, [[L, 128], [1, L]])
            ).then_inc(s_tp.h, 16))
            s_tp.inc(16)
            wait("sync", s_tp, s_tp.v)
            em("sync", lambda sy, i=i, tmpc=tmpc: sy.dma_start(
                RAP(dbg["ctxT"], i * 128 * L, [[L, 128], [1, L]]), tmpc[:]
            ).then_inc(s_st.h, 16))
            s_st.inc(16)
            wait("sync", s_st, s_st.v)
    if phases < 6:
        return _finish(nc, plan, dbg, (es, esC))

    wbfC = [sbC(f"wbfC{i}", [128, D], BF16) for i in range(DT)]
    stageC = sbC("stageC", [128, D], F32)
    strm = [sbC(f"strm{i}", [128, D], F32) for i in range(2)]
    maskt = [sbC(f"maskt{i}", [128, D], BF16) for i in range(2)]
    lnt = sbC("lnt", [128, D], F32)
    lnsq = sbC("lnsq", [128, D], F32)
    ytile = sbC("ytile", [128, D], F32)
    xqtile = sbC("xqtile", [128, D], BF16)
    hqTb = [sbC(f"hqTb{i}", [128, 512], BF16) for i in range(HT)]
    w1qt = [sbC(f"w1qt{i}", [128, 1024], BF16) for i in range(2)]
    w2qs = [sbC(f"w2qs{i}", [128, 512], BF16) for i in range(2)]
    ysb = [sbC(f"ysb{i}", [128, D], F32) for i in range(4)]
    b2qb = sbC("b2qb", [128, D], F32)

    for dc in range(2):
        mm_t = mm_group([(lambda: paux2[0:128, 0:512],
                          lambda: onesrow[0:1, 0:128],
                          lambda dc=dc: relrow[0:1, dc * 512:(dc + 1) * 512],
                          True, True)])
        wait("vector", s_mm, mm_t)
        tgt = s_v.inc()
        em("vector", lambda v, dc=dc, tg=tgt: v.tensor_copy(
            b2qb[:, dc * 512:(dc + 1) * 512], paux2[0:128, 0:512]
        ).then_inc(s_v.h, 1))
        wait("tensor", s_v, tgt)

    load_weight_bf16(wo_T, 1.0, wbfC, stageC)
    barrier()

    def ln_tile(src_ap, dst_ap):
        vsafe(lambda v: v.reduce_sum(col[:, 0:1], src_ap, axis=AX.X))
        vsafe(lambda v: v.tensor_scalar(col[:, 0:1], col[:, 0:1],
                                        consts[:, 8:9], None, ALU.mult))
        wait("scalar", s_v, vsync())
        asafe(lambda sc: sc.activation(lnsq[:, 0:D], src_ap, AF.Square,
                                       accum_out=col[:, 1:2]))
        wait("vector", s_a, asynC())
        vsafe(lambda v: v.tensor_scalar(col[:, 1:2], col[:, 1:2],
                                        consts[:, 8:9], None, ALU.mult))
        vsafe(lambda v: v.tensor_tensor(col[:, 2:3], col[:, 0:1],
                                        col[:, 0:1], ALU.mult))
        vsafe(lambda v: v.tensor_tensor(col[:, 3:4], col[:, 1:2],
                                        col[:, 2:3], ALU.subtract))
        wait("scalar", s_v, vsync())
        asafe(lambda sc: sc.activation(col[:, 4:5], col[:, 3:4], AF.Ln,
                                       bias=consts[:, 7:8]))
        asafe(lambda sc: sc.activation(col[:, 4:5], col[:, 4:5], AF.Exp,
                                       scale=-0.5))
        wait("vector", s_a, asynC())
        vsafe(lambda v: v.scalar_tensor_tensor(
            col[:, 5:6], col[:, 0:1], -1.0, col[:, 4:5], ALU.mult, ALU.mult))
        tgt = s_v.inc()
        em("vector", lambda v, d=dst_ap, x=src_ap: v.tensor_scalar(
            d, x, col[:, 4:5], col[:, 5:6], ALU.mult, ALU.add).then_inc(s_v.h, 1))
        return tgt

    for lt in range(LT):
        em("sync", lambda sy, lt=lt: sy.dma_start(
            strm[lt % 2][:, :], RAP(x_nat, lt * 128 * D, [[D, 128], [1, D]])
        ).then_inc(s_ld.h, 16))
        s_ld.inc(16)
        x0_t = s_ld.v
        for dc in range(2):
            for ti in range(DT):
                em("sync", lambda sy, lt=lt, ti=ti: sy.dma_start(
                    ctxt[ti % 2][:, :], RAP(ctxd, ti * 128 * L + lt * 128,
                                            [[L, 128], [1, 128]])
                ).then_inc(s_tp.h, 16))
                s_tp.inc(16)
                wait("tensor", s_tp, s_tp.v)
                mm_t = mm_group([(lambda: paux[:, 0:512],
                                  lambda ti=ti: ctxt[ti % 2][:, :],
                                  lambda ti=ti, dc=dc:
                                  wbfC[ti][:, dc * 512:(dc + 1) * 512],
                                  ti == 0, ti == DT - 1)])
                wait("sync", s_mm, mm_t)
            wait("vector", s_mm, s_mm.v)
            wait("vector", s_ld, x0_t)
            tgt = s_v.inc()
            em("vector", lambda v, lt=lt, dc=dc, tg=tgt: v.tensor_tensor(
                lnt[:, dc * 512:(dc + 1) * 512], paux[:, 0:512],
                strm[lt % 2][:, dc * 512:(dc + 1) * 512], ALU.add
            ).then_inc(s_v.h, 1))
            wait("tensor", s_v, tgt)
        ln_t = ln_tile(lnt[:, 0:D], ytile[:, 0:D])
        wait("gpsimd", s_v, ln_t)
        em("gpsimd", lambda g, lt=lt: g.dma_start(
            RAP(x1d, lt * 128 * D, [[D, 128], [1, D]]), ytile[:, 0:D]
        ).then_inc(s_gw.h, 16))
        s_gw.inc(16)
        for qq in range(2):
            vtg = q_chain(ytile[:, qq * 512:(qq + 1) * 512], 512,
                          [(3, -6, xqtile[:, qq * 512:(qq + 1) * 512])])
        wait("gpsimd", s_v, vtg)
        em("gpsimd", lambda g, lt=lt: g.dma_start(
            RAP(xqd, lt * 128 * 128, [[128, 128], [L * 128, DT], [1, 128]]),
            xqtile[:, 0:D].rearrange("p (b c) -> p b c", c=128)
        ).then_inc(s_gw.h, 16))
        s_gw.inc(16)
        wait("sync", s_gw, s_gw.v)
        wait("vector", s_gw, s_gw.v)
        wait("scalar", s_gw, s_gw.v)
        if "x1" in dbg:
            em("sync", lambda sy, lt=lt: sy.dma_start(
                RAP(dbg["x1"], lt * 128 * D, [[D, 128], [1, D]]), ytile[:, 0:D]
            ).then_inc(s_st.h, 16))
            s_st.inc(16)
            wait("sync", s_st, s_st.v)
    barrier()
    for ti in range(DT):
        em("sync", lambda sy, ti=ti: sy.dma_start_transpose(
            xqTb[ti][:], RAP(xqd, ti * L * 128, [[128, L], [1, 128]])
        ).then_inc(s_tp.h, 16))
        s_tp.inc(16)
    wait("sync", s_tp, s_tp.v)
    barrier()
    if phases < 7:
        return _finish(nc, plan, dbg, (es, esC))

    for lc in range(NIC):
        for ht in range(HT):
            em("sync", lambda sy, ht=ht: sy.dma_start(
                w1qt[ht % 2][:, :], RAP(w1qd, ht * DT * 16384,
                                        [[128, 128], [16384, DT], [1, 128]])
            ).then_inc(s_tp.h, 16))
            s_tp.inc(16)
            wait("tensor", s_tp, s_tp.v)
            items = [(lambda: paux[:, 0:512],
                      lambda ht=ht, ti=ti: w1qt[ht % 2][:, ti * 128:(ti + 1) * 128],
                      lambda ti=ti, lc=lc: xqTb[ti][:, lc * 512:(lc + 1) * 512],
                      ti == 0, ti == DT - 1) for ti in range(DT)]
            mm_t = mm_group(items)
            wait("sync", s_mm, mm_t)
            wait("vector", s_mm, mm_t)
            tgt = s_v.inc()
            em("vector", lambda v, ht=ht, tg=tgt: v.tensor_scalar(
                qrt[0][:, 0:512], paux[:, 0:512], b1qb[:, ht:ht + 1], 0.0,
                ALU.add, ALU.max).then_inc(s_v.h, 1))
            wait("tensor", s_v, tgt)
            wait("scalar", s_v, tgt)
            vtg = q_chain(qrt[0][:, 0:512], 512,
                          [(10, -14, qrt[1][:, 0:512]),
                           (3, -6, hqTb[ht][:, 0:512])])
            if "hq0" in dbg and lc == 0 and ht == 0:
                wait("sync", s_v, vtg)
                em("sync", lambda sy: sy.dma_start(
                    dbg["hq0"].ap(), hqTb[0][:, 0:512]).then_inc(s_st.h, 16))
                s_st.inc(16)
        wait("tensor", s_v, s_v.v)
        for dc in range(2):
            for ht in range(HT):
                em("sync", lambda sy, ht=ht, dc=dc: sy.dma_start(
                    w2qs[ht % 2][:, :], RAP(w2qd, ht * 128 * D + dc * 512,
                                            [[D, 128], [1, 512]])
                ).then_inc(s_tp.h, 16))
                s_tp.inc(16)
                wait("tensor", s_tp, s_tp.v)
                for lt2 in range(4):
                    mm_group([(lambda lt2=lt2: pctx[lt2][:, 0:512],
                               lambda ht=ht, lt2=lt2:
                               hqTb[ht][:, lt2 * 128:(lt2 + 1) * 128],
                               lambda ht=ht: w2qs[ht % 2][:, :],
                               ht == 0, ht == HT - 1)],
                              inc=(lt2 == 3))
                wait("sync", s_mm, s_mm.v)
            mmB = s_mm.v
            wait("vector", s_mm, mmB)
            for lt2 in range(4):
                tgt = s_v.inc()
                em("vector", lambda v, lt2=lt2, dc=dc, tg=tgt: v.tensor_tensor(
                    ysb[lt2][:, dc * 512:(dc + 1) * 512], pctx[lt2][:, 0:512],
                    b2qb[:, dc * 512:(dc + 1) * 512], ALU.add).then_inc(s_v.h, 1))
            wait("tensor", s_v, s_v.v)
        wait("scalar", s_v, s_v.v)
        wait("vector", s_v, s_v.v)
        for lt2 in range(4):
            lt = lc * 4 + lt2
            em("sync", lambda sy, lt=lt: sy.dma_start(
                maskt[lt % 2][:, :],
                RAP(maskin, lt * 128 * D, [[D, 128], [1, D]])
            ).then_inc(s_ld.h, 16))
            s_ld.inc(16)
            em("sync", lambda sy, lt=lt: sy.dma_start(
                strm[lt % 2][:, :], RAP(x1d, lt * 128 * D, [[D, 128], [1, D]])
            ).then_inc(s_ld.h, 16))
            s_ld.inc(16)
            ld_t = s_ld.v
            for qq in range(2):
                vtg = q_chain(ysb[lt2][:, qq * 512:(qq + 1) * 512], 512,
                              [(10, -14, ytile[:, qq * 512:(qq + 1) * 512])])
            wait("vector", s_ld, ld_t)
            em("vector", lambda v, lt=lt: v.scalar_tensor_tensor(
                ytile[:, 0:D], ytile[:, 0:D], consts[:, 6:7], maskt[lt % 2][:, :],
                ALU.mult, ALU.mult))
            tgt = s_v.inc()
            em("vector", lambda v, lt=lt, tg=tgt: v.tensor_tensor(
                lnt[:, 0:D], ytile[:, 0:D], strm[lt % 2][:, :], ALU.add
            ).then_inc(s_v.h, 1))
            wait("scalar", s_v, tgt)
            ln_t = ln_tile(lnt[:, 0:D], ytile[:, 0:D])
            wait("sync", s_v, ln_t)
            em("sync", lambda sy, lt=lt: sy.dma_start(
                RAP(out, lt * 128 * D, [[D, 128], [1, D]]), ytile[:, 0:D]
            ).then_inc(s_st.h, 16))
            s_st.inc(16)
            wait("vector", s_st, s_st.v)
            wait("scalar", s_st, s_st.v)

    st_all = s_st.v
    for eng in ENGS:
        wait(eng, s_st, st_all)
    return _finish(nc, plan, dbg, (es, esC))


def _finish(nc, plan, dbg, stacks):
    with nc.Block() as block:
        def run_for(eng_name):
            def body(eng):
                for e, fn in plan:
                    if e == eng_name:
                        fn(eng)
            return body
        block.sync(run_for("sync"))
        block.gpsimd(run_for("gpsimd"))
        block.tensor(run_for("tensor"))
        block.vector(run_for("vector"))
        block.scalar(run_for("scalar"))
    for st in stacks:
        try:
            st.close()
        except Exception:
            pass
    return nc, dbg


def _host_constants():
    t = np.arange(L, dtype=np.float64)
    inv_freq = 1.0 / (10000.0 ** (np.arange(0, 64, 2, dtype=np.float64) / 64.0))
    ang = np.outer(t, inv_freq)
    cos = np.cos(ang)
    sin = np.sin(ang)
    p = np.arange(128)
    m = (p % 64) // 2
    c2 = cos[:, m].T.astype(np.float32)
    s2 = sin[:, m].T.astype(np.float32)
    s2 = np.where((p % 2 == 0)[:, None], -s2, s2).astype(np.float32)
    ident = np.eye(128, dtype=ml_dtypes.bfloat16)
    return np.ascontiguousarray(c2), np.ascontiguousarray(s2), ident


def _threefry2x32(k0, k1, x0, x1):
    k0 = np.uint32(k0); k1 = np.uint32(k1)
    k2 = np.uint32(k0 ^ k1 ^ np.uint32(0x1BD11BDA))
    ks = [k0, k1, k2]
    R = [[13, 15, 26, 6], [17, 29, 16, 24]]
    x0 = (x0 + ks[0]).astype(np.uint32)
    x1 = (x1 + ks[1]).astype(np.uint32)
    for i in range(5):
        for r in (R[0] if i % 2 == 0 else R[1]):
            x0 = (x0 + x1).astype(np.uint32)
            x1 = ((x1 << np.uint32(r)) | (x1 >> np.uint32(32 - r))).astype(np.uint32)
            x1 = (x0 ^ x1).astype(np.uint32)
        x0 = (x0 + ks[(i + 1) % 3]).astype(np.uint32)
        x1 = (x1 + ks[(i + 2) % 3] + np.uint32(i + 1)).astype(np.uint32)
    return x0, x1


def _dropout_mask():
    n = 8 * L * D
    iota = np.arange(n, dtype=np.uint64)
    c1 = (iota >> np.uint64(32)).astype(np.uint32)
    c2 = (iota & np.uint64(0xFFFFFFFF)).astype(np.uint32)
    b1, b2 = _threefry2x32(0, 42, c1, c2)
    if os.environ.get("BASS_SIM_FP64") == "1":
        bits = (b1.astype(np.uint64) << np.uint64(32)) | b2.astype(np.uint64)
        u = ((bits >> np.uint64(12)) | np.uint64(0x3FF0000000000000)
             ).view(np.float64) - 1.0
        keep = u < 0.8
    else:
        bits = (b1 ^ b2).astype(np.uint32)
        u = ((bits >> np.uint32(9)) | np.uint32(0x3F800000)
             ).view(np.float32) - np.float32(1.0)
        keep = u < np.float32(0.8)
    return keep.reshape(8, L, D).astype(ml_dtypes.bfloat16)


_CACHE = {}


def kernel(L0_attn_x, L0_attn_w_q, L0_attn_w_k, L0_attn_w_v, L0_attn_w_o,
           L0_attn_rel_emb, L0_ffn_w1, L0_ffn_b1, L0_ffn_w2, L0_ffn_b2):
    if "nc" not in _CACHE:
        _CACHE["nc"], _ = build_core_program()
    nc = _CACHE["nc"]

    c2, s2, ident = _host_constants()
    maskf = _dropout_mask()

    x = np.ascontiguousarray(np.asarray(L0_attn_x, dtype=np.float32))
    wq = np.ascontiguousarray(np.asarray(L0_attn_w_q, dtype=np.float32).T)
    wk = np.ascontiguousarray(np.asarray(L0_attn_w_k, dtype=np.float32).T)
    wv = np.ascontiguousarray(np.asarray(L0_attn_w_v, dtype=np.float32).T)
    wo = np.ascontiguousarray(np.asarray(L0_attn_w_o, dtype=np.float32).T)
    w1 = np.ascontiguousarray(np.asarray(L0_ffn_w1, dtype=np.float32).T)
    w2 = np.ascontiguousarray(np.asarray(L0_ffn_w2, dtype=np.float32).T)
    b1 = np.ascontiguousarray(np.asarray(L0_ffn_b1, dtype=np.float32)[None, :])
    b2 = np.ascontiguousarray(np.asarray(L0_ffn_b2, dtype=np.float32)[None, :])
    rel = np.asarray(L0_attn_rel_emb, dtype=np.float32)
    idx = [32 - w for w in range(32)]
    rel_Tw = np.ascontiguousarray(
        np.concatenate([rel[idx, :64].T, rel[0:1, :64].T], axis=1))
    rel_wrev = np.ascontiguousarray(
        np.concatenate([rel[idx, :64], rel[0:1, :64]], axis=0))

    in_maps = []
    for b in range(8):
        xb = np.ascontiguousarray(x[b])
        in_maps.append({
            "x_nat": xb,
            "x_T": np.ascontiguousarray(xb.T),
            "wq_T": wq, "wk_T": wk, "wv_T": wv, "wo_T": wo,
            "w1_T": w1, "w2_T": w2, "b1": b1, "b2": b2,
            "rel_Tw": rel_Tw, "rel_wrev": rel_wrev,
            "rel0r": np.ascontiguousarray(rel[0:1, :64]),
            "maskin": np.ascontiguousarray(maskf[b]),
            "c2": c2, "s2": s2, "ident": ident,
        })
    res = run_bass_kernel_spmd(nc, in_maps, core_ids=list(range(8)))
    return np.stack([r["out"].reshape(L, D) for r in res.results], axis=0)


# version 05
# speedup vs baseline: 2.7077x; 1.2607x over previous
import os
import math
import contextlib
import numpy as np
import ml_dtypes
import concourse.bass as bass
import concourse.mybir as mybir
from bass_rust import AP as RAP
from concourse.bass_utils import run_bass_kernel_spmd

F32 = mybir.dt.float32
BF16 = mybir.dt.bfloat16
I32 = mybir.dt.int32
_EXACT_LADDER = (os.environ.get("BASS_SIM_FP64") == "1"
                 or os.environ.get("BASS_DEV_LADDER") == "1")
AF = mybir.ActivationFunctionType
ALU = mybir.AluOpType
AX = mybir.AxisListType

L = 2048
D = 1024
NH = 16
HID = 4096
LT = L // 128
DT = D // 128
HT = HID // 128
NIC = L // 512

P6 = L + 64
P7 = 320
SBD = 320
P5 = L + 64
OFFB = 127

LN2 = float(np.log(np.float64(2.0)))
C2LN2 = float(1.0 / (2.0 * np.log(np.float64(2.0))))
NEG1E18 = -1.0e18
EPS = 1e-5
ENGS = ("sync", "gpsimd", "tensor", "vector", "scalar")


class Sem:
    def __init__(self, h):
        self.h = h
        self.v = 0

    def inc(self, n=1):
        self.v += n
        return self.v


def build_core_program(debug=(), nheads=NH, phases=99, skip_ctxrel=False):
    nc = bass.Bass("TRN2", target_bir_lowering=False, detect_race_conditions=False)

    din = lambda name, shape, dt: nc.dram_tensor(name, shape, dt, kind="ExternalInput")
    x_nat = din("x_nat", [L, D], F32)
    x_T = din("x_T", [D, L], F32)
    wq_T = din("wq_T", [D, D], F32)
    wk_T = din("wk_T", [D, D], F32)
    wv_T = din("wv_T", [D, D], F32)
    wo_T = din("wo_T", [D, D], F32)
    w1_T = din("w1_T", [D, HID], F32)
    w2_T = din("w2_T", [HID, D], F32)
    b1in = din("b1", [1, HID], F32)
    b2in = din("b2", [1, D], F32)
    rel_Tw = din("rel_Tw", [64, 33], F32)
    rel_wrev = din("rel_wrev", [33, 64], F32)
    rel0r_in = din("rel0r", [1, 64], F32)
    maskin = din("maskin", [L, D], BF16)
    c2in = din("c2", [128, L], F32)
    s2in = din("s2", [128, L], F32)
    identin = din("ident", [128, 128], BF16)

    out = nc.dram_tensor("out", [L, D], F32, kind="ExternalOutput")
    dbg = {}
    def ddbg(name, shape, dt):
        if name in debug:
            dbg[name] = nc.dram_tensor("dbg_" + name, shape, dt, kind="ExternalOutput")
    ddbg("qT", [D, L], BF16)
    ddbg("kT", [D, L], BF16)
    ddbg("vh", [L, NH * 66], BF16)
    ddbg("negu", [2, NH], F32)
    ddbg("nrm", [2, 16], F32)
    ddbg("um", [2, 48], F32)
    ddbg("unew", [2, 80], F32)
    ddbg("ctxT", [D, L], BF16)
    ddbg("sg", [1, L], F32)
    ddbg("a0", [32, L], BF16)
    ddbg("x1", [L, D], F32)
    ddbg("hq0", [128, 512], BF16)
    ddbg("qd", [4, 512], F32)

    dram = lambda name, n, dt: nc.dram_tensor(name, [n], dt, kind="Internal")
    hbuf = dram("hbuf", 4 * 32 * P6, BF16)
    mbuf = dram("mbuf", 2 * 16 * 32 * 128, BF16)
    wbuf = dram("wbuf", NH * L * P7, BF16)
    dbuf = dram("dbuf", L * SBD, BF16)
    cbuf = dram("cbuf", L * 128, BF16)
    btbuf = dram("btbuf", 32 * P5, BF16)
    relbuf = dram("relbuf", 64 * 32, BF16)
    ubuf = dram("ubuf", 64, F32)
    sgbuf = dram("sgbuf", L, F32)
    ctxd = dram("ctxd", D * L, BF16)
    w1qd = dram("w1qd", D * HID, BF16)
    w2qd = dram("w2qd", HID * D, BF16)
    x1d = dram("x1d", L * D, F32)
    xqd = dram("xqd", L * D, BF16)

    es = contextlib.ExitStack()
    esA = contextlib.ExitStack()
    esB = contextlib.ExitStack()
    esB2 = contextlib.ExitStack()
    esC = contextlib.ExitStack()

    def mk(stk, side):
        def f(name, shape, dt):
            return stk.enter_context(nc.sbuf_tensor(name, shape, dt, side=side))
        return f
    sbG = mk(es, "left")
    sbA = mk(esA, "right")
    sbB = mk(esB, "left")
    sbB2 = mk(esB2, "right")
    sbC = mk(esC, "right")
    psum = lambda name: es.enter_context(nc.psum_tensor(name, [128, 512], F32))
    sem = lambda name: Sem(es.enter_context(nc.semaphore(name)))

    xqTb = [sbG(f"xqTb{i}", [128, L], BF16) for i in range(DT)]
    vhb = [sbG(f"vhb{i}", [128, NH * 66], BF16) for i in range(LT)]
    scr = {k: sbG("scr_" + k, [128, 512], F32) for k in ("a", "b", "c", "s", "u", "w")}
    scr["i"] = sbG("scr_i", [128, 512], I32)
    scr["j"] = sbG("scr_j", [128, 512], I32)
    qrt = [sbG(f"qrt{i}", [128, 512], F32) for i in range(3)]
    ctxe = sbG("ctxe", [64, 512], BF16)
    identb = sbG("identb", [128, 128], BF16)
    relblk = sbG("relblk", [128, 64], BF16)
    rl2 = sbG("rl2", [32, 64], BF16)
    rl2f = sbG("rl2f", [32, 64], F32)
    relrow = sbG("relrow", [1, D], BF16)
    onesrow = sbG("onesrow", [1, 128], BF16)
    onesf = sbG("onesf", [1, 128], F32)
    hhsel = sbG("hhsel", [128, 2], BF16)
    consts = sbG("consts", [128, 12], F32)
    nrm = sbG("nrm", [2, 16], F32)
    umath = sbG("umath", [2, 48], F32)
    bmcol = sbG("bmcol", [2, 2], F32)
    negu = sbG("negu", [128, NH], F32)
    unew = sbG("unew", [128, 80], F32)
    col = sbG("col", [128, 8], F32)
    b1qb = sbG("b1qb", [128, HT], F32)
    b1raw = sbG("b1raw", [128, HT], F32)
    gt = [sbG(f"gt{i}", [128, 32], BF16) for i in range(2)]
    wint = [sbG(f"wint{i}", [128, 160], BF16) for i in range(2)]
    ctxt = [sbG(f"ctxt{i}", [128, 128], BF16) for i in range(2)]
    zt = sbG("zt", [128, P7], BF16)

    c2b = sbA("c2b", [128, L], F32)
    s2b = sbA("s2b", [128, L], F32)
    stageA = sbA("stageA", [128, D], F32)
    wbfA = [sbA(f"wbfA{i}", [128, D], BF16) for i in range(DT)]
    relw = sbA("relw", [64, 33], F32)
    relv = sbA("relv", [33, 64], F32)
    rel0b = sbA("rel0b", [128, D], F32)
    rel0r = sbA("rel0r_t", [1, 64], F32)

    qTb = [sbB(f"qTb{i}", [128, L], BF16) for i in range(DT)]
    kTb = [sbB(f"kTb{i}", [128, L], BF16) for i in range(DT)]
    attnb = [sbB(f"attnb{i}", [128, L], BF16) for i in range(2)]


    pqk = [psum("pqk0"), psum("pqk1")]
    pctx = [psum(f"pctx{i}") for i in range(4)]
    paux = psum("paux")
    paux2 = psum("paux2")

    s_ld = sem("s_ld")
    s_gw = sem("s_gw")
    s_tp = sem("s_tp")
    s_mm = sem("s_mm")
    s_v = sem("s_v")
    s_a = sem("s_a")
    s_st = sem("s_st")
    s_zz = sem("s_zz")
    s_bar = sem("s_bar")

    plan = []
    em = lambda eng, fn: plan.append((eng, fn))
    def wait(eng, s, val):
        em(eng, lambda e, ss=s.h, vv=val: e.wait_ge(ss, vv))

    nbar = [0]
    def barrier():
        nbar[0] += 1
        tgt = 5 * nbar[0]
        for eng in ENGS:
            em(eng, lambda e: e.drain().then_inc(s_bar.h, 1))
            wait(eng, s_bar, tgt)

    def vsafe(fn):
        tgt = s_v.inc()
        em("vector", lambda v, f=fn: f(v).then_inc(s_v.h, 1))
        em("vector", lambda v, t=tgt: v.wait_ge(s_v.h, t))
        return tgt

    def asafe(fn):
        tgt = s_a.inc()
        em("scalar", lambda sc, f=fn: f(sc).then_inc(s_a.h, 1))
        em("scalar", lambda sc, t=tgt: sc.wait_ge(s_a.h, t))
        return tgt

    def vsync():
        tgt = s_v.inc()
        em("vector", lambda v, tg=tgt: v.drain().then_inc(s_v.h, 1))
        return tgt

    def asynC():
        tgt = s_a.inc()
        em("scalar", lambda s, tg=tgt: s.drain().then_inc(s_a.h, 1))
        return tgt

    def mm_group(items, inc=True):
        n = len(items)
        for idx, (o, l, r, st, sp) in enumerate(items):
            if idx == n - 1 and inc:
                tgt = s_mm.inc()
                em("tensor", lambda t, o=o, l=l, r=r, st=st, sp=sp, tg=tgt:
                   t.matmul(o(), l(), r(), start=st, stop=sp).then_inc(s_mm.h, 1))
            else:
                em("tensor", lambda t, o=o, l=l, r=r, st=st, sp=sp:
                   t.matmul(o(), l(), r(), start=st, stop=sp))
        return s_mm.v

    LNG = consts[:, 0:1]
    BIAS_P = {3: consts[:, 1:2], 10: consts[:, 3:4]}
    BIAS_N = {3: consts[:, 2:3], 10: consts[:, 4:5]}

    qd_done = []

    def q_chain_fast(x_ap, W, stages, P=128):
        E = vsafe if W < 64 else (lambda fn: em("vector", fn))
        sa = lambda n: scr[n][:P, :W]
        src = x_ap
        for st_i, (man, min_e, dst) in enumerate(stages):
            E(lambda v: v.tensor_scalar(sa("i"), x_ap.bitcast(I32), 23, 255,
                                        ALU.logical_shift_right, ALU.bitwise_and))
            E(lambda v, me=min_e: v.tensor_scalar(
                sa("c"), sa("i"), -127, int(me), ALU.add, ALU.max))
            E(lambda v, man=man: v.tensor_scalar(
                sa("j"), sa("c"), -(1 << 23), (127 + man) << 23,
                ALU.mult, ALU.add))
            E(lambda v, src=src: v.tensor_tensor(
                sa("u"), src, scr["j"][:P, :W].bitcast(F32), ALU.mult))
            E(lambda v, man=man: v.tensor_scalar(
                sa("j"), sa("c"), 1 << 23, (127 - man) << 23,
                ALU.mult, ALU.add))
            E(lambda v: v.tensor_copy(sa("i"), sa("u")))
            if st_i == len(stages) - 1 and W >= 64:
                tgt = s_v.inc()
                em("vector", lambda v, d=dst, tg=tgt: v.tensor_tensor(
                    d, sa("i"), scr["j"][:P, :W].bitcast(F32), ALU.mult
                ).then_inc(s_v.h, 1))
            else:
                E(lambda v, d=dst: v.tensor_tensor(
                    d, sa("i"), scr["j"][:P, :W].bitcast(F32), ALU.mult))
            src = dst
        return s_v.v

    def q_chain(x_ap, W, stages, P=128):
        if not _EXACT_LADDER:
            return q_chain_fast(x_ap, W, stages, P)
        def V(fn):
            return vsafe(fn)
        def A(fn):
            return asafe(fn)
        sa = lambda n: scr[n][:P, :W]

        def rstrict(src_ap, r_dst, tie_even):
            V(lambda v: v.tensor_copy(sa("i"), src_ap))
            V(lambda v: v.tensor_tensor(sa("b"), src_ap, sa("i"), ALU.subtract))
            V(lambda v: v.tensor_scalar(sa("w"), sa("b"), 0.5, None, ALU.is_gt))
            V(lambda v: v.tensor_tensor(sa("w"), sa("i"), sa("w"), ALU.add))
            V(lambda v: v.tensor_scalar(sa("c"), sa("b"), -0.5, None, ALU.is_lt))
            if not tie_even:
                V(lambda v, r=r_dst: v.tensor_tensor(r, sa("w"), sa("c"),
                                                     ALU.subtract))
                return
            V(lambda v: v.tensor_tensor(sa("w"), sa("w"), sa("c"), ALU.subtract))
            V(lambda v: v.tensor_tensor(sa("c"), sa("b"), sa("b"), ALU.mult))
            V(lambda v: v.tensor_scalar(sa("c"), sa("c"), 0.25, None, ALU.is_equal))
            V(lambda v: v.tensor_scalar(sa("i"), sa("i"), 1, None, ALU.bitwise_and))
            V(lambda v: v.tensor_tensor(sa("c"), sa("c"), sa("i"), ALU.mult))
            V(lambda v: v.tensor_tensor(sa("c"), sa("c"), sa("b"), ALU.mult))
            V(lambda v: v.tensor_scalar(sa("c"), sa("c"), 2.0, None, ALU.mult))
            V(lambda v, r=r_dst: v.tensor_tensor(r, sa("w"), sa("c"), ALU.add))

        wait("scalar", s_v, s_v.v)
        A(lambda sc, x=x_ap: sc.activation(sa("a"), x, AF.Square))
        A(lambda sc: sc.activation(sa("b"), sa("a"), AF.Ln, bias=LNG[:P, :]))
        wait("vector", s_a, asynC())
        V(lambda v: v.tensor_scalar(sa("u"), sa("b"), C2LN2, -0.5,
                                    ALU.mult, ALU.add))
        rstrict(sa("u"), sa("a"), False)
        if "qd" in dbg and not qd_done:
            qd_done.append(1)
            V(lambda v: v.tensor_copy(scr["i"][0:1, 0:512], scr["u"][0:1, 0:512]))
            V(lambda v: v.tensor_scalar(scr["w"][0:1, 0:512], scr["i"][0:1, 0:512],
                                        0.0, None, ALU.add))
            wait("sync", s_v, vsync())
            for row, tt in ((0, "u"), (1, "w"), (2, "a"), (3, "b")):
                em("sync", lambda sy, row=row, tt=tt: sy.dma_start(
                    RAP(dbg["qd"], row * 512, [[512, 1], [1, 512]]),
                    scr[tt][0:1, 0:512]).then_inc(s_st.h, 16))
                s_st.inc(16)
            wait("vector", s_st, s_st.v)
        src = x_ap
        for (man, min_e, dst) in stages:
            V(lambda v, me=min_e: v.tensor_scalar(
                sa("c"), sa("a"), float(me), None, ALU.max))
            V(lambda v: v.tensor_copy(sa("i"), sa("c")))
            V(lambda v, man=man: v.tensor_scalar(
                sa("j"), sa("i"), -1, 127 + man, ALU.mult, ALU.add))
            V(lambda v: v.tensor_scalar(sa("j"), sa("j"), 23, None,
                                        ALU.logical_shift_left))
            V(lambda v, src=src: v.tensor_tensor(
                sa("u"), src, scr["j"][:P, :W].bitcast(F32), ALU.mult))
            V(lambda v, man=man: v.tensor_scalar(
                sa("j"), sa("i"), 1, 127 - man, ALU.mult, ALU.add))
            V(lambda v: v.tensor_scalar(sa("j"), sa("j"), 23, None,
                                        ALU.logical_shift_left))
            V(lambda v: v.tensor_copy(sa("s"), scr["j"][:P, :W].bitcast(F32)))
            rstrict(sa("u"), sa("u"), True)
            V(lambda v, d=dst: v.tensor_tensor(d, sa("u"), sa("s"), ALU.mult))
            src = dst
        return vsync()

    for c, val in ((0, 1e-38), (1, 3.0 * LN2), (2, -3.0 * LN2), (3, 10.0 * LN2),
                   (4, -10.0 * LN2), (5, 0.8), (7, EPS), (8, 1.0 / float(D))):
        vsafe(lambda v, c=c, val=val: v.memset(consts[:, c:c + 1], val))
    vsafe(lambda v: v.reciprocal(consts[:, 6:7], consts[:, 5:6]))
    em("vector", lambda v: v.memset(zt[:], 0.0))
    em("vector", lambda v: v.memset(onesrow[:], 1.0))
    em("vector", lambda v: v.memset(onesf[:], 1.0))
    em("vector", lambda v: v.memset(hhsel[:, :], 0.0))
    em("vector", lambda v: v.memset(hhsel[0:64, 0:1], 1.0))
    em("vector", lambda v: v.memset(hhsel[64:128, 1:2], 1.0))
    em("vector", lambda v: v.memset(relblk[:, :], 0.0))
    em("vector", lambda v: v.memset(umath[:, :], 0.0))
    em("vector", lambda v: v.memset(nrm[:, :], 0.0))
    wait("gpsimd", s_v, vsync())

    def zfill(buf, span):
        pos = 0
        while pos < span:
            n = min(128 * P7, span - pos)
            full = n // P7
            if full > 0:
                em("gpsimd", lambda g, b=buf, p=pos, f=full: g.dma_start(
                    RAP(b, p, [[P7, f], [1, P7]]),
                    zt[0:f, :]).then_inc(s_zz.h, 16))
                s_zz.inc(16)
                pos += full * P7
            else:
                em("gpsimd", lambda g, b=buf, p=pos, n=n: g.dma_start(
                    RAP(b, p, [[1, 1], [1, n]]),
                    zt[0:1, 0:n]).then_inc(s_zz.h, 16))
                s_zz.inc(16)
                pos += n

    for buf, span in ((hbuf, 4 * 32 * P6), (wbuf, NH * L * P7), (dbuf, L * SBD),
                      (cbuf, L * 128), (btbuf, 32 * P5)):
        zfill(buf, span)
    zz_all = s_zz.v

    wait("vector", s_zz, zz_all)
    em("vector", lambda v: v.memset(zt[0:128, 0:OFFB], NEG1E18))
    wait("gpsimd", s_v, vsync())
    for h16 in range(NH):
        for c in range(LT):
            em("gpsimd", lambda g, h16=h16, c=c: g.dma_start(
                RAP(wbuf, h16 * L * P7 + c * 128 * P7, [[P7, 128], [1, OFFB]]),
                zt[0:128, 0:OFFB]).then_inc(s_gw.h, 16))
            s_gw.inc(16)

    for tile, src in ((c2b, c2in), (s2b, s2in), (identb, identin),
                      (relw, rel_Tw), (relv, rel_wrev), (rel0r, rel0r_in)):
        em("sync", lambda sy, t=tile, s=src: sy.dma_start(
            t[:], s.ap()).then_inc(s_ld.h, 16))
        s_ld.inc(16)
    ld_tab = s_ld.v

    wait("vector", s_ld, ld_tab)
    vsafe(lambda v: v.tensor_scalar(
        qrt[0][0:64, 0:32], relw[:, 0:32], relw[:, 32:33], None, ALU.subtract))
    em("vector", lambda v: v.tensor_copy(ctxe[0:64, 0:32], qrt[0][0:64, 0:32]))
    wait("gpsimd", s_v, vsync())
    em("gpsimd", lambda g: g.dma_start(
        RAP(relbuf, 0, [[32, 64], [1, 32]]), ctxe[0:64, 0:32]
    ).then_inc(s_gw.h, 16))
    s_gw.inc(16)
    wait("sync", s_gw, s_gw.v)
    em("sync", lambda sy: sy.dma_start(
        relblk[0:64, 0:32], RAP(relbuf, 0, [[32, 64], [1, 32]])
    ).then_inc(s_tp.h, 16))
    s_tp.inc(16)
    em("sync", lambda sy: sy.dma_start(
        relblk[64:128, 32:64], RAP(relbuf, 0, [[32, 64], [1, 32]])
    ).then_inc(s_tp.h, 16))
    s_tp.inc(16)

    wait("tensor", s_ld, ld_tab)
    wait("tensor", s_v, s_v.v)
    mm_t = mm_group([(lambda: paux[0:32, 0:64], lambda: onesf[0:1, 0:32],
                      lambda: rel0r[0:1, 0:64], True, True)])
    wait("vector", s_mm, mm_t)
    vsafe(lambda v: v.tensor_tensor(rl2f[:, :], relv[0:32, 0:64],
                                    paux[0:32, 0:64], ALU.subtract))
    em("vector", lambda v: v.tensor_copy(rl2[:, :], rl2f[:, :]))
    vsafe(lambda v: v.tensor_copy(relrow[0:1, 0:64], rel0r[0:1, 0:64]))
    for hh in range(1, NH):
        em("vector", lambda v, hh=hh: v.tensor_copy(
            relrow[0:1, hh * 64:(hh + 1) * 64], relrow[0:1, 0:64]))
    wait("tensor", s_v, vsync())
    for dc in range(2):
        mm_t = mm_group([(lambda: paux2[0:128, 0:512],
                          lambda: onesrow[0:1, 0:128],
                          lambda dc=dc: relrow[0:1, dc * 512:(dc + 1) * 512],
                          True, True)])
        wait("vector", s_mm, mm_t)
        vt = s_v.inc()
        em("vector", lambda v, dc=dc, tg=vt: v.tensor_copy(
            rel0b[:, dc * 512:(dc + 1) * 512], paux2[0:128, 0:512]
        ).then_inc(s_v.h, 1))
        wait("tensor", s_v, vt)
    barrier()

    xTb = xqTb
    for i in range(DT):
        for hf in range(2):
            em("sync", lambda sy, i=i, hf=hf: sy.dma_start(
                stageA[:], RAP(x_T, i * 128 * L + hf * 1024, [[L, 128], [1, 1024]])
            ).then_inc(s_ld.h, 16))
            s_ld.inc(16)
            wait("scalar", s_ld, s_ld.v)
            tgt = s_a.inc()
            em("scalar", lambda s, i=i, hf=hf, tg=tgt: s.activation(
                xTb[i][:, hf * 1024:(hf + 1) * 1024], stageA[:],
                AF.Copy).then_inc(s_a.h, 1))
            wait("sync", s_a, tgt)
    barrier()

    def load_weight_bf16(src, scale, wbf, stg):
        for i in range(DT):
            em("sync", lambda sy, i=i, s=src, stg=stg: sy.dma_start(
                stg[:, 0:D], RAP(s, i * 128 * D, [[D, 128], [1, D]])
            ).then_inc(s_ld.h, 16))
            s_ld.inc(16)
            wait("scalar", s_ld, s_ld.v)
            tgt = s_a.inc()
            em("scalar", lambda s, i=i, sc=scale, wbf=wbf, stg=stg, tg=tgt:
               s.activation(wbf[i][:, :], stg[:, 0:D], AF.Copy, scale=sc
                            ).then_inc(s_a.h, 1))
            wait("sync", s_a, tgt)

    def proj_qk(dstb):
        for dt_ in range(DT):
            for lc in range(NIC):
                pb = pqk[(dt_ * NIC + lc) % 2]
                items = [(lambda pb=pb: pb[:, 0:512],
                          lambda ti=ti, dt_=dt_: wbfA[ti][:, dt_ * 128:(dt_ + 1) * 128],
                          lambda ti=ti, lc=lc: xTb[ti][:, lc * 512:(lc + 1) * 512],
                          ti == 0, ti == DT - 1) for ti in range(DT)]
                mm_t = mm_group(items)
                wait("vector", s_mm, mm_t)
                sl = slice(lc * 512, (lc + 1) * 512)
                em("vector", lambda v, pb=pb: v.tensor_copy(
                    qrt[0][:, 0:512], pb[:, 0:512]))
                em("vector", lambda v: v.stream_shuffle(
                    qrt[1][:, 0:512], qrt[0][:, 0:512], [i ^ 1 for i in range(32)]))
                em("vector", lambda v, sl=sl: v.tensor_tensor(
                    qrt[2][:, 0:512], qrt[0][:, 0:512], c2b[:, sl], ALU.mult))
                em("vector", lambda v, sl=sl: v.tensor_tensor(
                    qrt[1][:, 0:512], qrt[1][:, 0:512], s2b[:, sl], ALU.mult))
                tgt = s_v.inc()
                em("vector", lambda v, dt_=dt_, sl=sl, dstb=dstb, tg=tgt:
                   v.tensor_tensor(dstb[dt_][:, sl], qrt[2][:, 0:512],
                                   qrt[1][:, 0:512], ALU.add).then_inc(s_v.h, 1))
                wait("tensor", s_v, tgt)

    load_weight_bf16(wq_T, 0.125, wbfA, stageA)
    barrier()
    proj_qk(qTb)
    barrier()
    load_weight_bf16(wk_T, 1.0, wbfA, stageA)
    barrier()
    proj_qk(kTb)
    barrier()
    load_weight_bf16(wv_T, 1.0, wbfA, stageA)
    for lt in range(LT):
        for h in range(NH):
            em("vector", lambda v, lt=lt, h=h: v.memset(
                vhb[lt][:, h * 66 + 64:h * 66 + 65], 1.0))
            em("vector", lambda v, lt=lt, h=h: v.memset(
                vhb[lt][:, h * 66 + 65:h * 66 + 66], 0.0))
    vsync()
    barrier()
    for lt in range(LT):
        for dc in range(2):
            pb = pqk[(lt * 2 + dc) % 2]
            items = [(lambda pb=pb: pb[:, 0:512],
                      lambda ti=ti, lt=lt: xTb[ti][:, lt * 128:(lt + 1) * 128],
                      lambda ti=ti, dc=dc: wbfA[ti][:, dc * 512:(dc + 1) * 512],
                      ti == 0, ti == DT - 1) for ti in range(DT)]
            mm_t = mm_group(items)
            wait("vector", s_mm, mm_t)
            em("vector", lambda v, pb=pb, dc=dc: v.tensor_tensor(
                qrt[0][:, 0:512], pb[:, 0:512],
                rel0b[:, dc * 512:(dc + 1) * 512], ALU.add))
            for hl in range(8):
                h = dc * 8 + hl
                em("vector", lambda v, lt=lt, h=h, hl=hl: v.tensor_copy(
                    vhb[lt][:, h * 66:h * 66 + 64],
                    qrt[0][:, hl * 64:(hl + 1) * 64]))
            tgt = s_v.inc()
            em("vector", lambda v, tg=tgt: v.drain().then_inc(s_v.h, 1))
            wait("tensor", s_v, tgt)
    barrier()
    esA.close()

    if "qT" in dbg:
        for i in range(DT):
            em("sync", lambda sy, i=i: sy.dma_start(
                RAP(dbg["qT"], i * 128 * L, [[L, 128], [1, L]]), qTb[i][:]
            ).then_inc(s_st.h, 16))
            s_st.inc(16)
    if "kT" in dbg:
        for i in range(DT):
            em("sync", lambda sy, i=i: sy.dma_start(
                RAP(dbg["kT"], i * 128 * L, [[L, 128], [1, L]]), kTb[i][:]
            ).then_inc(s_st.h, 16))
            s_st.inc(16)
    if "vh" in dbg:
        for lt in range(LT):
            em("sync", lambda sy, lt=lt: sy.dma_start(
                RAP(dbg["vh"], lt * 128 * NH * 66, [[NH * 66, 128], [1, NH * 66]]),
                vhb[lt][:]).then_inc(s_st.h, 16))
            s_st.inc(16)
    if phases < 3:
        return _finish(nc, plan, dbg, (es, esB, esB2, esC))

    sqb = sbB2("sqb", [128, L], BF16)
    sgb = sbB2("sgb", [64, L], F32)
    a0w = sbB2("a0w", [32, L], BF16)
    bndt = sbB2("bndt", [128, L], BF16)
    hsb = sbB2("hsb", [64, L], BF16)
    stageQ = sbB2("stageQ", [128, L], F32)
    wqtile = sbB2("wqtile", [128, L], BF16)

    wprep_done = [0] * NH
    prep_state = {"copy_last": 0}
    mb_guard = [0, 0]
    gt_guard = [0, 0]
    hsb_guard = [0]

    def prep_pair(p):
        prep_copy_last = 0
        wait("scalar", s_gw, hsb_guard[0])
        for ic in range(NIC):
            mm_t = mm_group([(lambda: paux[0:64, 0:512],
                              lambda: relblk[:, 0:64],
                              lambda p=p, ic=ic:
                              qTb[p][:, ic * 512:(ic + 1) * 512],
                              True, True)])
            wait("scalar", s_mm, mm_t)
            tgt = s_a.inc()
            em("scalar", lambda sc, ic=ic: sc.activation(
                hsb[0:64, ic * 512:(ic + 1) * 512], paux[0:64, 0:512],
                AF.Copy).then_inc(s_a.h, 1))
            wait("tensor", s_a, tgt)
            prep_copy_last = tgt
        wait("gpsimd", s_a, prep_copy_last)
        for hh in range(2):
            s4 = (p % 2) * 2 + hh
            em("gpsimd", lambda g, hh=hh, s4=s4: g.dma_start(
                RAP(hbuf, s4 * 32 * P6, [[P6, 32], [1, L]]),
                hsb[hh * 32:(hh + 1) * 32, 0:L]).then_inc(s_gw.h, 16))
            s_gw.inc(16)
        hsb_guard[0] = s_gw.v
        for hh in range(2):
            h2 = 2 * p + hh
            s4 = (p % 2) * 2 + hh
            wait("gpsimd", s_tp, mb_guard[h2 % 2])
            em("gpsimd", lambda g, h2=h2, s4=s4: g.dma_start(
                RAP(mbuf, (h2 % 2) * 16 * 4096,
                    [[4096, 16], [128, 32], [1, 128]]),
                RAP(hbuf, s4 * 32 * P6, [[128, 16], [P6 + 1, 32], [1, 128]])
            ).then_inc(s_gw.h, 16))
            s_gw.inc(16)
            mh_t = s_gw.v
            wait("sync", s_gw, mh_t)
            for jt in range(LT):
                wait("sync", s_gw, gt_guard[jt % 2])
                em("sync", lambda sy, h2=h2, jt=jt: sy.dma_start_transpose(
                    gt[jt % 2][:], RAP(mbuf, (h2 % 2) * 16 * 4096 + jt * 4096,
                                       [[128, 32], [1, 128]])
                ).then_inc(s_tp.h, 16))
                s_tp.inc(16)
                wait("gpsimd", s_tp, s_tp.v)
                em("gpsimd", lambda g, h2=h2, jt=jt: g.dma_start(
                    RAP(wbuf, h2 * L * P7 + jt * 128 * P7 + OFFB,
                        [[P7, 128], [1, 32]]),
                    gt[jt % 2][:]).then_inc(s_gw.h, 16))
                s_gw.inc(16)
                gt_guard[jt % 2] = s_gw.v
            mb_guard[h2 % 2] = s_tp.v
            wprep_done[h2] = s_gw.v
        prep_state["copy_last"] = prep_copy_last

    prep_units = [(lambda p=p: prep_pair(p)) for p in range(NH // 2)]
    for _ in range(3):
        prep_units.pop(0)()

    wait("tensor", s_a, prep_state["copy_last"])
    em("scalar", lambda s: s.activation(qrt[0][0:32, 0:64], rl2f[:, :], AF.Square))
    wait("vector", s_a, asynC())
    em("vector", lambda v: v.reduce_sum(qrt[1][0:32, 0:1], qrt[0][0:32, 0:64], axis=AX.X))
    wait("gpsimd", s_v, vsync())
    em("gpsimd", lambda g: g.dma_start(RAP(ubuf, 0, [[1, 32], [1, 1]]),
                                       qrt[1][0:32, 0:1]).then_inc(s_gw.h, 16))
    s_gw.inc(16)
    wait("sync", s_gw, s_gw.v)
    em("sync", lambda sy: sy.dma_start(qrt[2][0:1, 0:32],
                                       RAP(ubuf, 0, [[32, 1], [1, 32]])
                                       ).then_inc(s_tp.h, 16))
    s_tp.inc(16)
    wait("vector", s_tp, s_tp.v)
    em("vector", lambda v: v.reduce_max(qrt[2][0:1, 40:41], qrt[2][0:1, 0:32], axis=AX.X))
    wait("gpsimd", s_v, vsync())
    em("gpsimd", lambda g: g.dma_start(RAP(ubuf, 32, [[2, 1], [1, 2]]),
                                       qrt[2][0:1, 40:42]).then_inc(s_gw.h, 16))
    s_gw.inc(16)

    for srcb, off in ((qTb, 0), (kTb, 8)):
        for th in range(DT):
            tgt = s_v.inc()
            em("vector", lambda v, th=th, srcb=srcb, tg=tgt: v.tensor_tensor(
                sqb[:, :], srcb[th][:], srcb[th][:], ALU.mult).then_inc(s_v.h, 1))
            wait("tensor", s_v, tgt)
            vsafe(lambda v, off=off, th=th: v.memset(
                nrm[0:2, off + th:off + th + 1], 0.0))
            for ic in range(NIC):
                mm_t = mm_group([(lambda: paux[0:2, 0:512],
                                  lambda: hhsel[:, 0:2],
                                  lambda ic=ic: sqb[:, ic * 512:(ic + 1) * 512],
                                  True, True)])
                wait("vector", s_mm, mm_t)
                vsafe(lambda v: v.reduce_max(umath[0:2, 0:1],
                                             paux[0:2, 0:512], axis=AX.X))
                tgt = vsafe(lambda v, off=off, th=th: v.tensor_tensor(
                    nrm[0:2, off + th:off + th + 1],
                    nrm[0:2, off + th:off + th + 1],
                    umath[0:2, 0:1], ALU.max))
                wait("tensor", s_v, tgt)

    wait("gpsimd", s_v, s_v.v)
    em("gpsimd", lambda g: g.dma_start(RAP(ubuf, 0, [[16, 2], [1, 16]]),
                                       nrm[0:2, 0:16]).then_inc(s_gw.h, 16))
    s_gw.inc(16)
    wait("sync", s_gw, s_gw.v)
    em("sync", lambda sy: sy.dma_start(
        unew[:, 0:33], RAP(ubuf, 0, [[0, 128], [1, 33]])).then_inc(s_tp.h, 16))
    s_tp.inc(16)
    wait("scalar", s_tp, s_tp.v)
    asafe(lambda sc: sc.activation(unew[:, 40:73], unew[:, 0:33], AF.Ln,
                                   bias=LNG[:, :]))
    asafe(lambda sc: sc.activation(unew[:, 0:33], unew[:, 40:73],
                                   AF.Exp, scale=0.5))
    wait("vector", s_a, asynC())
    for hh in range(2):
        vsafe(lambda v, hh=hh: v.tensor_scalar(
            unew[:, 40 + hh * 8:48 + hh * 8], unew[:, hh * 16 + 8:hh * 16 + 16],
            unew[:, 32:33], None, ALU.add))
        vsafe(lambda v, hh=hh: v.tensor_tensor(
            unew[:, 60 + hh * 8:68 + hh * 8], unew[:, hh * 16:hh * 16 + 8],
            unew[:, 40 + hh * 8:48 + hh * 8], ALU.mult))
        vsafe(lambda v, hh=hh: v.tensor_scalar(
            negu[:, hh * 8:(hh + 1) * 8], unew[:, 60 + hh * 8:68 + hh * 8],
            -1.0, -2.0, ALU.mult, ALU.add))
    vsync()
    if "negu" in dbg:
        wait("sync", s_v, s_v.v)
        em("sync", lambda sy: sy.dma_start(
            RAP(dbg["negu"], 0, [[16, 2], [1, 16]]), negu[0:2, 0:16]
        ).then_inc(s_st.h, 16))
        s_st.inc(16)
    if "nrm" in dbg:
        wait("sync", s_v, s_v.v)
        em("sync", lambda sy: sy.dma_start(
            RAP(dbg["nrm"], 0, [[16, 2], [1, 16]]), nrm[0:2, 0:16]
        ).then_inc(s_st.h, 16))
        s_st.inc(16)
    if "um" in dbg:
        wait("sync", s_v, s_v.v)
        em("sync", lambda sy: sy.dma_start(
            RAP(dbg["um"], 0, [[48, 2], [1, 48]]), umath[0:2, 0:48]
        ).then_inc(s_st.h, 16))
        s_st.inc(16)
    if "unew" in dbg:
        wait("sync", s_v, s_v.v)
        em("sync", lambda sy: sy.dma_start(
            RAP(dbg["unew"], 0, [[80, 2], [1, 80]]), unew[0:2, 0:80]
        ).then_inc(s_st.h, 16))
        s_st.inc(16)
    barrier()
    if phases < 4:
        return _finish(nc, plan, dbg, (es, esB, esB2, esC))

    for ht in range(HT):
        em("sync", lambda sy, ht=ht: sy.dma_start(
            b1raw[:, ht:ht + 1], RAP(b1in, ht * 128, [[1, 128], [1, 1]])
        ).then_inc(s_ld.h, 16))
        s_ld.inc(16)
    wait("scalar", s_ld, s_ld.v)
    wait("vector", s_ld, s_ld.v)
    q_chain(b1raw[:, 0:HT], HT, [(3, -6, b1qb[:, 0:HT])])
    em("sync", lambda sy: sy.dma_start(stageQ[0:1, 0:1024],
                                       RAP(b2in, 0, [[1, 1], [1, 1024]])
                                       ).then_inc(s_ld.h, 16))
    s_ld.inc(16)
    wait("scalar", s_ld, s_ld.v)
    wait("vector", s_ld, s_ld.v)
    q_chain(stageQ[0:1, 0:512], 512, [(3, -6, qrt[0][0:1, 0:512])], P=1)
    q_chain(stageQ[0:1, 512:1024], 512, [(3, -6, qrt[1][0:1, 0:512])], P=1)
    em("vector", lambda v: v.tensor_copy(relrow[0:1, 0:512], qrt[0][0:1, 0:512]))
    em("vector", lambda v: v.tensor_copy(relrow[0:1, 512:1024], qrt[1][0:1, 0:512]))
    vsync()

    def _w1_unit(ti, half):
        wait("sync", s_v, s_v.v)
        em("sync", lambda sy: sy.dma_start(
            stageQ[:, :], RAP(w1_T, ti * 128 * HID + half * 2048,
                              [[HID, 128], [1, 2048]])
        ).then_inc(s_ld.h, 16))
        s_ld.inc(16)
        wait("vector", s_ld, s_ld.v)
        for qq in range(4):
            vtg = q_chain(stageQ[:, qq * 512:(qq + 1) * 512], 512,
                          [(3, -6, wqtile[:, qq * 512:(qq + 1) * 512])])
        wait("gpsimd", s_v, vtg)
        em("gpsimd", lambda g: g.dma_start(
            RAP(w1qd, (half * 16) * (DT * 16384) + ti * 16384,
                [[128, 128], [DT * 16384, 16], [1, 128]]),
            wqtile[:, 0:2048].rearrange("p (b c) -> p b c", c=128)
        ).then_inc(s_gw.h, 16))
        s_gw.inc(16)
        wait("vector", s_gw, s_gw.v)

    def _w2_unit(ht):
        wait("sync", s_v, s_v.v)
        em("sync", lambda sy: sy.dma_start(
            stageQ[:, 0:D], RAP(w2_T, ht * 128 * D, [[D, 128], [1, D]])
        ).then_inc(s_ld.h, 16))
        s_ld.inc(16)
        wait("vector", s_ld, s_ld.v)
        for qq in range(2):
            vtg = q_chain(stageQ[:, qq * 512:(qq + 1) * 512], 512,
                          [(3, -6, wqtile[:, qq * 512:(qq + 1) * 512])])
        wait("gpsimd", s_v, vtg)
        em("gpsimd", lambda g: g.dma_start(
            RAP(w2qd, ht * 128 * D, [[D, 128], [1, D]]), wqtile[:, 0:D]
        ).then_inc(s_gw.h, 16))
        s_gw.inc(16)
        wait("vector", s_gw, s_gw.v)

    p4_units = ([(lambda ti=ti, half=half: _w1_unit(ti, half))
                 for ti in range(DT) for half in range(2)]
                + [(lambda ht=ht: _w2_unit(ht)) for ht in range(HT)])
    if phases < 5:
        for u in p4_units:
            u()
        p4_units = []
        barrier()
        return _finish(nc, plan, dbg, (es, esB, esB2, esC))

    pqk_guard = [0, 0]
    attnb_guard = [(0, 0), (0, 0)]
    wint_guard = [0, 0]
    pctx_guard = [0]
    sqb_guard = [0]
    n_units_per_head = (len(p4_units) + nheads - 1) // max(nheads, 1)
    for h in range(nheads):
        if h % 2 == 0 and prep_units:
            prep_units.pop(0)()
        for _ in range(n_units_per_head):
            if p4_units:
                p4_units.pop(0)()
        th, hh = h // 2, h % 2
        rows = slice(hh * 64, hh * 64 + 64)
        wait("sync", s_gw, wprep_done[h])

        for jt in range(LT):
            wwidth = min(160, L - jt * 128)
            wait("sync", s_mm, wint_guard[jt % 2])
            em("sync", lambda sy, h=h, jt=jt, ww=wwidth: sy.dma_start(
                wint[jt % 2][:, 0:ww],
                RAP(wbuf, h * L * P7 + jt * 128 * P7 + OFFB,
                    [[P7 - 1, 128], [1, ww]])
            ).then_inc(s_tp.h, 16))
            s_tp.inc(16)
            win_t = s_tp.v
            icmin = (jt * 128) // 512
            coff0 = jt * 128 - icmin * 512
            gmm, ggw = attnb_guard[jt % 2]
            wait("scalar", s_mm, gmm)
            wait("scalar", s_gw, ggw)
            last_exp = 0
            for ic in range(icmin, NIC):
                pb = pqk[ic % 2]
                coff = coff0 if ic == icmin else 0
                haswin = (ic == icmin) or (
                    ic == icmin + 1 and wwidth > 512 - coff0)
                items = [(lambda pb=pb, coff=coff: pb[:, coff:512],
                          lambda th=th, rows=rows, jt=jt:
                          kTb[th][rows, jt * 128:(jt + 1) * 128],
                          lambda th=th, rows=rows, ic=ic, coff=coff:
                          qTb[th][rows, ic * 512 + coff:(ic + 1) * 512],
                          True, not haswin)]
                if ic == icmin:
                    wait("tensor", s_tp, win_t)
                    w1w = min(wwidth, 512 - coff0)
                    items.append((lambda pb=pb, coff=coff, w1w=w1w:
                                  pb[:, coff:coff + w1w],
                                  lambda: identb[:, :],
                                  lambda jt=jt, w1w=w1w: wint[jt % 2][:, 0:w1w],
                                  False, True))
                elif haswin:
                    w1w = 512 - coff0
                    items.append((lambda pb=pb, w2w=wwidth - w1w: pb[:, 0:w2w],
                                  lambda: identb[:, :],
                                  lambda jt=jt, w1w=w1w, ww=wwidth:
                                  wint[jt % 2][:, w1w:ww],
                                  False, True))
                wait("tensor", s_a, pqk_guard[ic % 2])
                mm_t = mm_group(items)
                if haswin:
                    wint_guard[jt % 2] = mm_t
                wait("scalar", s_mm, mm_t)
                tgt = s_a.inc()
                em("scalar", lambda s, jt=jt, ic=ic, coff=coff, pb=pb, h=h, tg=tgt:
                   s.activation(attnb[jt % 2][:, ic * 512 + coff:(ic + 1) * 512],
                                pb[:, coff:512], AF.Exp,
                                bias=negu[:, (h % 2) * 8 + h // 2:
                                          (h % 2) * 8 + h // 2 + 1]).then_inc(s_a.h, 1))
                pqk_guard[ic % 2] = tgt
                last_exp = tgt
            wait("gpsimd", s_a, last_exp)
            em("gpsimd", lambda g, jt=jt, ww=wwidth: g.dma_start(
                RAP(dbuf, jt * 128 * SBD + OFFB, [[SBD - 1, 128], [1, ww]]),
                attnb[jt % 2][:, jt * 128:jt * 128 + ww]
            ).then_inc(s_gw.h, 16))
            s_gw.inc(16)
            dbuf_t = s_gw.v
            wait("tensor", s_a, last_exp)
            if jt == 0:
                wait("tensor", s_a, pctx_guard[0])
            for ic in range(icmin, NIC):
                coff = coff0 if ic == icmin else 0
                mm_group([(lambda ic=ic, coff=coff: pctx[ic][0:65, coff:512],
                           lambda jt=jt, h=h: vhb[jt][:, h * 66:h * 66 + 65],
                           lambda jt=jt, ic=ic, coff=coff:
                           attnb[jt % 2][:, ic * 512 + coff:(ic + 1) * 512],
                           jt == 0, False)], inc=(ic == NIC - 1))
            attnb_guard[jt % 2] = (s_mm.v, dbuf_t)

        wait("gpsimd", s_gw, s_gw.v)
        em("gpsimd", lambda g: g.dma_start(
            RAP(cbuf, 0, [[128, L], [1, 32]]),
            RAP(dbuf, OFFB, [[SBD, L], [1, 32]])).then_inc(s_gw.h, 16))
        s_gw.inc(16)
        wait("sync", s_gw, s_gw.v)
        em("sync", lambda sy: sy.dma_start_transpose(
            bndt[:], RAP(cbuf, 0, [[128, L], [1, 128]])).then_inc(s_tp.h, 16))
        s_tp.inc(16)
        wait("gpsimd", s_tp, s_tp.v)
        em("gpsimd", lambda g: g.dma_start(
            RAP(btbuf, 0, [[P5, 32], [1, L]]), bndt[0:32, :]
        ).then_inc(s_gw.h, 16))
        s_gw.inc(16)
        wait("sync", s_gw, s_gw.v)
        em("sync", lambda sy: sy.dma_start(
            a0w[:, :], RAP(btbuf, 0, [[P5 - 1, 32], [1, L]])
        ).then_inc(s_tp.h, 16))
        s_tp.inc(16)
        wait("tensor", s_tp, s_tp.v)
        wait("tensor", s_mm, s_mm.v)
        for ic in range(NIC):
            if not skip_ctxrel:
                mm_group([(lambda ic=ic: pctx[ic][0:64, 0:512],
                           lambda: rl2[:, :],
                           lambda ic=ic: a0w[:, ic * 512:(ic + 1) * 512],
                           False, False)], inc=False)
            mm_group([(lambda ic=ic: pctx[ic][0:65, 0:1],
                       lambda: zt[:, 160:225],
                       lambda: zt[:, 0:1],
                       False, True)], inc=(ic == NIC - 1))
        mm_ctx = s_mm.v
        wait("scalar", s_mm, mm_ctx)
        wait("scalar", s_v, sqb_guard[0])
        for ic in range(NIC):
            em("scalar", lambda s, ic=ic: s.activation(
                sgb[0:1, ic * 512:ic * 512 + 512], pctx[ic][64:65, 0:512],
                AF.Copy))
        ev_t = 0
        for ic in range(NIC):
            if ic == NIC - 1:
                ev_t = s_a.inc()
                em("scalar", lambda s, ic=ic: s.activation(
                    sqb[0:64, ic * 512:ic * 512 + 512], pctx[ic][0:64, 0:512],
                    AF.Copy).then_inc(s_a.h, 1))
            else:
                em("scalar", lambda s, ic=ic: s.activation(
                    sqb[0:64, ic * 512:ic * 512 + 512], pctx[ic][0:64, 0:512],
                    AF.Copy))
        pctx_guard[0] = ev_t
        for ic in range(NIC):
            em("scalar", lambda s, ic=ic: s.activation(
                qrt[1][0:1, 0:512], sgb[0:1, ic * 512:ic * 512 + 512], AF.Ln,
                bias=LNG[0:1, :]))
            buf = qrt[0] if ic % 2 == 0 else qrt[2]
            tgt = s_a.inc()
            em("scalar", lambda s, b=buf, tg=tgt: s.activation(
                b[0:1, 0:512], qrt[1][0:1, 0:512], AF.Exp,
                scale=-1.0).then_inc(s_a.h, 1))
            wait("gpsimd", s_a, tgt)
            gtg = s_gw.inc(16)
            em("gpsimd", lambda g, ic=ic, b=buf: g.dma_start(
                RAP(sgbuf, ic * 512, [[1, 1], [1, 512]]),
                b[0:1, 0:512]).then_inc(s_gw.h, 16))
            if ic >= 1:
                wait("scalar", s_gw, gtg - 16)
        wait("sync", s_gw, s_gw.v)
        em("sync", lambda sy: sy.dma_start(
            sgb[0:64, :], RAP(sgbuf, 0, [[0, 64], [1, L]])
        ).then_inc(s_tp.h, 16))
        s_tp.inc(16)
        wait("vector", s_tp, s_tp.v)
        for ic in range(NIC):
            tgt = s_v.inc()
            em("vector", lambda v, ic=ic, tg=tgt: v.tensor_tensor(
                ctxe[0:64, 0:512], sqb[0:64, ic * 512:(ic + 1) * 512],
                sgb[0:64, ic * 512:(ic + 1) * 512], ALU.mult).then_inc(s_v.h, 1))
            wait("gpsimd", s_v, tgt)
            gtg = s_gw.inc(16)
            em("gpsimd", lambda g, h=h, ic=ic, tg=gtg: g.dma_start(
                RAP(ctxd, (h // 2) * 128 * L + (h % 2) * 64 * L + ic * 512,
                    [[L, 64], [1, 512]]), ctxe[0:64, 0:512]).then_inc(s_gw.h, 16))
            wait("vector", s_gw, gtg)
        sqb_guard[0] = s_v.v

    for u in p4_units:
        u()
    p4_units = []
    barrier()
    esB.close()
    esB2.close()
    if "ctxT" in dbg:
        tmpc = sbC("tmpc", [128, L], BF16)
        for i in range(DT):
            em("sync", lambda sy, i=i, tmpc=tmpc: sy.dma_start(
                tmpc[:], RAP(ctxd, i * 128 * L, [[L, 128], [1, L]])
            ).then_inc(s_tp.h, 16))
            s_tp.inc(16)
            wait("sync", s_tp, s_tp.v)
            em("sync", lambda sy, i=i, tmpc=tmpc: sy.dma_start(
                RAP(dbg["ctxT"], i * 128 * L, [[L, 128], [1, L]]), tmpc[:]
            ).then_inc(s_st.h, 16))
            s_st.inc(16)
            wait("sync", s_st, s_st.v)
    if phases < 6:
        return _finish(nc, plan, dbg, (es, esC))

    wbfC = [sbC(f"wbfC{i}", [128, D], BF16) for i in range(DT)]
    stageC = sbC("stageC", [128, D], F32)
    strm = [sbC(f"strm{i}", [128, D], F32) for i in range(2)]
    maskt = [sbC(f"maskt{i}", [128, D], BF16) for i in range(2)]
    lnt = sbC("lnt", [128, D], F32)
    ytile = sbC("ytile", [128, D], F32)
    xqtile = sbC("xqtile", [128, D], BF16)
    hqTb = [sbC(f"hqTb{i}", [128, 512], BF16) for i in range(HT)]
    w1qt = [sbC(f"w1qt{i}", [128, 1024], BF16) for i in range(2)]
    w2qs = [sbC(f"w2qs{i}", [128, 512], BF16) for i in range(2)]
    ysb = [sbC(f"ysb{i}", [128, D], F32) for i in range(4)]
    b2qb = sbC("b2qb", [128, D], F32)
    ctxt8 = [sbC(f"ctxt8_{i}", [128, D], BF16) for i in range(2)]
    ytil2 = [ytile, sbC("ytile2", [128, D], F32)]
    xqt2 = [xqtile, sbC("xqtile2", [128, D], BF16)]

    for dc in range(2):
        mm_t = mm_group([(lambda: paux2[0:128, 0:512],
                          lambda: onesrow[0:1, 0:128],
                          lambda dc=dc: relrow[0:1, dc * 512:(dc + 1) * 512],
                          True, True)])
        wait("vector", s_mm, mm_t)
        tgt = s_v.inc()
        em("vector", lambda v, dc=dc, tg=tgt: v.tensor_copy(
            b2qb[:, dc * 512:(dc + 1) * 512], paux2[0:128, 0:512]
        ).then_inc(s_v.h, 1))
        wait("tensor", s_v, tgt)

    load_weight_bf16(wo_T, 1.0, wbfC, stageC)
    barrier()

    def ln_tile(src_ap, dst_ap):
        vsafe(lambda v: v.reduce_sum(col[:, 0:1], src_ap, axis=AX.X))
        vsafe(lambda v: v.tensor_scalar(col[:, 0:1], col[:, 0:1],
                                        consts[:, 8:9], None, ALU.mult))
        wait("scalar", s_v, vsync())
        asafe(lambda sc: sc.activation(ysb[0][:, 0:D], src_ap, AF.Square,
                                       accum_out=col[:, 1:2]))
        wait("vector", s_a, asynC())
        vsafe(lambda v: v.tensor_scalar(col[:, 1:2], col[:, 1:2],
                                        consts[:, 8:9], None, ALU.mult))
        vsafe(lambda v: v.tensor_tensor(col[:, 2:3], col[:, 0:1],
                                        col[:, 0:1], ALU.mult))
        vsafe(lambda v: v.tensor_tensor(col[:, 3:4], col[:, 1:2],
                                        col[:, 2:3], ALU.subtract))
        wait("scalar", s_v, vsync())
        asafe(lambda sc: sc.activation(col[:, 4:5], col[:, 3:4], AF.Ln,
                                       bias=consts[:, 7:8]))
        asafe(lambda sc: sc.activation(col[:, 4:5], col[:, 4:5], AF.Exp,
                                       scale=-0.5))
        wait("vector", s_a, asynC())
        vsafe(lambda v: v.scalar_tensor_tensor(
            col[:, 5:6], col[:, 0:1], -1.0, col[:, 4:5], ALU.mult, ALU.mult))
        tgt = s_v.inc()
        em("vector", lambda v, d=dst_ap, x=src_ap: v.tensor_scalar(
            d, x, col[:, 4:5], col[:, 5:6], ALU.mult, ALU.add).then_inc(s_v.h, 1))
        return tgt

    ctx_mm_last = {}
    tmm_tgt = {}
    cpy_tgt = {}
    res_tgt = {}
    qch_tgt = {}
    x1st_tgt = {}

    def p6_transposes(lt):
        if lt - 2 in cpy_tgt:
            wait("tensor", s_a, cpy_tgt[lt - 2])
        wait("tensor", s_v, qch_tgt[lt])
        items = [(lambda lt=lt, ti=ti: pctx[(lt % 2) * 2 + ti // 4]
                  [0:128, (ti % 4) * 128:(ti % 4) * 128 + 128],
                  lambda lt=lt, ti=ti: xqt2[lt % 2][:, ti * 128:(ti + 1) * 128],
                  lambda: identb[:, :], True, True) for ti in range(DT)]
        tmm_tgt[lt] = mm_group(items)
        wait("scalar", s_mm, tmm_tgt[lt])
        tgt = 0
        for ti in range(DT):
            if ti == DT - 1:
                tgt = s_a.inc()
                em("scalar", lambda sc, lt=lt, ti=ti: sc.activation(
                    xqTb[ti][:, lt * 128:(lt + 1) * 128],
                    pctx[(lt % 2) * 2 + ti // 4]
                    [0:128, (ti % 4) * 128:(ti % 4) * 128 + 128],
                    AF.Copy).then_inc(s_a.h, 1))
            else:
                em("scalar", lambda sc, lt=lt, ti=ti: sc.activation(
                    xqTb[ti][:, lt * 128:(lt + 1) * 128],
                    pctx[(lt % 2) * 2 + ti // 4]
                    [0:128, (ti % 4) * 128:(ti % 4) * 128 + 128],
                    AF.Copy))
        cpy_tgt[lt] = tgt

    for lt in range(LT):
        if lt - 2 in ctx_mm_last:
            wait("sync", s_mm, ctx_mm_last[lt - 2])
        em("sync", lambda sy, lt=lt: sy.dma_start(
            ctxt8[lt % 2][:, :].rearrange("p (t c) -> p t c", c=128),
            RAP(ctxd, lt * 128, [[L, 128], [128 * L, DT], [1, 128]])
        ).then_inc(s_tp.h, 16))
        s_tp.inc(16)
        ctx_t = s_tp.v
        if (lt - 2, 1) in res_tgt:
            wait("sync", s_v, res_tgt[(lt - 2, 1)])
        em("sync", lambda sy, lt=lt: sy.dma_start(
            strm[lt % 2][:, :], RAP(x_nat, lt * 128 * D, [[D, 128], [1, D]])
        ).then_inc(s_ld.h, 16))
        s_ld.inc(16)
        x0_t = s_ld.v
        wait("tensor", s_tp, ctx_t)
        for dc in range(2):
            pb = paux if dc == 0 else paux2
            if (lt - 1, dc) in res_tgt:
                wait("tensor", s_v, res_tgt[(lt - 1, dc)])
            mm_t = mm_group([(lambda pb=pb: pb[:, 0:512],
                              lambda lt=lt, ti=ti:
                              ctxt8[lt % 2][:, ti * 128:(ti + 1) * 128],
                              lambda ti=ti, dc=dc:
                              wbfC[ti][:, dc * 512:(dc + 1) * 512],
                              ti == 0, ti == DT - 1) for ti in range(DT)])
            if dc == 1:
                ctx_mm_last[lt] = mm_t
            wait("vector", s_mm, mm_t)
            if dc == 0:
                wait("vector", s_ld, x0_t)
            tgt = s_v.inc()
            em("vector", lambda v, lt=lt, dc=dc, pb=pb, tg=tgt: v.tensor_tensor(
                lnt[:, dc * 512:(dc + 1) * 512], pb[:, 0:512],
                strm[lt % 2][:, dc * 512:(dc + 1) * 512], ALU.add
            ).then_inc(s_v.h, 1))
            res_tgt[(lt, dc)] = tgt
        if lt > 0:
            p6_transposes(lt - 1)
        if lt - 2 in x1st_tgt:
            wait("vector", s_gw, x1st_tgt[lt - 2])
        ln_t = ln_tile(lnt[:, 0:D], ytil2[lt % 2][:, 0:D])
        wait("gpsimd", s_v, ln_t)
        em("gpsimd", lambda g, lt=lt: g.dma_start(
            RAP(x1d, lt * 128 * D, [[D, 128], [1, D]]), ytil2[lt % 2][:, 0:D]
        ).then_inc(s_gw.h, 16))
        s_gw.inc(16)
        x1st_tgt[lt] = s_gw.v
        if lt - 2 in tmm_tgt:
            wait("vector", s_mm, tmm_tgt[lt - 2])
        for qq in range(2):
            vtg = q_chain(ytil2[lt % 2][:, qq * 512:(qq + 1) * 512], 512,
                          [(3, -6, xqt2[lt % 2][:, qq * 512:(qq + 1) * 512])])
        qch_tgt[lt] = vtg
        if "x1" in dbg:
            wait("sync", s_v, vtg)
            em("sync", lambda sy, lt=lt: sy.dma_start(
                RAP(dbg["x1"], lt * 128 * D, [[D, 128], [1, D]]),
                ytil2[lt % 2][:, 0:D]).then_inc(s_st.h, 16))
            s_st.inc(16)
            wait("sync", s_st, s_st.v)
    p6_transposes(LT - 1)
    barrier()
    if phases < 7:
        return _finish(nc, plan, dbg, (es, esC))

    mmA2 = []
    mmB2 = []
    ev2 = []
    st2 = []
    for lc in range(NIC):
        for ht in range(HT):
            if len(mmA2) >= 2:
                wait("sync", s_mm, mmA2[-2])
            em("sync", lambda sy, ht=ht: sy.dma_start(
                w1qt[ht % 2][:, :], RAP(w1qd, ht * DT * 16384,
                                        [[128, 128], [16384, DT], [1, 128]])
            ).then_inc(s_tp.h, 16))
            s_tp.inc(16)
            wait("tensor", s_tp, s_tp.v)
            items = [(lambda: paux[:, 0:512],
                      lambda ht=ht, ti=ti: w1qt[ht % 2][:, ti * 128:(ti + 1) * 128],
                      lambda ti=ti, lc=lc: xqTb[ti][:, lc * 512:(lc + 1) * 512],
                      ti == 0, ti == DT - 1) for ti in range(DT)]
            mm_t = mm_group(items)
            mmA2.append(mm_t)
            wait("vector", s_mm, mm_t)
            tgt = s_v.inc()
            em("vector", lambda v, ht=ht, tg=tgt: v.tensor_scalar(
                qrt[0][:, 0:512], paux[:, 0:512], b1qb[:, ht:ht + 1], 0.0,
                ALU.add, ALU.max).then_inc(s_v.h, 1))
            wait("tensor", s_v, tgt)
            wait("scalar", s_v, tgt)
            vtg = q_chain(qrt[0][:, 0:512], 512,
                          [(10, -14, qrt[1][:, 0:512]),
                           (3, -6, hqTb[ht][:, 0:512])])
            if "hq0" in dbg and lc == 0 and ht == 0:
                wait("sync", s_v, vtg)
                em("sync", lambda sy: sy.dma_start(
                    dbg["hq0"].ap(), hqTb[0][:, 0:512]).then_inc(s_st.h, 16))
                s_st.inc(16)
        wait("tensor", s_v, s_v.v)
        wait("vector", s_a, s_a.v)
        for dc in range(2):
            for ht in range(HT):
                if len(mmB2) >= 2:
                    wait("sync", s_mm, mmB2[-2])
                em("sync", lambda sy, ht=ht, dc=dc: sy.dma_start(
                    w2qs[ht % 2][:, :], RAP(w2qd, ht * 128 * D + dc * 512,
                                            [[D, 128], [1, 512]])
                ).then_inc(s_tp.h, 16))
                s_tp.inc(16)
                wait("tensor", s_tp, s_tp.v)
                for lt2 in range(4):
                    mm_group([(lambda lt2=lt2: pctx[lt2][:, 0:512],
                               lambda ht=ht, lt2=lt2:
                               hqTb[ht][:, lt2 * 128:(lt2 + 1) * 128],
                               lambda ht=ht: w2qs[ht % 2][:, :],
                               ht == 0, ht == HT - 1)],
                              inc=(lt2 == 3))
                mmB2.append(s_mm.v)
            mmB = s_mm.v
            wait("vector", s_mm, mmB)
            for lt2 in range(4):
                tgt = s_v.inc()
                em("vector", lambda v, lt2=lt2, dc=dc, tg=tgt: v.tensor_tensor(
                    ysb[lt2][:, dc * 512:(dc + 1) * 512], pctx[lt2][:, 0:512],
                    b2qb[:, dc * 512:(dc + 1) * 512], ALU.add).then_inc(s_v.h, 1))
            wait("tensor", s_v, s_v.v)
        wait("scalar", s_v, s_v.v)
        wait("vector", s_v, s_v.v)
        for lt2 in range(4):
            lt = lc * 4 + lt2
            if len(ev2) >= 2:
                wait("sync", s_v, ev2[-2])
            em("sync", lambda sy, lt=lt: sy.dma_start(
                maskt[lt % 2][:, :],
                RAP(maskin, lt * 128 * D, [[D, 128], [1, D]])
            ).then_inc(s_ld.h, 16))
            s_ld.inc(16)
            em("sync", lambda sy, lt=lt: sy.dma_start(
                strm[lt % 2][:, :], RAP(x1d, lt * 128 * D, [[D, 128], [1, D]])
            ).then_inc(s_ld.h, 16))
            s_ld.inc(16)
            ld_t = s_ld.v
            if len(st2) >= 2:
                wait("vector", s_st, st2[-2])
            for qq in range(2):
                vtg = q_chain(ysb[lt2][:, qq * 512:(qq + 1) * 512], 512,
                              [(10, -14, ytil2[lt % 2][:, qq * 512:(qq + 1) * 512])])
            wait("vector", s_ld, ld_t)
            em("vector", lambda v, lt=lt: v.scalar_tensor_tensor(
                ytil2[lt % 2][:, 0:D], ytil2[lt % 2][:, 0:D], consts[:, 6:7],
                maskt[lt % 2][:, :], ALU.mult, ALU.mult))
            tgt = s_v.inc()
            em("vector", lambda v, lt=lt, tg=tgt: v.tensor_tensor(
                lnt[:, 0:D], ytil2[lt % 2][:, 0:D], strm[lt % 2][:, :], ALU.add
            ).then_inc(s_v.h, 1))
            ev2.append(tgt)
            wait("scalar", s_v, tgt)
            ln_t = ln_tile(lnt[:, 0:D], ytil2[lt % 2][:, 0:D])
            wait("sync", s_v, ln_t)
            em("sync", lambda sy, lt=lt: sy.dma_start(
                RAP(out, lt * 128 * D, [[D, 128], [1, D]]), ytil2[lt % 2][:, 0:D]
            ).then_inc(s_st.h, 16))
            s_st.inc(16)
            st2.append(s_st.v)

    st_all = s_st.v
    for eng in ENGS:
        wait(eng, s_st, st_all)
    return _finish(nc, plan, dbg, (es, esC))


def _finish(nc, plan, dbg, stacks):
    with nc.Block() as block:
        def run_for(eng_name):
            def body(eng):
                for e, fn in plan:
                    if e == eng_name:
                        fn(eng)
            return body
        block.sync(run_for("sync"))
        block.gpsimd(run_for("gpsimd"))
        block.tensor(run_for("tensor"))
        block.vector(run_for("vector"))
        block.scalar(run_for("scalar"))
    for st in stacks:
        try:
            st.close()
        except Exception:
            pass
    return nc, dbg


def _host_constants():
    t = np.arange(L, dtype=np.float64)
    inv_freq = 1.0 / (10000.0 ** (np.arange(0, 64, 2, dtype=np.float64) / 64.0))
    ang = np.outer(t, inv_freq)
    cos = np.cos(ang)
    sin = np.sin(ang)
    p = np.arange(128)
    m = (p % 64) // 2
    c2 = cos[:, m].T.astype(np.float32)
    s2 = sin[:, m].T.astype(np.float32)
    s2 = np.where((p % 2 == 0)[:, None], -s2, s2).astype(np.float32)
    ident = np.eye(128, dtype=ml_dtypes.bfloat16)
    return np.ascontiguousarray(c2), np.ascontiguousarray(s2), ident


def _threefry2x32(k0, k1, x0, x1):
    k0 = np.uint32(k0); k1 = np.uint32(k1)
    k2 = np.uint32(k0 ^ k1 ^ np.uint32(0x1BD11BDA))
    ks = [k0, k1, k2]
    R = [[13, 15, 26, 6], [17, 29, 16, 24]]
    x0 = (x0 + ks[0]).astype(np.uint32)
    x1 = (x1 + ks[1]).astype(np.uint32)
    for i in range(5):
        for r in (R[0] if i % 2 == 0 else R[1]):
            x0 = (x0 + x1).astype(np.uint32)
            x1 = ((x1 << np.uint32(r)) | (x1 >> np.uint32(32 - r))).astype(np.uint32)
            x1 = (x0 ^ x1).astype(np.uint32)
        x0 = (x0 + ks[(i + 1) % 3]).astype(np.uint32)
        x1 = (x1 + ks[(i + 2) % 3] + np.uint32(i + 1)).astype(np.uint32)
    return x0, x1


def _dropout_mask():
    n = 8 * L * D
    iota = np.arange(n, dtype=np.uint64)
    c1 = (iota >> np.uint64(32)).astype(np.uint32)
    c2 = (iota & np.uint64(0xFFFFFFFF)).astype(np.uint32)
    b1, b2 = _threefry2x32(0, 42, c1, c2)
    if os.environ.get("BASS_SIM_FP64") == "1":
        bits = (b1.astype(np.uint64) << np.uint64(32)) | b2.astype(np.uint64)
        u = ((bits >> np.uint64(12)) | np.uint64(0x3FF0000000000000)
             ).view(np.float64) - 1.0
        keep = u < 0.8
    else:
        bits = (b1 ^ b2).astype(np.uint32)
        u = ((bits >> np.uint32(9)) | np.uint32(0x3F800000)
             ).view(np.float32) - np.float32(1.0)
        keep = u < np.float32(0.8)
    return keep.reshape(8, L, D).astype(ml_dtypes.bfloat16)


_CACHE = {}


def kernel(L0_attn_x, L0_attn_w_q, L0_attn_w_k, L0_attn_w_v, L0_attn_w_o,
           L0_attn_rel_emb, L0_ffn_w1, L0_ffn_b1, L0_ffn_w2, L0_ffn_b2):
    if "nc" not in _CACHE:
        _CACHE["nc"], _ = build_core_program()
    nc = _CACHE["nc"]

    c2, s2, ident = _host_constants()
    maskf = _dropout_mask()

    x = np.ascontiguousarray(np.asarray(L0_attn_x, dtype=np.float32))
    wq = np.ascontiguousarray(np.asarray(L0_attn_w_q, dtype=np.float32).T)
    wk = np.ascontiguousarray(np.asarray(L0_attn_w_k, dtype=np.float32).T)
    wv = np.ascontiguousarray(np.asarray(L0_attn_w_v, dtype=np.float32).T)
    wo = np.ascontiguousarray(np.asarray(L0_attn_w_o, dtype=np.float32).T)
    w1 = np.ascontiguousarray(np.asarray(L0_ffn_w1, dtype=np.float32).T)
    w2 = np.ascontiguousarray(np.asarray(L0_ffn_w2, dtype=np.float32).T)
    b1 = np.ascontiguousarray(np.asarray(L0_ffn_b1, dtype=np.float32)[None, :])
    b2 = np.ascontiguousarray(np.asarray(L0_ffn_b2, dtype=np.float32)[None, :])
    rel = np.asarray(L0_attn_rel_emb, dtype=np.float32)
    idx = [32 - w for w in range(32)]
    rel_Tw = np.ascontiguousarray(
        np.concatenate([rel[idx, :64].T, rel[0:1, :64].T], axis=1))
    rel_wrev = np.ascontiguousarray(
        np.concatenate([rel[idx, :64], rel[0:1, :64]], axis=0))

    in_maps = []
    for b in range(8):
        xb = np.ascontiguousarray(x[b])
        in_maps.append({
            "x_nat": xb,
            "x_T": np.ascontiguousarray(xb.T),
            "wq_T": wq, "wk_T": wk, "wv_T": wv, "wo_T": wo,
            "w1_T": w1, "w2_T": w2, "b1": b1, "b2": b2,
            "rel_Tw": rel_Tw, "rel_wrev": rel_wrev,
            "rel0r": np.ascontiguousarray(rel[0:1, :64]),
            "maskin": np.ascontiguousarray(maskf[b]),
            "c2": c2, "s2": s2, "ident": ident,
        })
    res = run_bass_kernel_spmd(nc, in_maps, core_ids=list(range(8)))
    return np.stack([r["out"].reshape(L, D) for r in res.results], axis=0)


# version 07
# speedup vs baseline: 2.7857x; 1.0288x over previous
import os
import math
import contextlib
import numpy as np
import ml_dtypes
import concourse.bass as bass
import concourse.mybir as mybir
from bass_rust import AP as RAP
from concourse.bass_utils import run_bass_kernel_spmd

F32 = mybir.dt.float32
BF16 = mybir.dt.bfloat16
I32 = mybir.dt.int32
_EXACT_LADDER = (os.environ.get("BASS_SIM_FP64") == "1"
                 or os.environ.get("BASS_DEV_LADDER") == "1")
AF = mybir.ActivationFunctionType
ALU = mybir.AluOpType
AX = mybir.AxisListType

L = 2048
D = 1024
NH = 16
HID = 4096
LT = L // 128
DT = D // 128
HT = HID // 128
NIC = L // 512

P6 = L + 64
P7 = 320
SBD = 320
P5 = L + 64
OFFB = 127

LN2 = float(np.log(np.float64(2.0)))
C2LN2 = float(1.0 / (2.0 * np.log(np.float64(2.0))))
NEG1E18 = -1.0e18
EPS = 1e-5
ENGS = ("sync", "gpsimd", "tensor", "vector", "scalar")


class Sem:
    def __init__(self, h):
        self.h = h
        self.v = 0

    def inc(self, n=1):
        self.v += n
        return self.v


def build_core_program(debug=(), nheads=NH, phases=99, skip_ctxrel=False):
    nc = bass.Bass("TRN2", target_bir_lowering=False, detect_race_conditions=False)

    din = lambda name, shape, dt: nc.dram_tensor(name, shape, dt, kind="ExternalInput")
    x_nat = din("x_nat", [L, D], F32)
    x_T = din("x_T", [D, L], F32)
    wq_T = din("wq_T", [D, D], F32)
    wk_T = din("wk_T", [D, D], F32)
    wv_T = din("wv_T", [D, D], F32)
    wo_T = din("wo_T", [D, D], F32)
    w1_T = din("w1_T", [D, HID], F32)
    w2_T = din("w2_T", [HID, D], F32)
    b1in = din("b1", [1, HID], F32)
    b2in = din("b2", [1, D], F32)
    rel_Tw = din("rel_Tw", [64, 33], F32)
    rel_wrev = din("rel_wrev", [33, 64], F32)
    rel0r_in = din("rel0r", [1, 64], F32)
    maskin = din("maskin", [L, D], BF16)
    c2in = din("c2", [128, L], F32)
    s2in = din("s2", [128, L], F32)
    identin = din("ident", [128, 128], BF16)

    out = nc.dram_tensor("out", [L, D], F32, kind="ExternalOutput")
    dbg = {}
    def ddbg(name, shape, dt):
        if name in debug:
            dbg[name] = nc.dram_tensor("dbg_" + name, shape, dt, kind="ExternalOutput")
    ddbg("qT", [D, L], BF16)
    ddbg("kT", [D, L], BF16)
    ddbg("vh", [L, NH * 66], BF16)
    ddbg("negu", [2, NH], F32)
    ddbg("nrm", [2, 16], F32)
    ddbg("um", [2, 48], F32)
    ddbg("unew", [2, 80], F32)
    ddbg("ctxT", [D, L], BF16)
    ddbg("sg", [1, L], F32)
    ddbg("a0", [32, L], BF16)
    ddbg("x1", [L, D], F32)
    ddbg("hq0", [128, 512], BF16)
    ddbg("qd", [4, 512], F32)

    dram = lambda name, n, dt: nc.dram_tensor(name, [n], dt, kind="Internal")
    hbuf = dram("hbuf", 4 * 32 * P6, BF16)
    mbuf = dram("mbuf", 2 * 16 * 32 * 128, BF16)
    wbuf = dram("wbuf", NH * L * P7, BF16)
    dbuf = dram("dbuf", L * SBD, BF16)
    cbuf = dram("cbuf", L * 128, BF16)
    btbuf = dram("btbuf", 32 * P5, BF16)
    relbuf = dram("relbuf", 64 * 32, BF16)
    ubuf = dram("ubuf", 64, F32)
    sgbuf = dram("sgbuf", L, F32)
    ctxd = dram("ctxd", D * L, BF16)
    w1qd = dram("w1qd", D * HID, BF16)
    w2qd = dram("w2qd", HID * D, BF16)
    x1d = dram("x1d", L * D, F32)
    xqd = dram("xqd", L * D, BF16)

    es = contextlib.ExitStack()
    esA = contextlib.ExitStack()
    esB = contextlib.ExitStack()
    esB2 = contextlib.ExitStack()
    esC = contextlib.ExitStack()

    def mk(stk, side):
        def f(name, shape, dt):
            return stk.enter_context(nc.sbuf_tensor(name, shape, dt, side=side))
        return f
    sbG = mk(es, "left")
    sbA = mk(esA, "right")
    sbB = mk(esB, "left")
    sbB2 = mk(esB2, "right")
    sbC = mk(esC, "right")
    psum = lambda name: es.enter_context(nc.psum_tensor(name, [128, 512], F32))
    sem = lambda name: Sem(es.enter_context(nc.semaphore(name)))

    xqTb = [sbG(f"xqTb{i}", [128, L], BF16) for i in range(DT)]
    vhb = [sbG(f"vhb{i}", [128, NH * 66], BF16) for i in range(LT)]
    scr = {k: sbG("scr_" + k, [128, 512], F32) for k in ("a", "b", "c", "s", "u", "w")}
    scr["i"] = sbG("scr_i", [128, 512], I32)
    scr["j"] = sbG("scr_j", [128, 512], I32)
    qrt = [sbG(f"qrt{i}", [128, 512], F32) for i in range(3)]
    ctxe = sbG("ctxe", [64, 512], BF16)
    identb = sbG("identb", [128, 128], BF16)
    relblk = sbG("relblk", [128, 64], BF16)
    rl2 = sbG("rl2", [32, 64], BF16)
    rl2f = sbG("rl2f", [32, 64], F32)
    relrow = sbG("relrow", [1, D], BF16)
    onesrow = sbG("onesrow", [1, 128], BF16)
    onesf = sbG("onesf", [1, 128], F32)
    hhsel = sbG("hhsel", [128, 2], BF16)
    consts = sbG("consts", [128, 12], F32)
    nrm = sbG("nrm", [2, 16], F32)
    umath = sbG("umath", [2, 48], F32)
    bmcol = sbG("bmcol", [2, 2], F32)
    negu = sbG("negu", [128, NH], F32)
    unew = sbG("unew", [128, 80], F32)
    col = sbG("col", [128, 8], F32)
    b1qb = sbG("b1qb", [128, HT], F32)
    b1raw = sbG("b1raw", [128, HT], F32)
    gt = [sbG(f"gt{i}", [128, 32], BF16) for i in range(2)]
    wint = [sbG(f"wint{i}", [128, 160], BF16) for i in range(2)]
    ctxt = [sbG(f"ctxt{i}", [128, 128], BF16) for i in range(2)]
    zt = sbG("zt", [128, P7], BF16)

    c2b = sbA("c2b", [128, L], F32)
    s2b = sbA("s2b", [128, L], F32)
    stageA = sbA("stageA", [128, D], F32)
    wbfA = [sbA(f"wbfA{i}", [128, D], BF16) for i in range(DT)]
    relw = sbA("relw", [64, 33], F32)
    relv = sbA("relv", [33, 64], F32)
    rel0b = sbA("rel0b", [128, D], F32)
    rel0r = sbA("rel0r_t", [1, 64], F32)

    qTb = [sbB(f"qTb{i}", [128, L], BF16) for i in range(DT)]
    kTb = [sbB(f"kTb{i}", [128, L], BF16) for i in range(DT)]
    attnb = [sbB(f"attnb{i}", [128, L], BF16) for i in range(2)]


    pqk = [psum("pqk0"), psum("pqk1")]
    pctx = [psum(f"pctx{i}") for i in range(4)]
    paux = psum("paux")
    paux2 = psum("paux2")

    s_ld = sem("s_ld")
    s_gw = sem("s_gw")
    s_tp = sem("s_tp")
    s_mm = sem("s_mm")
    s_v = sem("s_v")
    s_a = sem("s_a")
    s_st = sem("s_st")
    s_zz = sem("s_zz")
    s_bar = sem("s_bar")

    plan = []
    em = lambda eng, fn: plan.append((eng, fn))
    def wait(eng, s, val):
        em(eng, lambda e, ss=s.h, vv=val: e.wait_ge(ss, vv))

    nbar = [0]
    def barrier():
        nbar[0] += 1
        tgt = 5 * nbar[0]
        for eng in ENGS:
            em(eng, lambda e: e.drain().then_inc(s_bar.h, 1))
            wait(eng, s_bar, tgt)

    def vsafe(fn):
        tgt = s_v.inc()
        em("vector", lambda v, f=fn: f(v).then_inc(s_v.h, 1))
        em("vector", lambda v, t=tgt: v.wait_ge(s_v.h, t))
        return tgt

    def asafe(fn):
        tgt = s_a.inc()
        em("scalar", lambda sc, f=fn: f(sc).then_inc(s_a.h, 1))
        em("scalar", lambda sc, t=tgt: sc.wait_ge(s_a.h, t))
        return tgt

    def vsync():
        tgt = s_v.inc()
        em("vector", lambda v, tg=tgt: v.drain().then_inc(s_v.h, 1))
        return tgt

    def asynC():
        tgt = s_a.inc()
        em("scalar", lambda s, tg=tgt: s.drain().then_inc(s_a.h, 1))
        return tgt

    def mm_group(items, inc=True):
        n = len(items)
        for idx, (o, l, r, st, sp) in enumerate(items):
            if idx == n - 1 and inc:
                tgt = s_mm.inc()
                em("tensor", lambda t, o=o, l=l, r=r, st=st, sp=sp, tg=tgt:
                   t.matmul(o(), l(), r(), start=st, stop=sp).then_inc(s_mm.h, 1))
            else:
                em("tensor", lambda t, o=o, l=l, r=r, st=st, sp=sp:
                   t.matmul(o(), l(), r(), start=st, stop=sp))
        return s_mm.v

    LNG = consts[:, 0:1]
    BIAS_P = {3: consts[:, 1:2], 10: consts[:, 3:4]}
    BIAS_N = {3: consts[:, 2:3], 10: consts[:, 4:5]}

    qd_done = []

    def q_chain_fast(x_ap, W, stages, P=128):
        E = vsafe if W < 64 else (lambda fn: em("vector", fn))
        sa = lambda n: scr[n][:P, :W]
        src = x_ap
        for st_i, (man, min_e, dst) in enumerate(stages):
            E(lambda v: v.tensor_scalar(sa("i"), x_ap.bitcast(I32), 23, 255,
                                        ALU.logical_shift_right, ALU.bitwise_and))
            E(lambda v, me=min_e: v.tensor_scalar(
                sa("c"), sa("i"), -127, int(me), ALU.add, ALU.max))
            E(lambda v, man=man: v.tensor_scalar(
                sa("j"), sa("c"), -(1 << 23), (127 + man) << 23,
                ALU.mult, ALU.add))
            E(lambda v, src=src: v.tensor_tensor(
                sa("u"), src, scr["j"][:P, :W].bitcast(F32), ALU.mult))
            E(lambda v, man=man: v.tensor_scalar(
                sa("j"), sa("c"), 1 << 23, (127 - man) << 23,
                ALU.mult, ALU.add))
            E(lambda v: v.tensor_copy(sa("i"), sa("u")))
            if st_i == len(stages) - 1 and W >= 64:
                tgt = s_v.inc()
                em("vector", lambda v, d=dst, tg=tgt: v.tensor_tensor(
                    d, sa("i"), scr["j"][:P, :W].bitcast(F32), ALU.mult
                ).then_inc(s_v.h, 1))
            else:
                E(lambda v, d=dst: v.tensor_tensor(
                    d, sa("i"), scr["j"][:P, :W].bitcast(F32), ALU.mult))
            src = dst
        return s_v.v

    def q_chain(x_ap, W, stages, P=128):
        if not _EXACT_LADDER:
            return q_chain_fast(x_ap, W, stages, P)
        def V(fn):
            return vsafe(fn)
        def A(fn):
            return asafe(fn)
        sa = lambda n: scr[n][:P, :W]

        def rstrict(src_ap, r_dst, tie_even):
            V(lambda v: v.tensor_copy(sa("i"), src_ap))
            V(lambda v: v.tensor_tensor(sa("b"), src_ap, sa("i"), ALU.subtract))
            V(lambda v: v.tensor_scalar(sa("w"), sa("b"), 0.5, None, ALU.is_gt))
            V(lambda v: v.tensor_tensor(sa("w"), sa("i"), sa("w"), ALU.add))
            V(lambda v: v.tensor_scalar(sa("c"), sa("b"), -0.5, None, ALU.is_lt))
            if not tie_even:
                V(lambda v, r=r_dst: v.tensor_tensor(r, sa("w"), sa("c"),
                                                     ALU.subtract))
                return
            V(lambda v: v.tensor_tensor(sa("w"), sa("w"), sa("c"), ALU.subtract))
            V(lambda v: v.tensor_tensor(sa("c"), sa("b"), sa("b"), ALU.mult))
            V(lambda v: v.tensor_scalar(sa("c"), sa("c"), 0.25, None, ALU.is_equal))
            V(lambda v: v.tensor_scalar(sa("i"), sa("i"), 1, None, ALU.bitwise_and))
            V(lambda v: v.tensor_tensor(sa("c"), sa("c"), sa("i"), ALU.mult))
            V(lambda v: v.tensor_tensor(sa("c"), sa("c"), sa("b"), ALU.mult))
            V(lambda v: v.tensor_scalar(sa("c"), sa("c"), 2.0, None, ALU.mult))
            V(lambda v, r=r_dst: v.tensor_tensor(r, sa("w"), sa("c"), ALU.add))

        wait("scalar", s_v, s_v.v)
        A(lambda sc, x=x_ap: sc.activation(sa("a"), x, AF.Square))
        A(lambda sc: sc.activation(sa("b"), sa("a"), AF.Ln, bias=LNG[:P, :]))
        wait("vector", s_a, asynC())
        V(lambda v: v.tensor_scalar(sa("u"), sa("b"), C2LN2, -0.5,
                                    ALU.mult, ALU.add))
        rstrict(sa("u"), sa("a"), False)
        if "qd" in dbg and not qd_done:
            qd_done.append(1)
            V(lambda v: v.tensor_copy(scr["i"][0:1, 0:512], scr["u"][0:1, 0:512]))
            V(lambda v: v.tensor_scalar(scr["w"][0:1, 0:512], scr["i"][0:1, 0:512],
                                        0.0, None, ALU.add))
            wait("sync", s_v, vsync())
            for row, tt in ((0, "u"), (1, "w"), (2, "a"), (3, "b")):
                em("sync", lambda sy, row=row, tt=tt: sy.dma_start(
                    RAP(dbg["qd"], row * 512, [[512, 1], [1, 512]]),
                    scr[tt][0:1, 0:512]).then_inc(s_st.h, 16))
                s_st.inc(16)
            wait("vector", s_st, s_st.v)
        src = x_ap
        for (man, min_e, dst) in stages:
            V(lambda v, me=min_e: v.tensor_scalar(
                sa("c"), sa("a"), float(me), None, ALU.max))
            V(lambda v: v.tensor_copy(sa("i"), sa("c")))
            V(lambda v, man=man: v.tensor_scalar(
                sa("j"), sa("i"), -1, 127 + man, ALU.mult, ALU.add))
            V(lambda v: v.tensor_scalar(sa("j"), sa("j"), 23, None,
                                        ALU.logical_shift_left))
            V(lambda v, src=src: v.tensor_tensor(
                sa("u"), src, scr["j"][:P, :W].bitcast(F32), ALU.mult))
            V(lambda v, man=man: v.tensor_scalar(
                sa("j"), sa("i"), 1, 127 - man, ALU.mult, ALU.add))
            V(lambda v: v.tensor_scalar(sa("j"), sa("j"), 23, None,
                                        ALU.logical_shift_left))
            V(lambda v: v.tensor_copy(sa("s"), scr["j"][:P, :W].bitcast(F32)))
            rstrict(sa("u"), sa("u"), True)
            V(lambda v, d=dst: v.tensor_tensor(d, sa("u"), sa("s"), ALU.mult))
            src = dst
        return vsync()

    for c, val in ((0, 1e-38), (1, 3.0 * LN2), (2, -3.0 * LN2), (3, 10.0 * LN2),
                   (4, -10.0 * LN2), (5, 0.8), (7, EPS), (8, 1.0 / float(D))):
        vsafe(lambda v, c=c, val=val: v.memset(consts[:, c:c + 1], val))
    vsafe(lambda v: v.reciprocal(consts[:, 6:7], consts[:, 5:6]))
    em("vector", lambda v: v.memset(zt[:], 0.0))
    em("vector", lambda v: v.memset(onesrow[:], 1.0))
    em("vector", lambda v: v.memset(onesf[:], 1.0))
    em("vector", lambda v: v.memset(hhsel[:, :], 0.0))
    em("vector", lambda v: v.memset(hhsel[0:64, 0:1], 1.0))
    em("vector", lambda v: v.memset(hhsel[64:128, 1:2], 1.0))
    em("vector", lambda v: v.memset(relblk[:, :], 0.0))
    em("vector", lambda v: v.memset(umath[:, :], 0.0))
    em("vector", lambda v: v.memset(nrm[:, :], 0.0))
    wait("gpsimd", s_v, vsync())

    def zfill(buf, span):
        pos = 0
        while pos < span:
            n = min(128 * P7, span - pos)
            full = n // P7
            if full > 0:
                em("gpsimd", lambda g, b=buf, p=pos, f=full: g.dma_start(
                    RAP(b, p, [[P7, f], [1, P7]]),
                    zt[0:f, :]).then_inc(s_zz.h, 16))
                s_zz.inc(16)
                pos += full * P7
            else:
                em("gpsimd", lambda g, b=buf, p=pos, n=n: g.dma_start(
                    RAP(b, p, [[1, 1], [1, n]]),
                    zt[0:1, 0:n]).then_inc(s_zz.h, 16))
                s_zz.inc(16)
                pos += n

    for buf, span in ((hbuf, 4 * 32 * P6), (dbuf, L * SBD),
                      (cbuf, L * 128), (btbuf, 32 * P5)):
        zfill(buf, span)
    zz_all = s_zz.v

    for tile, src in ((c2b, c2in), (s2b, s2in), (identb, identin),
                      (relw, rel_Tw), (relv, rel_wrev), (rel0r, rel0r_in)):
        em("sync", lambda sy, t=tile, s=src: sy.dma_start(
            t[:], s.ap()).then_inc(s_ld.h, 16))
        s_ld.inc(16)
    ld_tab = s_ld.v

    wait("vector", s_ld, ld_tab)
    vsafe(lambda v: v.tensor_scalar(
        qrt[0][0:64, 0:32], relw[:, 0:32], relw[:, 32:33], None, ALU.subtract))
    em("vector", lambda v: v.tensor_copy(ctxe[0:64, 0:32], qrt[0][0:64, 0:32]))
    wait("gpsimd", s_v, vsync())
    em("gpsimd", lambda g: g.dma_start(
        RAP(relbuf, 0, [[32, 64], [1, 32]]), ctxe[0:64, 0:32]
    ).then_inc(s_gw.h, 16))
    s_gw.inc(16)
    wait("sync", s_gw, s_gw.v)
    em("sync", lambda sy: sy.dma_start(
        relblk[0:64, 0:32], RAP(relbuf, 0, [[32, 64], [1, 32]])
    ).then_inc(s_tp.h, 16))
    s_tp.inc(16)
    em("sync", lambda sy: sy.dma_start(
        relblk[64:128, 32:64], RAP(relbuf, 0, [[32, 64], [1, 32]])
    ).then_inc(s_tp.h, 16))
    s_tp.inc(16)

    wait("tensor", s_ld, ld_tab)
    wait("tensor", s_v, s_v.v)
    mm_t = mm_group([(lambda: paux[0:32, 0:64], lambda: onesf[0:1, 0:32],
                      lambda: rel0r[0:1, 0:64], True, True)])
    wait("vector", s_mm, mm_t)
    vsafe(lambda v: v.tensor_tensor(rl2f[:, :], relv[0:32, 0:64],
                                    paux[0:32, 0:64], ALU.subtract))
    em("vector", lambda v: v.tensor_copy(rl2[:, :], rl2f[:, :]))
    vsafe(lambda v: v.tensor_copy(relrow[0:1, 0:64], rel0r[0:1, 0:64]))
    for hh in range(1, NH):
        em("vector", lambda v, hh=hh: v.tensor_copy(
            relrow[0:1, hh * 64:(hh + 1) * 64], relrow[0:1, 0:64]))
    wait("tensor", s_v, vsync())
    for dc in range(2):
        mm_t = mm_group([(lambda: paux2[0:128, 0:512],
                          lambda: onesrow[0:1, 0:128],
                          lambda dc=dc: relrow[0:1, dc * 512:(dc + 1) * 512],
                          True, True)])
        wait("vector", s_mm, mm_t)
        vt = s_v.inc()
        em("vector", lambda v, dc=dc, tg=vt: v.tensor_copy(
            rel0b[:, dc * 512:(dc + 1) * 512], paux2[0:128, 0:512]
        ).then_inc(s_v.h, 1))
        wait("tensor", s_v, vt)
    barrier()

    wz_units = []
    _pos = 0
    while _pos < NH * L * P7:
        wz_units.append(lambda p=_pos: (
            em("gpsimd", lambda g, p=p: g.dma_start(
                RAP(wbuf, p, [[P7, 128], [1, P7]]),
                zt[0:128, :]).then_inc(s_zz.h, 16)),
            s_zz.inc(16)))
        _pos += 128 * P7

    def wz_pop(n):
        for _ in range(min(n, len(wz_units))):
            wz_units.pop(0)()

    def wfill_flush():
        wz_pop(len(wz_units))
        wait("vector", s_zz, s_zz.v)
        em("vector", lambda v: v.memset(zt[0:128, 0:OFFB], NEG1E18))
        wait("gpsimd", s_v, vsync())
        for h16 in range(NH):
            for c in range(LT):
                em("gpsimd", lambda g, h16=h16, c=c: g.dma_start(
                    RAP(wbuf, h16 * L * P7 + c * 128 * P7,
                        [[P7, 128], [1, OFFB]]),
                    zt[0:128, 0:OFFB]).then_inc(s_gw.h, 16))
                s_gw.inc(16)

    xTb = xqTb
    for i in range(DT):
        for hf in range(2):
            em("sync", lambda sy, i=i, hf=hf: sy.dma_start(
                stageA[:], RAP(x_T, i * 128 * L + hf * 1024, [[L, 128], [1, 1024]])
            ).then_inc(s_ld.h, 16))
            s_ld.inc(16)
            wait("scalar", s_ld, s_ld.v)
            tgt = s_a.inc()
            em("scalar", lambda s, i=i, hf=hf, tg=tgt: s.activation(
                xTb[i][:, hf * 1024:(hf + 1) * 1024], stageA[:],
                AF.Copy).then_inc(s_a.h, 1))
            wait("sync", s_a, tgt)
    barrier()
    wz_pop(40)

    def load_weight_bf16(src, scale, wbf, stg):
        for i in range(DT):
            em("sync", lambda sy, i=i, s=src, stg=stg: sy.dma_start(
                stg[:, 0:D], RAP(s, i * 128 * D, [[D, 128], [1, D]])
            ).then_inc(s_ld.h, 16))
            s_ld.inc(16)
            wait("scalar", s_ld, s_ld.v)
            tgt = s_a.inc()
            em("scalar", lambda s, i=i, sc=scale, wbf=wbf, stg=stg, tg=tgt:
               s.activation(wbf[i][:, :], stg[:, 0:D], AF.Copy, scale=sc
                            ).then_inc(s_a.h, 1))
            wait("sync", s_a, tgt)

    def proj_qk(dstb):
        for dt_ in range(DT):
            for lc in range(NIC):
                pb = pqk[(dt_ * NIC + lc) % 2]
                items = [(lambda pb=pb: pb[:, 0:512],
                          lambda ti=ti, dt_=dt_: wbfA[ti][:, dt_ * 128:(dt_ + 1) * 128],
                          lambda ti=ti, lc=lc: xTb[ti][:, lc * 512:(lc + 1) * 512],
                          ti == 0, ti == DT - 1) for ti in range(DT)]
                mm_t = mm_group(items)
                wait("vector", s_mm, mm_t)
                sl = slice(lc * 512, (lc + 1) * 512)
                em("vector", lambda v, pb=pb: v.tensor_copy(
                    qrt[0][:, 0:512], pb[:, 0:512]))
                em("vector", lambda v: v.stream_shuffle(
                    qrt[1][:, 0:512], qrt[0][:, 0:512], [i ^ 1 for i in range(32)]))
                em("vector", lambda v, sl=sl: v.tensor_tensor(
                    qrt[2][:, 0:512], qrt[0][:, 0:512], c2b[:, sl], ALU.mult))
                em("vector", lambda v, sl=sl: v.tensor_tensor(
                    qrt[1][:, 0:512], qrt[1][:, 0:512], s2b[:, sl], ALU.mult))
                tgt = s_v.inc()
                em("vector", lambda v, dt_=dt_, sl=sl, dstb=dstb, tg=tgt:
                   v.tensor_tensor(dstb[dt_][:, sl], qrt[2][:, 0:512],
                                   qrt[1][:, 0:512], ALU.add).then_inc(s_v.h, 1))
                wait("tensor", s_v, tgt)

    load_weight_bf16(wq_T, 0.125, wbfA, stageA)
    barrier()
    wz_pop(40)
    proj_qk(qTb)
    barrier()
    wz_pop(40)
    load_weight_bf16(wk_T, 1.0, wbfA, stageA)
    barrier()
    wz_pop(40)
    proj_qk(kTb)
    barrier()
    wz_pop(40)
    load_weight_bf16(wv_T, 1.0, wbfA, stageA)
    for lt in range(LT):
        for h in range(NH):
            em("vector", lambda v, lt=lt, h=h: v.memset(
                vhb[lt][:, h * 66 + 64:h * 66 + 65], 1.0))
            em("vector", lambda v, lt=lt, h=h: v.memset(
                vhb[lt][:, h * 66 + 65:h * 66 + 66], 0.0))
    vsync()
    barrier()
    wz_pop(40)
    for lt in range(LT):
        for dc in range(2):
            pb = pqk[(lt * 2 + dc) % 2]
            items = [(lambda pb=pb: pb[:, 0:512],
                      lambda ti=ti, lt=lt: xTb[ti][:, lt * 128:(lt + 1) * 128],
                      lambda ti=ti, dc=dc: wbfA[ti][:, dc * 512:(dc + 1) * 512],
                      ti == 0, ti == DT - 1) for ti in range(DT)]
            mm_t = mm_group(items)
            wait("vector", s_mm, mm_t)
            em("vector", lambda v, pb=pb, dc=dc: v.tensor_tensor(
                qrt[0][:, 0:512], pb[:, 0:512],
                rel0b[:, dc * 512:(dc + 1) * 512], ALU.add))
            for hl in range(8):
                h = dc * 8 + hl
                em("vector", lambda v, lt=lt, h=h, hl=hl: v.tensor_copy(
                    vhb[lt][:, h * 66:h * 66 + 64],
                    qrt[0][:, hl * 64:(hl + 1) * 64]))
            tgt = s_v.inc()
            em("vector", lambda v, tg=tgt: v.drain().then_inc(s_v.h, 1))
            wait("tensor", s_v, tgt)
    barrier()
    esA.close()

    if "qT" in dbg:
        for i in range(DT):
            em("sync", lambda sy, i=i: sy.dma_start(
                RAP(dbg["qT"], i * 128 * L, [[L, 128], [1, L]]), qTb[i][:]
            ).then_inc(s_st.h, 16))
            s_st.inc(16)
    if "kT" in dbg:
        for i in range(DT):
            em("sync", lambda sy, i=i: sy.dma_start(
                RAP(dbg["kT"], i * 128 * L, [[L, 128], [1, L]]), kTb[i][:]
            ).then_inc(s_st.h, 16))
            s_st.inc(16)
    if "vh" in dbg:
        for lt in range(LT):
            em("sync", lambda sy, lt=lt: sy.dma_start(
                RAP(dbg["vh"], lt * 128 * NH * 66, [[NH * 66, 128], [1, NH * 66]]),
                vhb[lt][:]).then_inc(s_st.h, 16))
            s_st.inc(16)
    if phases < 3:
        return _finish(nc, plan, dbg, (es, esB, esB2, esC))

    sqb = sbB2("sqb", [128, L], BF16)
    sgb = sbB2("sgb", [64, L], F32)
    a0w = sbB2("a0w", [32, L], BF16)
    bndt = sbB2("bndt", [128, L], BF16)
    hsb = sbB2("hsb", [64, L], BF16)
    stageQ = sbB2("stageQ", [128, L], F32)
    wqtile = sbB2("wqtile", [128, L], BF16)

    wprep_done = [0] * NH
    prep_state = {"copy_last": 0}
    mb_guard = [0, 0]
    gt_guard = [0, 0]
    hsb_guard = [0]

    def prep_pair(p):
        prep_copy_last = 0
        wait("scalar", s_gw, hsb_guard[0])
        for ic in range(NIC):
            mm_t = mm_group([(lambda: paux[0:64, 0:512],
                              lambda: relblk[:, 0:64],
                              lambda p=p, ic=ic:
                              qTb[p][:, ic * 512:(ic + 1) * 512],
                              True, True)])
            wait("scalar", s_mm, mm_t)
            tgt = s_a.inc()
            em("scalar", lambda sc, ic=ic: sc.activation(
                hsb[0:64, ic * 512:(ic + 1) * 512], paux[0:64, 0:512],
                AF.Copy).then_inc(s_a.h, 1))
            wait("tensor", s_a, tgt)
            prep_copy_last = tgt
        wait("gpsimd", s_a, prep_copy_last)
        for hh in range(2):
            s4 = (p % 2) * 2 + hh
            em("gpsimd", lambda g, hh=hh, s4=s4: g.dma_start(
                RAP(hbuf, s4 * 32 * P6, [[P6, 32], [1, L]]),
                hsb[hh * 32:(hh + 1) * 32, 0:L]).then_inc(s_gw.h, 16))
            s_gw.inc(16)
        hsb_guard[0] = s_gw.v
        for hh in range(2):
            h2 = 2 * p + hh
            s4 = (p % 2) * 2 + hh
            wait("gpsimd", s_tp, mb_guard[h2 % 2])
            em("gpsimd", lambda g, h2=h2, s4=s4: g.dma_start(
                RAP(mbuf, (h2 % 2) * 16 * 4096,
                    [[4096, 16], [128, 32], [1, 128]]),
                RAP(hbuf, s4 * 32 * P6, [[128, 16], [P6 + 1, 32], [1, 128]])
            ).then_inc(s_gw.h, 16))
            s_gw.inc(16)
            mh_t = s_gw.v
            wait("sync", s_gw, mh_t)
            for jt in range(LT):
                wait("sync", s_gw, gt_guard[jt % 2])
                em("sync", lambda sy, h2=h2, jt=jt: sy.dma_start_transpose(
                    gt[jt % 2][:], RAP(mbuf, (h2 % 2) * 16 * 4096 + jt * 4096,
                                       [[128, 32], [1, 128]])
                ).then_inc(s_tp.h, 16))
                s_tp.inc(16)
                wait("gpsimd", s_tp, s_tp.v)
                em("gpsimd", lambda g, h2=h2, jt=jt: g.dma_start(
                    RAP(wbuf, h2 * L * P7 + jt * 128 * P7 + OFFB,
                        [[P7, 128], [1, 32]]),
                    gt[jt % 2][:]).then_inc(s_gw.h, 16))
                s_gw.inc(16)
                gt_guard[jt % 2] = s_gw.v
            mb_guard[h2 % 2] = s_tp.v
            wprep_done[h2] = s_gw.v
        prep_state["copy_last"] = prep_copy_last

    wfill_flush()

    prep_units = [(lambda p=p: prep_pair(p)) for p in range(NH // 2)]
    for _ in range(3):
        prep_units.pop(0)()

    wait("tensor", s_a, prep_state["copy_last"])
    em("scalar", lambda s: s.activation(qrt[0][0:32, 0:64], rl2f[:, :], AF.Square))
    wait("vector", s_a, asynC())
    em("vector", lambda v: v.reduce_sum(qrt[1][0:32, 0:1], qrt[0][0:32, 0:64], axis=AX.X))
    wait("gpsimd", s_v, vsync())
    em("gpsimd", lambda g: g.dma_start(RAP(ubuf, 0, [[1, 32], [1, 1]]),
                                       qrt[1][0:32, 0:1]).then_inc(s_gw.h, 16))
    s_gw.inc(16)
    wait("sync", s_gw, s_gw.v)
    em("sync", lambda sy: sy.dma_start(qrt[2][0:1, 0:32],
                                       RAP(ubuf, 0, [[32, 1], [1, 32]])
                                       ).then_inc(s_tp.h, 16))
    s_tp.inc(16)
    wait("vector", s_tp, s_tp.v)
    em("vector", lambda v: v.reduce_max(qrt[2][0:1, 40:41], qrt[2][0:1, 0:32], axis=AX.X))
    wait("gpsimd", s_v, vsync())
    em("gpsimd", lambda g: g.dma_start(RAP(ubuf, 32, [[2, 1], [1, 2]]),
                                       qrt[2][0:1, 40:42]).then_inc(s_gw.h, 16))
    s_gw.inc(16)

    for srcb, off in ((qTb, 0), (kTb, 8)):
        for th in range(DT):
            tgt = s_v.inc()
            em("vector", lambda v, th=th, srcb=srcb, tg=tgt: v.tensor_tensor(
                sqb[:, :], srcb[th][:], srcb[th][:], ALU.mult).then_inc(s_v.h, 1))
            wait("tensor", s_v, tgt)
            vsafe(lambda v, off=off, th=th: v.memset(
                nrm[0:2, off + th:off + th + 1], 0.0))
            for ic in range(NIC):
                mm_t = mm_group([(lambda: paux[0:2, 0:512],
                                  lambda: hhsel[:, 0:2],
                                  lambda ic=ic: sqb[:, ic * 512:(ic + 1) * 512],
                                  True, True)])
                wait("vector", s_mm, mm_t)
                vsafe(lambda v: v.reduce_max(umath[0:2, 0:1],
                                             paux[0:2, 0:512], axis=AX.X))
                tgt = vsafe(lambda v, off=off, th=th: v.tensor_tensor(
                    nrm[0:2, off + th:off + th + 1],
                    nrm[0:2, off + th:off + th + 1],
                    umath[0:2, 0:1], ALU.max))
                wait("tensor", s_v, tgt)

    wait("gpsimd", s_v, s_v.v)
    em("gpsimd", lambda g: g.dma_start(RAP(ubuf, 0, [[16, 2], [1, 16]]),
                                       nrm[0:2, 0:16]).then_inc(s_gw.h, 16))
    s_gw.inc(16)
    wait("sync", s_gw, s_gw.v)
    em("sync", lambda sy: sy.dma_start(
        unew[:, 0:33], RAP(ubuf, 0, [[0, 128], [1, 33]])).then_inc(s_tp.h, 16))
    s_tp.inc(16)
    wait("scalar", s_tp, s_tp.v)
    asafe(lambda sc: sc.activation(unew[:, 40:73], unew[:, 0:33], AF.Ln,
                                   bias=LNG[:, :]))
    asafe(lambda sc: sc.activation(unew[:, 0:33], unew[:, 40:73],
                                   AF.Exp, scale=0.5))
    wait("vector", s_a, asynC())
    for hh in range(2):
        vsafe(lambda v, hh=hh: v.tensor_scalar(
            unew[:, 40 + hh * 8:48 + hh * 8], unew[:, hh * 16 + 8:hh * 16 + 16],
            unew[:, 32:33], None, ALU.add))
        vsafe(lambda v, hh=hh: v.tensor_tensor(
            unew[:, 60 + hh * 8:68 + hh * 8], unew[:, hh * 16:hh * 16 + 8],
            unew[:, 40 + hh * 8:48 + hh * 8], ALU.mult))
        vsafe(lambda v, hh=hh: v.tensor_scalar(
            negu[:, hh * 8:(hh + 1) * 8], unew[:, 60 + hh * 8:68 + hh * 8],
            -1.0, -2.0, ALU.mult, ALU.add))
    vsync()
    if "negu" in dbg:
        wait("sync", s_v, s_v.v)
        em("sync", lambda sy: sy.dma_start(
            RAP(dbg["negu"], 0, [[16, 2], [1, 16]]), negu[0:2, 0:16]
        ).then_inc(s_st.h, 16))
        s_st.inc(16)
    if "nrm" in dbg:
        wait("sync", s_v, s_v.v)
        em("sync", lambda sy: sy.dma_start(
            RAP(dbg["nrm"], 0, [[16, 2], [1, 16]]), nrm[0:2, 0:16]
        ).then_inc(s_st.h, 16))
        s_st.inc(16)
    if "um" in dbg:
        wait("sync", s_v, s_v.v)
        em("sync", lambda sy: sy.dma_start(
            RAP(dbg["um"], 0, [[48, 2], [1, 48]]), umath[0:2, 0:48]
        ).then_inc(s_st.h, 16))
        s_st.inc(16)
    if "unew" in dbg:
        wait("sync", s_v, s_v.v)
        em("sync", lambda sy: sy.dma_start(
            RAP(dbg["unew"], 0, [[80, 2], [1, 80]]), unew[0:2, 0:80]
        ).then_inc(s_st.h, 16))
        s_st.inc(16)
    barrier()
    if phases < 4:
        return _finish(nc, plan, dbg, (es, esB, esB2, esC))

    for ht in range(HT):
        em("sync", lambda sy, ht=ht: sy.dma_start(
            b1raw[:, ht:ht + 1], RAP(b1in, ht * 128, [[1, 128], [1, 1]])
        ).then_inc(s_ld.h, 16))
        s_ld.inc(16)
    wait("scalar", s_ld, s_ld.v)
    wait("vector", s_ld, s_ld.v)
    q_chain(b1raw[:, 0:HT], HT, [(3, -6, b1qb[:, 0:HT])])
    em("sync", lambda sy: sy.dma_start(stageQ[0:1, 0:1024],
                                       RAP(b2in, 0, [[1, 1], [1, 1024]])
                                       ).then_inc(s_ld.h, 16))
    s_ld.inc(16)
    wait("scalar", s_ld, s_ld.v)
    wait("vector", s_ld, s_ld.v)
    q_chain(stageQ[0:1, 0:512], 512, [(3, -6, qrt[0][0:1, 0:512])], P=1)
    q_chain(stageQ[0:1, 512:1024], 512, [(3, -6, qrt[1][0:1, 0:512])], P=1)
    em("vector", lambda v: v.tensor_copy(relrow[0:1, 0:512], qrt[0][0:1, 0:512]))
    em("vector", lambda v: v.tensor_copy(relrow[0:1, 512:1024], qrt[1][0:1, 0:512]))
    vsync()

    def _w1_unit(ti, half):
        wait("sync", s_v, s_v.v)
        em("sync", lambda sy: sy.dma_start(
            stageQ[:, :], RAP(w1_T, ti * 128 * HID + half * 2048,
                              [[HID, 128], [1, 2048]])
        ).then_inc(s_ld.h, 16))
        s_ld.inc(16)
        wait("vector", s_ld, s_ld.v)
        for qq in range(4):
            vtg = q_chain(stageQ[:, qq * 512:(qq + 1) * 512], 512,
                          [(3, -6, wqtile[:, qq * 512:(qq + 1) * 512])])
        wait("gpsimd", s_v, vtg)
        em("gpsimd", lambda g: g.dma_start(
            RAP(w1qd, (half * 16) * (DT * 16384) + ti * 16384,
                [[128, 128], [DT * 16384, 16], [1, 128]]),
            wqtile[:, 0:2048].rearrange("p (b c) -> p b c", c=128)
        ).then_inc(s_gw.h, 16))
        s_gw.inc(16)
        wait("vector", s_gw, s_gw.v)

    def _w2_unit(ht):
        wait("sync", s_v, s_v.v)
        em("sync", lambda sy: sy.dma_start(
            stageQ[:, 0:D], RAP(w2_T, ht * 128 * D, [[D, 128], [1, D]])
        ).then_inc(s_ld.h, 16))
        s_ld.inc(16)
        wait("vector", s_ld, s_ld.v)
        for qq in range(2):
            vtg = q_chain(stageQ[:, qq * 512:(qq + 1) * 512], 512,
                          [(3, -6, wqtile[:, qq * 512:(qq + 1) * 512])])
        wait("gpsimd", s_v, vtg)
        em("gpsimd", lambda g: g.dma_start(
            RAP(w2qd, ht * 128 * D, [[D, 128], [1, D]]), wqtile[:, 0:D]
        ).then_inc(s_gw.h, 16))
        s_gw.inc(16)
        wait("vector", s_gw, s_gw.v)

    p4_units = ([(lambda ti=ti, half=half: _w1_unit(ti, half))
                 for ti in range(DT) for half in range(2)]
                + [(lambda ht=ht: _w2_unit(ht)) for ht in range(HT)])
    if phases < 5:
        for u in p4_units:
            u()
        p4_units = []
        barrier()
        return _finish(nc, plan, dbg, (es, esB, esB2, esC))

    pqk_guard = [0, 0]
    attnb_guard = [(0, 0), (0, 0)]
    wint_guard = [0, 0]
    pctx_guard = [0]
    sqb_guard = [0]
    n_units_per_head = (len(p4_units) + nheads - 1) // max(nheads, 1)
    for h in range(nheads):
        if h % 2 == 0 and prep_units:
            prep_units.pop(0)()
        for _ in range(n_units_per_head):
            if p4_units:
                p4_units.pop(0)()
        th, hh = h // 2, h % 2
        rows = slice(hh * 64, hh * 64 + 64)
        wait("sync", s_gw, wprep_done[h])

        for jt in range(LT):
            wwidth = min(160, L - jt * 128)
            wait("sync", s_mm, wint_guard[jt % 2])
            em("sync", lambda sy, h=h, jt=jt, ww=wwidth: sy.dma_start(
                wint[jt % 2][:, 0:ww],
                RAP(wbuf, h * L * P7 + jt * 128 * P7 + OFFB,
                    [[P7 - 1, 128], [1, ww]])
            ).then_inc(s_tp.h, 16))
            s_tp.inc(16)
            win_t = s_tp.v
            icmin = (jt * 128) // 512
            coff0 = jt * 128 - icmin * 512
            gmm, ggw = attnb_guard[jt % 2]
            wait("scalar", s_mm, gmm)
            wait("scalar", s_gw, ggw)
            last_exp = 0
            for ic in range(icmin, NIC):
                pb = pqk[ic % 2]
                coff = coff0 if ic == icmin else 0
                haswin = (ic == icmin) or (
                    ic == icmin + 1 and wwidth > 512 - coff0)
                items = [(lambda pb=pb, coff=coff: pb[:, coff:512],
                          lambda th=th, rows=rows, jt=jt:
                          kTb[th][rows, jt * 128:(jt + 1) * 128],
                          lambda th=th, rows=rows, ic=ic, coff=coff:
                          qTb[th][rows, ic * 512 + coff:(ic + 1) * 512],
                          True, not haswin)]
                if ic == icmin:
                    wait("tensor", s_tp, win_t)
                    w1w = min(wwidth, 512 - coff0)
                    items.append((lambda pb=pb, coff=coff, w1w=w1w:
                                  pb[:, coff:coff + w1w],
                                  lambda: identb[:, :],
                                  lambda jt=jt, w1w=w1w: wint[jt % 2][:, 0:w1w],
                                  False, True))
                elif haswin:
                    w1w = 512 - coff0
                    items.append((lambda pb=pb, w2w=wwidth - w1w: pb[:, 0:w2w],
                                  lambda: identb[:, :],
                                  lambda jt=jt, w1w=w1w, ww=wwidth:
                                  wint[jt % 2][:, w1w:ww],
                                  False, True))
                wait("tensor", s_a, pqk_guard[ic % 2])
                mm_t = mm_group(items)
                if haswin:
                    wint_guard[jt % 2] = mm_t
                wait("scalar", s_mm, mm_t)
                tgt = s_a.inc()
                em("scalar", lambda s, jt=jt, ic=ic, coff=coff, pb=pb, h=h, tg=tgt:
                   s.activation(attnb[jt % 2][:, ic * 512 + coff:(ic + 1) * 512],
                                pb[:, coff:512], AF.Exp,
                                bias=negu[:, (h % 2) * 8 + h // 2:
                                          (h % 2) * 8 + h // 2 + 1]).then_inc(s_a.h, 1))
                pqk_guard[ic % 2] = tgt
                last_exp = tgt
            wait("gpsimd", s_a, last_exp)
            em("gpsimd", lambda g, jt=jt, ww=wwidth: g.dma_start(
                RAP(dbuf, jt * 128 * SBD + OFFB, [[SBD - 1, 128], [1, ww]]),
                attnb[jt % 2][:, jt * 128:jt * 128 + ww]
            ).then_inc(s_gw.h, 16))
            s_gw.inc(16)
            dbuf_t = s_gw.v
            wait("tensor", s_a, last_exp)
            if jt == 0:
                wait("tensor", s_a, pctx_guard[0])
            for ic in range(icmin, NIC):
                coff = coff0 if ic == icmin else 0
                mm_group([(lambda ic=ic, coff=coff: pctx[ic][0:65, coff:512],
                           lambda jt=jt, h=h: vhb[jt][:, h * 66:h * 66 + 65],
                           lambda jt=jt, ic=ic, coff=coff:
                           attnb[jt % 2][:, ic * 512 + coff:(ic + 1) * 512],
                           jt == 0, False)], inc=(ic == NIC - 1))
            attnb_guard[jt % 2] = (s_mm.v, dbuf_t)

        wait("gpsimd", s_gw, s_gw.v)
        em("gpsimd", lambda g: g.dma_start(
            RAP(cbuf, 0, [[128, L], [1, 32]]),
            RAP(dbuf, OFFB, [[SBD, L], [1, 32]])).then_inc(s_gw.h, 16))
        s_gw.inc(16)
        wait("sync", s_gw, s_gw.v)
        em("sync", lambda sy: sy.dma_start_transpose(
            bndt[:], RAP(cbuf, 0, [[128, L], [1, 128]])).then_inc(s_tp.h, 16))
        s_tp.inc(16)
        wait("gpsimd", s_tp, s_tp.v)
        em("gpsimd", lambda g: g.dma_start(
            RAP(btbuf, 0, [[P5, 32], [1, L]]), bndt[0:32, :]
        ).then_inc(s_gw.h, 16))
        s_gw.inc(16)
        wait("sync", s_gw, s_gw.v)
        em("sync", lambda sy: sy.dma_start(
            a0w[:, :], RAP(btbuf, 0, [[P5 - 1, 32], [1, L]])
        ).then_inc(s_tp.h, 16))
        s_tp.inc(16)
        wait("tensor", s_tp, s_tp.v)
        wait("tensor", s_mm, s_mm.v)
        for ic in range(NIC):
            if not skip_ctxrel:
                mm_group([(lambda ic=ic: pctx[ic][0:64, 0:512],
                           lambda: rl2[:, :],
                           lambda ic=ic: a0w[:, ic * 512:(ic + 1) * 512],
                           False, False)], inc=False)
            mm_group([(lambda ic=ic: pctx[ic][0:65, 0:1],
                       lambda: zt[:, 160:225],
                       lambda: zt[:, 0:1],
                       False, True)], inc=(ic == NIC - 1))
        mm_ctx = s_mm.v
        wait("scalar", s_mm, mm_ctx)
        wait("scalar", s_v, sqb_guard[0])
        for ic in range(NIC):
            em("scalar", lambda s, ic=ic: s.activation(
                sgb[0:1, ic * 512:ic * 512 + 512], pctx[ic][64:65, 0:512],
                AF.Copy))
        ev_t = 0
        for ic in range(NIC):
            if ic == NIC - 1:
                ev_t = s_a.inc()
                em("scalar", lambda s, ic=ic: s.activation(
                    sqb[0:64, ic * 512:ic * 512 + 512], pctx[ic][0:64, 0:512],
                    AF.Copy).then_inc(s_a.h, 1))
            else:
                em("scalar", lambda s, ic=ic: s.activation(
                    sqb[0:64, ic * 512:ic * 512 + 512], pctx[ic][0:64, 0:512],
                    AF.Copy))
        pctx_guard[0] = ev_t
        for ic in range(NIC):
            em("scalar", lambda s, ic=ic: s.activation(
                qrt[1][0:1, 0:512], sgb[0:1, ic * 512:ic * 512 + 512], AF.Ln,
                bias=LNG[0:1, :]))
            buf = qrt[0] if ic % 2 == 0 else qrt[2]
            tgt = s_a.inc()
            em("scalar", lambda s, b=buf, tg=tgt: s.activation(
                b[0:1, 0:512], qrt[1][0:1, 0:512], AF.Exp,
                scale=-1.0).then_inc(s_a.h, 1))
            wait("gpsimd", s_a, tgt)
            gtg = s_gw.inc(16)
            em("gpsimd", lambda g, ic=ic, b=buf: g.dma_start(
                RAP(sgbuf, ic * 512, [[1, 1], [1, 512]]),
                b[0:1, 0:512]).then_inc(s_gw.h, 16))
            if ic >= 1:
                wait("scalar", s_gw, gtg - 16)
        wait("sync", s_gw, s_gw.v)
        em("sync", lambda sy: sy.dma_start(
            sgb[0:64, :], RAP(sgbuf, 0, [[0, 64], [1, L]])
        ).then_inc(s_tp.h, 16))
        s_tp.inc(16)
        wait("vector", s_tp, s_tp.v)
        for ic in range(NIC):
            tgt = s_v.inc()
            em("vector", lambda v, ic=ic, tg=tgt: v.tensor_tensor(
                ctxe[0:64, 0:512], sqb[0:64, ic * 512:(ic + 1) * 512],
                sgb[0:64, ic * 512:(ic + 1) * 512], ALU.mult).then_inc(s_v.h, 1))
            wait("gpsimd", s_v, tgt)
            gtg = s_gw.inc(16)
            em("gpsimd", lambda g, h=h, ic=ic, tg=gtg: g.dma_start(
                RAP(ctxd, (h // 2) * 128 * L + (h % 2) * 64 * L + ic * 512,
                    [[L, 64], [1, 512]]), ctxe[0:64, 0:512]).then_inc(s_gw.h, 16))
            wait("vector", s_gw, gtg)
        sqb_guard[0] = s_v.v

    for u in p4_units:
        u()
    p4_units = []
    barrier()
    esB.close()
    esB2.close()
    if "ctxT" in dbg:
        tmpc = sbC("tmpc", [128, L], BF16)
        for i in range(DT):
            em("sync", lambda sy, i=i, tmpc=tmpc: sy.dma_start(
                tmpc[:], RAP(ctxd, i * 128 * L, [[L, 128], [1, L]])
            ).then_inc(s_tp.h, 16))
            s_tp.inc(16)
            wait("sync", s_tp, s_tp.v)
            em("sync", lambda sy, i=i, tmpc=tmpc: sy.dma_start(
                RAP(dbg["ctxT"], i * 128 * L, [[L, 128], [1, L]]), tmpc[:]
            ).then_inc(s_st.h, 16))
            s_st.inc(16)
            wait("sync", s_st, s_st.v)
    if phases < 6:
        return _finish(nc, plan, dbg, (es, esC))

    wbfC = [sbC(f"wbfC{i}", [128, D], BF16) for i in range(DT)]
    stageC = sbC("stageC", [128, D], F32)
    strm = [sbC(f"strm{i}", [128, D], F32) for i in range(2)]
    maskt = [sbC(f"maskt{i}", [128, D], BF16) for i in range(2)]
    lnt = sbC("lnt", [128, D], F32)
    ytile = sbC("ytile", [128, D], F32)
    xqtile = sbC("xqtile", [128, D], BF16)
    hqTb = [sbC(f"hqTb{i}", [128, 512], BF16) for i in range(HT)]
    w1qt = [sbC(f"w1qt{i}", [128, 1024], BF16) for i in range(2)]
    w2qs = [sbC(f"w2qs{i}", [128, 512], BF16) for i in range(2)]
    ysb = [sbC(f"ysb{i}", [128, D], F32) for i in range(4)]
    b2qb = sbC("b2qb", [128, D], F32)
    ctxt8 = [sbC(f"ctxt8_{i}", [128, D], BF16) for i in range(2)]
    ytil2 = [ytile, sbC("ytile2", [128, D], F32)]
    xqt2 = [xqtile, sbC("xqtile2", [128, D], BF16)]

    for dc in range(2):
        mm_t = mm_group([(lambda: paux2[0:128, 0:512],
                          lambda: onesrow[0:1, 0:128],
                          lambda dc=dc: relrow[0:1, dc * 512:(dc + 1) * 512],
                          True, True)])
        wait("vector", s_mm, mm_t)
        tgt = s_v.inc()
        em("vector", lambda v, dc=dc, tg=tgt: v.tensor_copy(
            b2qb[:, dc * 512:(dc + 1) * 512], paux2[0:128, 0:512]
        ).then_inc(s_v.h, 1))
        wait("tensor", s_v, tgt)

    load_weight_bf16(wo_T, 1.0, wbfC, stageC)
    barrier()

    def ln_tile(src_ap, dst_ap):
        vsafe(lambda v: v.reduce_sum(col[:, 0:1], src_ap, axis=AX.X))
        vsafe(lambda v: v.tensor_scalar(col[:, 0:1], col[:, 0:1],
                                        consts[:, 8:9], None, ALU.mult))
        wait("scalar", s_v, vsync())
        asafe(lambda sc: sc.activation(ysb[0][:, 0:D], src_ap, AF.Square,
                                       accum_out=col[:, 1:2]))
        wait("vector", s_a, asynC())
        vsafe(lambda v: v.tensor_scalar(col[:, 1:2], col[:, 1:2],
                                        consts[:, 8:9], None, ALU.mult))
        vsafe(lambda v: v.tensor_tensor(col[:, 2:3], col[:, 0:1],
                                        col[:, 0:1], ALU.mult))
        vsafe(lambda v: v.tensor_tensor(col[:, 3:4], col[:, 1:2],
                                        col[:, 2:3], ALU.subtract))
        wait("scalar", s_v, vsync())
        asafe(lambda sc: sc.activation(col[:, 4:5], col[:, 3:4], AF.Ln,
                                       bias=consts[:, 7:8]))
        asafe(lambda sc: sc.activation(col[:, 4:5], col[:, 4:5], AF.Exp,
                                       scale=-0.5))
        wait("vector", s_a, asynC())
        vsafe(lambda v: v.scalar_tensor_tensor(
            col[:, 5:6], col[:, 0:1], -1.0, col[:, 4:5], ALU.mult, ALU.mult))
        tgt = s_v.inc()
        em("vector", lambda v, d=dst_ap, x=src_ap: v.tensor_scalar(
            d, x, col[:, 4:5], col[:, 5:6], ALU.mult, ALU.add).then_inc(s_v.h, 1))
        return tgt

    ctx_mm_last = {}
    tmm_tgt = {}
    cpy_tgt = {}
    res_tgt = {}
    qch_tgt = {}
    x1st_tgt = {}

    def p6_transposes(lt):
        if lt - 2 in cpy_tgt:
            wait("tensor", s_a, cpy_tgt[lt - 2])
        wait("tensor", s_v, qch_tgt[lt])
        items = [(lambda lt=lt, ti=ti: pctx[(lt % 2) * 2 + ti // 4]
                  [0:128, (ti % 4) * 128:(ti % 4) * 128 + 128],
                  lambda lt=lt, ti=ti: xqt2[lt % 2][:, ti * 128:(ti + 1) * 128],
                  lambda: identb[:, :], True, True) for ti in range(DT)]
        tmm_tgt[lt] = mm_group(items)
        wait("scalar", s_mm, tmm_tgt[lt])
        tgt = 0
        for ti in range(DT):
            if ti == DT - 1:
                tgt = s_a.inc()
                em("scalar", lambda sc, lt=lt, ti=ti: sc.activation(
                    xqTb[ti][:, lt * 128:(lt + 1) * 128],
                    pctx[(lt % 2) * 2 + ti // 4]
                    [0:128, (ti % 4) * 128:(ti % 4) * 128 + 128],
                    AF.Copy).then_inc(s_a.h, 1))
            else:
                em("scalar", lambda sc, lt=lt, ti=ti: sc.activation(
                    xqTb[ti][:, lt * 128:(lt + 1) * 128],
                    pctx[(lt % 2) * 2 + ti // 4]
                    [0:128, (ti % 4) * 128:(ti % 4) * 128 + 128],
                    AF.Copy))
        cpy_tgt[lt] = tgt

    for lt in range(LT):
        if lt - 2 in ctx_mm_last:
            wait("sync", s_mm, ctx_mm_last[lt - 2])
        em("sync", lambda sy, lt=lt: sy.dma_start(
            ctxt8[lt % 2][:, :].rearrange("p (t c) -> p t c", c=128),
            RAP(ctxd, lt * 128, [[L, 128], [128 * L, DT], [1, 128]])
        ).then_inc(s_tp.h, 16))
        s_tp.inc(16)
        ctx_t = s_tp.v
        if (lt - 2, 1) in res_tgt:
            wait("sync", s_v, res_tgt[(lt - 2, 1)])
        em("sync", lambda sy, lt=lt: sy.dma_start(
            strm[lt % 2][:, :], RAP(x_nat, lt * 128 * D, [[D, 128], [1, D]])
        ).then_inc(s_ld.h, 16))
        s_ld.inc(16)
        x0_t = s_ld.v
        wait("tensor", s_tp, ctx_t)
        for dc in range(2):
            pb = paux if dc == 0 else paux2
            if (lt - 1, dc) in res_tgt:
                wait("tensor", s_v, res_tgt[(lt - 1, dc)])
            mm_t = mm_group([(lambda pb=pb: pb[:, 0:512],
                              lambda lt=lt, ti=ti:
                              ctxt8[lt % 2][:, ti * 128:(ti + 1) * 128],
                              lambda ti=ti, dc=dc:
                              wbfC[ti][:, dc * 512:(dc + 1) * 512],
                              ti == 0, ti == DT - 1) for ti in range(DT)])
            if dc == 1:
                ctx_mm_last[lt] = mm_t
            wait("vector", s_mm, mm_t)
            if dc == 0:
                wait("vector", s_ld, x0_t)
            tgt = s_v.inc()
            em("vector", lambda v, lt=lt, dc=dc, pb=pb, tg=tgt: v.tensor_tensor(
                lnt[:, dc * 512:(dc + 1) * 512], pb[:, 0:512],
                strm[lt % 2][:, dc * 512:(dc + 1) * 512], ALU.add
            ).then_inc(s_v.h, 1))
            res_tgt[(lt, dc)] = tgt
        if lt > 0:
            p6_transposes(lt - 1)
        if lt - 2 in x1st_tgt:
            wait("vector", s_gw, x1st_tgt[lt - 2])
        ln_t = ln_tile(lnt[:, 0:D], ytil2[lt % 2][:, 0:D])
        wait("gpsimd", s_v, ln_t)
        em("gpsimd", lambda g, lt=lt: g.dma_start(
            RAP(x1d, lt * 128 * D, [[D, 128], [1, D]]), ytil2[lt % 2][:, 0:D]
        ).then_inc(s_gw.h, 16))
        s_gw.inc(16)
        x1st_tgt[lt] = s_gw.v
        if lt - 2 in tmm_tgt:
            wait("vector", s_mm, tmm_tgt[lt - 2])
        for qq in range(2):
            vtg = q_chain(ytil2[lt % 2][:, qq * 512:(qq + 1) * 512], 512,
                          [(3, -6, xqt2[lt % 2][:, qq * 512:(qq + 1) * 512])])
        qch_tgt[lt] = vtg
        if "x1" in dbg:
            wait("sync", s_v, vtg)
            em("sync", lambda sy, lt=lt: sy.dma_start(
                RAP(dbg["x1"], lt * 128 * D, [[D, 128], [1, D]]),
                ytil2[lt % 2][:, 0:D]).then_inc(s_st.h, 16))
            s_st.inc(16)
            wait("sync", s_st, s_st.v)
    p6_transposes(LT - 1)
    barrier()
    if phases < 7:
        return _finish(nc, plan, dbg, (es, esC))

    mmA2 = []
    mmB2 = []
    ev2 = []
    st2 = []
    for lc in range(NIC):
        for ht in range(HT):
            if len(mmA2) >= 2:
                wait("sync", s_mm, mmA2[-2])
            em("sync", lambda sy, ht=ht: sy.dma_start(
                w1qt[ht % 2][:, :], RAP(w1qd, ht * DT * 16384,
                                        [[128, 128], [16384, DT], [1, 128]])
            ).then_inc(s_tp.h, 16))
            s_tp.inc(16)
            wait("tensor", s_tp, s_tp.v)
            items = [(lambda: paux[:, 0:512],
                      lambda ht=ht, ti=ti: w1qt[ht % 2][:, ti * 128:(ti + 1) * 128],
                      lambda ti=ti, lc=lc: xqTb[ti][:, lc * 512:(lc + 1) * 512],
                      ti == 0, ti == DT - 1) for ti in range(DT)]
            mm_t = mm_group(items)
            mmA2.append(mm_t)
            wait("vector", s_mm, mm_t)
            tgt = s_v.inc()
            em("vector", lambda v, ht=ht, tg=tgt: v.tensor_scalar(
                qrt[0][:, 0:512], paux[:, 0:512], b1qb[:, ht:ht + 1], 0.0,
                ALU.add, ALU.max).then_inc(s_v.h, 1))
            wait("tensor", s_v, tgt)
            wait("scalar", s_v, tgt)
            vtg = q_chain(qrt[0][:, 0:512], 512,
                          [(10, -14, qrt[1][:, 0:512]),
                           (3, -6, hqTb[ht][:, 0:512])])
            if "hq0" in dbg and lc == 0 and ht == 0:
                wait("sync", s_v, vtg)
                em("sync", lambda sy: sy.dma_start(
                    dbg["hq0"].ap(), hqTb[0][:, 0:512]).then_inc(s_st.h, 16))
                s_st.inc(16)
        wait("tensor", s_v, s_v.v)
        wait("vector", s_a, s_a.v)
        for dc in range(2):
            for ht in range(HT):
                if len(mmB2) >= 2:
                    wait("sync", s_mm, mmB2[-2])
                em("sync", lambda sy, ht=ht, dc=dc: sy.dma_start(
                    w2qs[ht % 2][:, :], RAP(w2qd, ht * 128 * D + dc * 512,
                                            [[D, 128], [1, 512]])
                ).then_inc(s_tp.h, 16))
                s_tp.inc(16)
                wait("tensor", s_tp, s_tp.v)
                for lt2 in range(4):
                    mm_group([(lambda lt2=lt2: pctx[lt2][:, 0:512],
                               lambda ht=ht, lt2=lt2:
                               hqTb[ht][:, lt2 * 128:(lt2 + 1) * 128],
                               lambda ht=ht: w2qs[ht % 2][:, :],
                               ht == 0, ht == HT - 1)],
                              inc=(lt2 == 3))
                mmB2.append(s_mm.v)
            mmB = s_mm.v
            wait("vector", s_mm, mmB)
            for lt2 in range(4):
                tgt = s_v.inc()
                em("vector", lambda v, lt2=lt2, dc=dc, tg=tgt: v.tensor_tensor(
                    ysb[lt2][:, dc * 512:(dc + 1) * 512], pctx[lt2][:, 0:512],
                    b2qb[:, dc * 512:(dc + 1) * 512], ALU.add).then_inc(s_v.h, 1))
            wait("tensor", s_v, s_v.v)
        wait("scalar", s_v, s_v.v)
        wait("vector", s_v, s_v.v)
        for lt2 in range(4):
            lt = lc * 4 + lt2
            if len(ev2) >= 2:
                wait("sync", s_v, ev2[-2])
            em("sync", lambda sy, lt=lt: sy.dma_start(
                maskt[lt % 2][:, :],
                RAP(maskin, lt * 128 * D, [[D, 128], [1, D]])
            ).then_inc(s_ld.h, 16))
            s_ld.inc(16)
            em("sync", lambda sy, lt=lt: sy.dma_start(
                strm[lt % 2][:, :], RAP(x1d, lt * 128 * D, [[D, 128], [1, D]])
            ).then_inc(s_ld.h, 16))
            s_ld.inc(16)
            ld_t = s_ld.v
            if len(st2) >= 2:
                wait("vector", s_st, st2[-2])
            for qq in range(2):
                vtg = q_chain(ysb[lt2][:, qq * 512:(qq + 1) * 512], 512,
                              [(10, -14, ytil2[lt % 2][:, qq * 512:(qq + 1) * 512])])
            wait("vector", s_ld, ld_t)
            em("vector", lambda v, lt=lt: v.scalar_tensor_tensor(
                ytil2[lt % 2][:, 0:D], ytil2[lt % 2][:, 0:D], consts[:, 6:7],
                maskt[lt % 2][:, :], ALU.mult, ALU.mult))
            tgt = s_v.inc()
            em("vector", lambda v, lt=lt, tg=tgt: v.tensor_tensor(
                lnt[:, 0:D], ytil2[lt % 2][:, 0:D], strm[lt % 2][:, :], ALU.add
            ).then_inc(s_v.h, 1))
            ev2.append(tgt)
            wait("scalar", s_v, tgt)
            ln_t = ln_tile(lnt[:, 0:D], ytil2[lt % 2][:, 0:D])
            wait("sync", s_v, ln_t)
            em("sync", lambda sy, lt=lt: sy.dma_start(
                RAP(out, lt * 128 * D, [[D, 128], [1, D]]), ytil2[lt % 2][:, 0:D]
            ).then_inc(s_st.h, 16))
            s_st.inc(16)
            st2.append(s_st.v)

    st_all = s_st.v
    for eng in ENGS:
        wait(eng, s_st, st_all)
    return _finish(nc, plan, dbg, (es, esC))


def _finish(nc, plan, dbg, stacks):
    with nc.Block() as block:
        def run_for(eng_name):
            def body(eng):
                for e, fn in plan:
                    if e == eng_name:
                        fn(eng)
            return body
        block.sync(run_for("sync"))
        block.gpsimd(run_for("gpsimd"))
        block.tensor(run_for("tensor"))
        block.vector(run_for("vector"))
        block.scalar(run_for("scalar"))
    for st in stacks:
        try:
            st.close()
        except Exception:
            pass
    return nc, dbg


def _host_constants():
    t = np.arange(L, dtype=np.float64)
    inv_freq = 1.0 / (10000.0 ** (np.arange(0, 64, 2, dtype=np.float64) / 64.0))
    ang = np.outer(t, inv_freq)
    cos = np.cos(ang)
    sin = np.sin(ang)
    p = np.arange(128)
    m = (p % 64) // 2
    c2 = cos[:, m].T.astype(np.float32)
    s2 = sin[:, m].T.astype(np.float32)
    s2 = np.where((p % 2 == 0)[:, None], -s2, s2).astype(np.float32)
    ident = np.eye(128, dtype=ml_dtypes.bfloat16)
    return np.ascontiguousarray(c2), np.ascontiguousarray(s2), ident


def _threefry2x32(k0, k1, x0, x1):
    k0 = np.uint32(k0); k1 = np.uint32(k1)
    k2 = np.uint32(k0 ^ k1 ^ np.uint32(0x1BD11BDA))
    ks = [k0, k1, k2]
    R = [[13, 15, 26, 6], [17, 29, 16, 24]]
    x0 = (x0 + ks[0]).astype(np.uint32)
    x1 = (x1 + ks[1]).astype(np.uint32)
    for i in range(5):
        for r in (R[0] if i % 2 == 0 else R[1]):
            x0 = (x0 + x1).astype(np.uint32)
            x1 = ((x1 << np.uint32(r)) | (x1 >> np.uint32(32 - r))).astype(np.uint32)
            x1 = (x0 ^ x1).astype(np.uint32)
        x0 = (x0 + ks[(i + 1) % 3]).astype(np.uint32)
        x1 = (x1 + ks[(i + 2) % 3] + np.uint32(i + 1)).astype(np.uint32)
    return x0, x1


def _dropout_mask():
    n = 8 * L * D
    iota = np.arange(n, dtype=np.uint64)
    c1 = (iota >> np.uint64(32)).astype(np.uint32)
    c2 = (iota & np.uint64(0xFFFFFFFF)).astype(np.uint32)
    b1, b2 = _threefry2x32(0, 42, c1, c2)
    if os.environ.get("BASS_SIM_FP64") == "1":
        bits = (b1.astype(np.uint64) << np.uint64(32)) | b2.astype(np.uint64)
        u = ((bits >> np.uint64(12)) | np.uint64(0x3FF0000000000000)
             ).view(np.float64) - 1.0
        keep = u < 0.8
    else:
        bits = (b1 ^ b2).astype(np.uint32)
        u = ((bits >> np.uint32(9)) | np.uint32(0x3F800000)
             ).view(np.float32) - np.float32(1.0)
        keep = u < np.float32(0.8)
    return keep.reshape(8, L, D).astype(ml_dtypes.bfloat16)


_CACHE = {}


def kernel(L0_attn_x, L0_attn_w_q, L0_attn_w_k, L0_attn_w_v, L0_attn_w_o,
           L0_attn_rel_emb, L0_ffn_w1, L0_ffn_b1, L0_ffn_w2, L0_ffn_b2):
    if "nc" not in _CACHE:
        _CACHE["nc"], _ = build_core_program()
    nc = _CACHE["nc"]

    c2, s2, ident = _host_constants()
    maskf = _dropout_mask()

    x = np.ascontiguousarray(np.asarray(L0_attn_x, dtype=np.float32))
    wq = np.ascontiguousarray(np.asarray(L0_attn_w_q, dtype=np.float32).T)
    wk = np.ascontiguousarray(np.asarray(L0_attn_w_k, dtype=np.float32).T)
    wv = np.ascontiguousarray(np.asarray(L0_attn_w_v, dtype=np.float32).T)
    wo = np.ascontiguousarray(np.asarray(L0_attn_w_o, dtype=np.float32).T)
    w1 = np.ascontiguousarray(np.asarray(L0_ffn_w1, dtype=np.float32).T)
    w2 = np.ascontiguousarray(np.asarray(L0_ffn_w2, dtype=np.float32).T)
    b1 = np.ascontiguousarray(np.asarray(L0_ffn_b1, dtype=np.float32)[None, :])
    b2 = np.ascontiguousarray(np.asarray(L0_ffn_b2, dtype=np.float32)[None, :])
    rel = np.asarray(L0_attn_rel_emb, dtype=np.float32)
    idx = [32 - w for w in range(32)]
    rel_Tw = np.ascontiguousarray(
        np.concatenate([rel[idx, :64].T, rel[0:1, :64].T], axis=1))
    rel_wrev = np.ascontiguousarray(
        np.concatenate([rel[idx, :64], rel[0:1, :64]], axis=0))

    in_maps = []
    for b in range(8):
        xb = np.ascontiguousarray(x[b])
        in_maps.append({
            "x_nat": xb,
            "x_T": np.ascontiguousarray(xb.T),
            "wq_T": wq, "wk_T": wk, "wv_T": wv, "wo_T": wo,
            "w1_T": w1, "w2_T": w2, "b1": b1, "b2": b2,
            "rel_Tw": rel_Tw, "rel_wrev": rel_wrev,
            "rel0r": np.ascontiguousarray(rel[0:1, :64]),
            "maskin": np.ascontiguousarray(maskf[b]),
            "c2": c2, "s2": s2, "ident": ident,
        })
    res = run_bass_kernel_spmd(nc, in_maps, core_ids=list(range(8)))
    return np.stack([r["out"].reshape(L, D) for r in res.results], axis=0)
